# version 04
# baseline (speedup 1.0000x reference)
import numpy as np
import ml_dtypes
from contextlib import ExitStack

import concourse.bass as bass
import concourse.mybir as mybir
from concourse.bass_utils import run_bass_kernel_spmd

F32 = mybir.dt.float32
BF16 = mybir.dt.bfloat16
I32 = mybir.dt.int32
I16 = mybir.dt.int16
AF = mybir.ActivationFunctionType
OP = mybir.AluOpType
AX = mybir.AxisListType
BF = ml_dtypes.bfloat16

NCORES = 8
T = 1024
TC = 8
D = 2048
FC = 16
H = 16
HD = 128
CH = 256
NCH = 4
SEG = 512
NSEG = 2
NMLP = 16384
EPS = 1e-5
GELU_A = 0.044715
GELU_S = 1.5957691216057308
BSCALE = 0.08838834764831845

ENGINES = ("sync", "tensor", "vector", "scalar", "gpsimd")

STOP_AFTER = None
DEBUG_SCRATCH = ()
DEBUG_SKIP = ()
RACE_DETECT = True
LN_PASS_A_ONLY = False


class Prog:

    def __init__(self, nc, es):
        self.nc = nc
        self.es = es
        self.ops = {e: [] for e in ENGINES}
        self.sem_h = {}
        self.sem_n = {}

    def sem(self, name):
        if name not in self.sem_h:
            self.sem_h[name] = self.es.enter_context(self.nc.semaphore(name))
            self.sem_n[name] = 0
        return name

    def n(self, name):
        return self.sem_n[name]

    def dma(self, eng, stream, fn, waits=(), slot=0):
        s = self.sem(f"{stream}.{slot}")
        self.add(eng, fn, waits=waits, incs=[(s, 16)])
        return (s, self.n(s))

    def all_sems(self):
        return [(s, self.sem_n[s]) for s in self.sem_n]

    def add(self, eng, fn=None, waits=(), incs=()):
        waits = [(self.sem_h[s], v) for s, v in waits if v > 0]
        incs_h = [(self.sem_h[s], k) for s, k in incs]
        for s, k in incs:
            self.sem_n[s] += k

        def emit(e):
            for h, v in waits:
                e.wait_ge(h, v)
            if fn is not None:
                ins = fn(e)
                for h, k in incs_h:
                    ins.then_inc(h, k, skip_validation=(k == 16))
            else:
                for h, k in incs_h:
                    e.sem_inc(h, k)
        self.ops[eng].append(emit)

    def emit(self, block):
        for ename in ENGINES:
            lst = self.ops[ename]
            if not lst:
                continue
            sec = getattr(block, ename)

            def make(lst):
                def _(e):
                    for op in lst:
                        op(e)
                return _
            sec(make(lst))


def r3(ap, b=4):
    return ap.rearrange("p (b d) -> p b d", b=b)


def build():
    nc = bass.Bass("TRN2", target_bir_lowering=False, detect_race_conditions=RACE_DETECT)

    def din(name, shape, dt):
        return nc.dram_tensor(name, shape, dt, kind="ExternalInput").ap()

    x_in = din("x_in", [T, D], F32)
    wqkv_in = din("wqkv_in", [3 * D, D], F32)
    wout_in = din("wout_in", [D, D], F32)
    gpk_in = din("gpk_in", [NMLP, D // 2], I32)
    gsc_in = din("gsc_in", [NMLP, 64], F32)
    upk_in = din("upk_in", [NMLP, D // 2], I32)
    usc_in = din("usc_in", [NMLP, 64], F32)
    dpk_in = din("dpk_in", [D, NMLP // 2], I32)
    dsc_in = din("dsc_in", [D, 512], F32)
    idb_in = din("idb_in", [128, 128], BF16)
    idf_in = din("idf_in", [128, 128], F32)
    cose_in = din("cose_in", [T, 64], F32)
    coso_in = din("coso_in", [T, 64], F32)
    sine_in = din("sine_in", [T, 64], F32)
    sino_in = din("sino_in", [T, 64], F32)
    maska_in = din("maska_in", [256, 256], F32)
    ones_in = din("ones_in", [128, 2], BF16)
    onesr_in = din("onesr_in", [1, 128], BF16)
    onesrf_in = din("onesrf_in", [1, 128], F32)
    premask_in = din("premask_in", [128, 8], F32)
    lnc_in = din("lnc_in", [128, 2], F32)

    y_out = nc.dram_tensor("y_out", [T, D], F32, kind="ExternalOutput").ap()

    def dscr(name, shape, dt):
        kind = "ExternalOutput" if name in DEBUG_SCRATCH else "Internal"
        return nc.dram_tensor(name, shape, dt, kind=kind).ap()

    wqkvT_sc = dscr("wqkvT_sc", [D, 3 * D], BF16)
    woutT_sc = dscr("woutT_sc", [D, D], BF16)
    q_sc = dscr("q_sc", [T, D], BF16)
    k_sc = dscr("k_sc", [T, D], BF16)
    v_sc = dscr("v_sc", [T, D], BF16)
    wg_sc = dscr("wg_sc", [128 * 128, 16 * 128], BF16)
    wu_sc = dscr("wu_sc", [128 * 128, 16 * 128], BF16)
    wd_sc = dscr("wd_sc", [128 * 128, 16 * 128], BF16)
    cc_in = dscr("cc_in", [1, 128 * H * 132], F32)
    cc_out = dscr("cc_out", [NCORES, 128 * H * 132], F32)
    xdbg_sc = dscr("xdbg_sc", [128, FC * T], F32)
    sdbg_sc = dscr("sdbg_sc", [128, H * 132], F32)
    abdbg_sc = dscr("abdbg_sc", [128, 2048], F32)
    odbg_sc = dscr("odbg_sc", [128, H * 256], BF16)
    xlndbg_sc = dscr("xlndbg_sc", [128, FC * T], BF16)

    with ExitStack() as es:
        p = Prog(nc, es)
        for s in ("pe", "dve", "act", "pool", "cc", "fin",
                  "b_sync", "b_tensor", "b_vector", "b_scalar", "b_gpsimd"):
            p.sem(s)
        EB = es.enter_context

        def barrier():
            for e in ENGINES:
                p.add(e, None, incs=[(f"b_{e}", 1)])
            w = p.all_sems()
            for e in ENGINES:
                p.add(e, None, waits=w)

        X = EB(nc.sbuf_tensor("X", [128, FC * T], F32))
        idb = EB(nc.sbuf_tensor("idb", [128, 128], BF16))
        idf = EB(nc.sbuf_tensor("idf", [128, 128], F32))
        maska = EB(nc.sbuf_tensor("maska", [128, 2 * 256], F32))
        onesb = EB(nc.sbuf_tensor("onesb", [128, 2], BF16))
        ones_row = EB(nc.sbuf_tensor("ones_row", [1, 128], BF16))
        onesrf = EB(nc.sbuf_tensor("onesrf", [1, 128], F32))
        premask = EB(nc.sbuf_tensor("premask", [128, 8], F32))
        lnc = EB(nc.sbuf_tensor("lnc", [128, 2], F32))


        def ph_ln(tag, Xln, first, drain=None):
            sc = ExitStack()
            E = sc.enter_context
            xtm = [E(nc.sbuf_tensor(f"xtm{tag}{i}", [128, D], F32)) for i in range(2)]
            xlt = [E(nc.sbuf_tensor(f"xlt{tag}{i}", [128, D], BF16)) for i in range(2)]
            junk = E(nc.sbuf_tensor(f"junk{tag}", [128, D], BF16))
            st = {nm: E(nc.sbuf_tensor(f"{nm}{tag}", [128, TC], F32))
                  for nm in ("sx", "sxx", "mu", "musq", "var", "srt", "rstd", "nmr")}
            psX = [E(nc.psum_tensor(f"psX{tag}{i}", [128, 512], F32)) for i in range(2)]
            psL = [E(nc.psum_tensor(f"psL{tag}{i}", [128, 1024], BF16)) for i in range(2)]

            def make_xtm(tc, evX, post):
                if first:
                    w = [x for x in post.get(("xtmfree", tc - 2), []) if x[1] > 0]
                    tok = p.dma("sync", "xld", lambda e, tc=tc: e.dma_start(
                        out=xtm[tc % 2][:, :], in_=x_in[tc * 128:(tc + 1) * 128, :]),
                        waits=w, slot=tc % 2)
                    return [tok, CONSTS_TOK]
                else:
                    last = []
                    for g in range(4):
                        lin = tc * 4 + g
                        w = []
                        if ("psX", lin - 2) in post:
                            w.append(post[("psX", lin - 2)])
                        if g == 0:
                            w += post.get(("xtmfree", tc - 2), [])
                        w = [x for x in w if x[1] > 0]
                        for j in range(4):
                            fc = g * 4 + j
                            p.add("tensor", lambda e, tc=tc, g=g, j=j, fc=fc: e.transpose(
                                psX[g % 2][:, j * 128:(j + 1) * 128],
                                X[:, fc * T + tc * 128: fc * T + tc * 128 + 128], idf[:, :]),
                                waits=(w if j == 0 else []),
                                incs=([("pe", 1)] if j == 3 else []))
                        pe_ok = p.n("pe")
                        p.add("scalar", lambda e, tc=tc, g=g: e.copy(
                            out=xtm[tc % 2][:, :].rearrange("p (g j f) -> p g j f", g=4, j=4)[:, g, :, :],
                            in_=psX[g % 2][:, :].rearrange("p (j f) -> p j f", j=4)),
                            waits=[("pe", pe_ok)], incs=[("act", 1)])
                        post[("psX", lin)] = ("act", p.n("act"))
                    return [("act", p.n("act"))]

            post = {}
            for tc in range(TC):
                if drain:
                    drain()
                rdy = make_xtm(tc, None, post)
                xs = xtm[tc % 2][:, :]
                p.add("vector", lambda e, xs=xs, tc=tc: e.reduce_sum(
                    out=st["sx"][:, tc:tc + 1], in_=xs, axis=AX.X),
                    waits=rdy, incs=[("dve", 1)])
                rsum_ok = p.n("dve")
                p.add("scalar", lambda e, xs=xs, tc=tc: e.activation(
                    out=junk[:, :], in_=xs, func=AF.Square, accum_out=st["sxx"][:, tc:tc + 1]),
                    waits=rdy, incs=[("act", 1)])
                post[("xtmfree", tc)] = [("act", p.n("act")), ("dve", rsum_ok)]
            sxA, sxxA = p.n("dve"), p.n("act")

            lnc0b = lnc[:, 0:1].broadcast_to([128, TC])
            lnc1b = lnc[:, 1:2].broadcast_to([128, TC])
            p.add("gpsimd", lambda e: e.tensor_tensor(
                out=st["mu"][:, :], in0=lnc0b, in1=st["sx"][:, :], op=OP.mult),
                waits=[("dve", sxA)])
            p.add("gpsimd", lambda e: e.tensor_tensor(
                out=st["musq"][:, :], in0=st["mu"][:, :], in1=st["mu"][:, :], op=OP.mult))
            p.add("gpsimd", lambda e: e.tensor_tensor(
                out=st["var"][:, :], in0=lnc0b, in1=st["sxx"][:, :], op=OP.mult),
                waits=[("act", sxxA)])
            p.add("gpsimd", lambda e: e.tensor_tensor(
                out=st["var"][:, :], in0=st["var"][:, :], in1=st["musq"][:, :], op=OP.subtract),
                incs=[("pool", 1)])
            var_ok = p.n("pool")
            p.add("scalar", lambda e: e.activation(
                out=st["srt"][:, :], in_=st["var"][:, :], func=AF.Sqrt,
                bias=np.float32(EPS), scale=np.float32(1.0)),
                waits=[("pool", var_ok)], incs=[("act", 1)])
            srt_ok = p.n("act")
            p.add("vector", lambda e: e.reciprocal(out=st["rstd"][:, :], in_=st["srt"][:, :]),
                  waits=[("act", srt_ok)], incs=[("dve", 1)])
            rstd_ok = p.n("dve")
            p.add("gpsimd", lambda e: e.tensor_tensor(
                out=st["musq"][:, :], in0=lnc1b, in1=st["mu"][:, :], op=OP.mult))
            p.add("gpsimd", lambda e: e.tensor_tensor(
                out=st["nmr"][:, :], in0=st["musq"][:, :], in1=st["rstd"][:, :], op=OP.mult),
                waits=[("dve", rstd_ok)], incs=[("pool", 1)])
            nmr_ok = p.n("pool")

            if LN_PASS_A_ONLY and not first:
                sc.close()
                return
            passA_tail = [("act", p.n("act")), ("dve", p.n("dve"))]
            post = {("psX", -2): passA_tail[0], ("psX", -1): passA_tail[1],
                    ("xtmfree", -2): passA_tail, ("xtmfree", -1): passA_tail}
            aff = {}
            evL = {}
            for tc in range(TC):
                if drain:
                    drain()
                rdy = make_xtm(tc, None, post)
                if first:
                    for g in range(4):
                        lin = tc * 4 + g
                        w = list(rdy) if g == 0 else []
                        if ("psX", lin - 2) in post:
                            w.append(post[("psX", lin - 2)])
                        for j in range(4):
                            fc = g * 4 + j
                            p.add("tensor", lambda e, tc=tc, g=g, j=j, fc=fc: e.transpose(
                                psX[g % 2][:, j * 128:(j + 1) * 128],
                                xtm[tc % 2][:, fc * 128:(fc + 1) * 128], idf[:, :]),
                                waits=(w if j == 0 else []),
                                incs=([("pe", 1)] if j == 3 else []))
                        pe_ok = p.n("pe")
                        p.add("vector", lambda e, tc=tc, g=g: e.tensor_copy(
                            out=X[:, :].rearrange("p (f t) -> p f t", f=FC)[:, g * 4:(g + 1) * 4, tc * 128:(tc + 1) * 128],
                            in_=psX[g % 2][:, :].rearrange("p (j f) -> p j f", j=4)),
                            waits=[("pe", pe_ok)], incs=[("dve", 1)])
                        post[("psX", lin)] = ("dve", p.n("dve"))

                waff = list(rdy) + [("pool", nmr_ok)]
                if tc >= 2:
                    waff.append(("act", evL[(tc - 2) * 4 + 3]))
                p.add("vector", lambda e, tc=tc: e.tensor_scalar(
                    out=xlt[tc % 2][:, :], in0=xtm[tc % 2][:, :],
                    scalar1=st["rstd"][:, tc:tc + 1], scalar2=st["nmr"][:, tc:tc + 1],
                    op0=OP.mult, op1=OP.add),
                    waits=waff, incs=[("dve", 1)])
                aff[tc] = p.n("dve")
                post[("xtmfree", tc)] = [("dve", aff[tc])]

                for g in range(4):
                    lin = tc * 4 + g
                    w = [("dve", aff[tc])] if g == 0 else []
                    if lin >= 2:
                        w.append(("act", evL[lin - 2]))
                    for j in range(4):
                        fc = g * 4 + j
                        p.add("tensor", lambda e, tc=tc, g=g, j=j, fc=fc: e.transpose(
                            psL[g % 2][:, j * 128:(j + 1) * 128],
                            xlt[tc % 2][:, fc * 128:(fc + 1) * 128], idb[:, :]),
                            waits=(w if j == 0 else []),
                            incs=([("pe", 1)] if j == 3 else []))
                    pe_ok = p.n("pe")
                    p.add("scalar", lambda e, Xln=Xln, tc=tc, g=g: e.copy(
                        out=Xln[:, :].rearrange("p (f t) -> p f t", f=FC)[:, g * 4:(g + 1) * 4, tc * 128:(tc + 1) * 128],
                        in_=psL[g % 2][:, 0:512].rearrange("p (j f) -> p j f", j=4)),
                        waits=[("pe", pe_ok)], incs=[("act", 1)])
                    evL[lin] = p.n("act")
            sc.close()

        def ph_wt(drain=None):
            sc = ExitStack()
            E = sc.enter_context
            wraw = [E(nc.sbuf_tensor(f"wraw{i}", [128, D], F32)) for i in range(2)]
            wbf = [E(nc.sbuf_tensor(f"wbf{i}", [128, D], BF16)) for i in range(4)]
            wst = [E(nc.sbuf_tensor(f"wstg{i}", [128, 512], BF16)) for i in range(2)]
            psT = [E(nc.psum_tensor(f"psWT{i}", [128, 1024], BF16)) for i in range(2)]

            cvt = {}
            evc = {}
            dmo = {}
            units = [("qkv", og) for og in range(12)] + [("out", og) for og in range(4)]
            for u, (kind, og) in enumerate(units):
                if drain:
                    drain()
                src = wqkv_in if kind == "qkv" else wout_in
                dst = wqkvT_sc if kind == "qkv" else woutT_sc
                for j in range(4):
                    if j < 2:
                        w = [("act", cvt[(u - 1, j + 2)])] if u >= 1 else []
                    else:
                        w = [("act", cvt[(u, j - 2)])]
                    tok = p.dma("sync", "wraw", lambda e, src=src, og=og, j=j: e.dma_start(
                        out=wraw[j % 2][:, :], in_=src[og * 512 + j * 128: og * 512 + (j + 1) * 128, :]),
                        waits=w, slot=j % 2)
                    w2 = [tok]
                    if u >= 1:
                        w2.append(("pe", pe_units[u - 1]))
                    p.add("scalar", lambda e, j=j: e.copy(out=wbf[j][:, :], in_=wraw[j % 2][:, :]),
                          waits=w2, incs=[("act", 1)])
                    cvt[(u, j)] = p.n("act")
                for kc in range(16):
                    lin = u * 16 + kc
                    w = [("act", cvt[(u, 3)])] if kc == 0 else []
                    if lin >= 2:
                        w.append(("act", evc[lin - 2]))
                    for j in range(4):
                        p.add("tensor", lambda e, kc=kc, j=j: e.transpose(
                            psT[kc % 2][:, j * 128:(j + 1) * 128],
                            wbf[j][:, kc * 128:(kc + 1) * 128], idb[:, :]),
                            waits=(w if j == 0 else []),
                            incs=([("pe", 1)] if j == 3 else []))
                    pe_ok = p.n("pe")
                    w2 = [("pe", pe_ok)]
                    if lin >= 2:
                        w2.append(dmo[lin - 2])
                    p.add("scalar", lambda e, kc=kc: e.copy(out=wst[kc % 2][:, :], in_=psT[kc % 2][:, 0:512]),
                          waits=w2, incs=[("act", 1)])
                    evc[lin] = p.n("act")
                    dmo[lin] = p.dma("gpsimd", "wtw", lambda e, dst=dst, kc=kc, og=og: e.dma_start(
                        out=dst[kc * 128:(kc + 1) * 128, og * 512:(og + 1) * 512], in_=wst[kc % 2][:, :]),
                        waits=[("act", evc[lin])], slot=kc % 2)
                pe_units[u] = p.n("pe")
            sc.close()
        pe_units = {}

        def ph_qkv(Xln, drain=None):
            sc = ExitStack()
            E = sc.enter_context
            wqT = [E(nc.sbuf_tensor(f"wqT{i}", [128, 16 * 512], BF16)) for i in range(2)]
            stg = [E(nc.sbuf_tensor(f"qstg{i}", [128, 512], BF16)) for i in range(3)]
            ra = E(nc.sbuf_tensor("ra", [128, 256], F32))
            rb = E(nc.sbuf_tensor("rb", [128, 256], F32))
            rc = E(nc.sbuf_tensor("rcc", [128, 256], F32))
            rd = E(nc.sbuf_tensor("rdd", [128, 256], F32))
            qp = E(nc.sbuf_tensor("qp", [128, 512], F32))
            me = E(nc.sbuf_tensor("me", [128, 512], F32))
            ex = E(nc.sbuf_tensor("ex", [128, 512], F32))
            rl = E(nc.sbuf_tensor("rl", [128, 512], F32))
            psQ = [E(nc.psum_tensor(f"psQ{i}", [128, 512], F32)) for i in range(2)]

            psum_free = {}
            stage_ok = {}
            spill_ok = {}
            wq_ok = {}
            mm_done = {}
            for og in range(12):
                w = [("pe", mm_done[(og - 2) * 8 + 7])] if og >= 2 else []
                wq_ok[og] = p.dma("sync", "wqT", lambda e, og=og: e.dma_start(
                    out=wqT[og % 2][:, :].rearrange("p (kc f) -> p kc f", kc=16),
                    in_=wqkvT_sc[:, og * 512:(og + 1) * 512].rearrange("(kc p) f -> p kc f", p=128)),
                    waits=w, slot=og % 2)
                kind = "q" if og < 4 else ("k" if og < 8 else "v")
                for tc in range(TC):
                    if drain:
                        drain()
                    i = og * 8 + tc
                    w = [wq_ok[og]] if tc == 0 else []
                    if i >= 2:
                        w.append(psum_free[i - 2])
                    for kc in range(16):
                        p.add("tensor", lambda e, og=og, tc=tc, kc=kc, i=i: e.matmul(
                            psQ[i % 2][:, :],
                            Xln[:, kc * T + tc * 128: kc * T + tc * 128 + 128],
                            wqT[og % 2][:, kc * 512:(kc + 1) * 512],
                            start=(kc == 0), stop=(kc == 15)),
                            waits=(w if kc == 0 else []),
                            incs=([("pe", 1)] if kc == 15 else []))
                    mm_done[i] = p.n("pe")
                    stb = stg[i % 3]
                    wstg = [spill_ok[i - 3]] if i >= 3 else []
                    if kind == "v":
                        p.add("scalar", lambda e, stb=stb, i=i: e.copy(out=stb[:, :], in_=psQ[i % 2][:, :]),
                              waits=[("pe", mm_done[i])] + wstg, incs=[("act", 1)])
                        psum_free[i] = ("act", p.n("act"))
                        stage_ok[i] = ("act", p.n("act"))
                    else:
                        ps = psQ[i % 2][:, :]
                        pse = r3(ps)[:, :, 0::2]
                        pso = r3(ps)[:, :, 1::2]
                        ce = cose[:, tc * 64:(tc + 1) * 64].unsqueeze(1).broadcast_to([128, 4, 64])
                        co = coso[:, tc * 64:(tc + 1) * 64].unsqueeze(1).broadcast_to([128, 4, 64])
                        se = sine[:, tc * 64:(tc + 1) * 64].unsqueeze(1).broadcast_to([128, 4, 64])
                        so = sino[:, tc * 64:(tc + 1) * 64].unsqueeze(1).broadcast_to([128, 4, 64])
                        p.add("vector", lambda e, ce=ce, pse=pse: e.tensor_tensor(
                            out=r3(ra[:, :], 4), in0=ce, in1=pse, op=OP.mult),
                            waits=[("pe", mm_done[i])])
                        p.add("vector", lambda e, se=se, pso=pso: e.tensor_tensor(
                            out=r3(rb[:, :], 4), in0=se, in1=pso, op=OP.mult))
                        p.add("vector", lambda e, co=co, pso=pso: e.tensor_tensor(
                            out=r3(rc[:, :], 4), in0=co, in1=pso, op=OP.mult))
                        p.add("vector", lambda e, so=so, pse=pse: e.tensor_tensor(
                            out=r3(rd[:, :], 4), in0=so, in1=pse, op=OP.mult),
                            incs=[("dve", 1)])
                        psum_free[i] = ("dve", p.n("dve"))
                        p.add("vector", lambda e: e.tensor_tensor(
                            out=r3(qp[:, :])[:, :, 0::2], in0=r3(ra[:, :], 4), in1=r3(rb[:, :], 4),
                            op=OP.subtract))
                        p.add("vector", lambda e: e.tensor_tensor(
                            out=r3(qp[:, :])[:, :, 1::2], in0=r3(rc[:, :], 4), in1=r3(rd[:, :], 4),
                            op=OP.add))
                        p.add("vector", lambda e: e.tensor_scalar(
                            out=me[:, :], in0=qp[:, :], scalar1=0.0, scalar2=None, op0=OP.min),
                            incs=[("dve", 1)])
                        me_ok = p.n("dve")
                        p.add("scalar", lambda e: e.activation(out=ex[:, :], in_=me[:, :], func=AF.Exp),
                              waits=[("dve", me_ok)], incs=[("act", 1)])
                        ex_ok = p.n("act")
                        p.add("vector", lambda e: e.tensor_relu(out=rl[:, :], in_=qp[:, :]))
                        p.add("vector", lambda e, stb=stb: e.tensor_tensor(
                            out=stb[:, :], in0=ex[:, :], in1=rl[:, :], op=OP.add),
                            waits=[("act", ex_ok)] + wstg, incs=[("dve", 1)])
                        stage_ok[i] = ("dve", p.n("dve"))
                    dstm = {"q": q_sc, "k": k_sc, "v": v_sc}[kind]
                    col = (og % 4) * 512
                    spill_ok[i] = p.dma("sync", "spill", lambda e, dstm=dstm, tc=tc, col=col, stb=stb: e.dma_start(
                        out=dstm[tc * 128:(tc + 1) * 128, col:col + 512], in_=stb[:, :]),
                        waits=[stage_ok[i]], slot=i % 3)
            sc.close()

        def ph_attn_a(drain=None):
            sc = ExitStack()
            E = sc.enter_context
            S_run = E(nc.sbuf_tensor("S_run", [128, H * 132], F32))
            S_bf = E(nc.sbuf_tensor("S_bf", [128, H * 132], BF16))
            sc23 = ExitStack()
            E2 = sc23.enter_context
            Sloc = E2(nc.sbuf_tensor("Sloc", [128, H * 132], F32))
            kh = [E2(nc.sbuf_tensor(f"khA{i}", [128, TC * 128], BF16)) for i in range(2)]
            vh = [E2(nc.sbuf_tensor(f"vhA{i}", [128, TC * 128], BF16)) for i in range(2)]
            rkb = [E2(nc.sbuf_tensor(f"rkb{i}", [128, H * 132], F32)) for i in range(2)]
            tmpS = E2(nc.sbuf_tensor("tmpS", [128, H * 132], F32))
            psS = E2(nc.psum_tensor("psS", [128, 512], F32))
            psSz = E2(nc.psum_tensor("psSz", [128, 512], F32))

            ev_ok = 0
            for h in range(H):
                if drain:
                    drain()
                w = []
                if h >= 2:
                    w.append(("pe", st_mm[h - 2]))
                t1 = p.dma("sync", "kh", lambda e, h=h: e.dma_start(
                    out=kh[h % 2][:, :].rearrange("p (tc d) -> p tc d", tc=TC),
                    in_=k_sc[:, h * 128:(h + 1) * 128].rearrange("(tc p) d -> p tc d", p=128)),
                    waits=w, slot=h % 2)
                t2 = p.dma("sync", "vh", lambda e, h=h: e.dma_start(
                    out=vh[h % 2][:, :].rearrange("p (tc d) -> p tc d", tc=TC),
                    in_=v_sc[:, h * 128:(h + 1) * 128].rearrange("(tc p) d -> p tc d", p=128)),
                    slot=h % 2)
                w = [t1, t2]
                if h >= 1:
                    w.append(("dve", ev_ok))
                for mc in range(TC):
                    p.add("tensor", lambda e, h=h, mc=mc: e.matmul(
                        psS[:, 0:128], kh[h % 2][:, mc * 128:(mc + 1) * 128],
                        vh[h % 2][:, mc * 128:(mc + 1) * 128],
                        start=(mc == 0), stop=(mc == TC - 1)),
                        waits=(w if mc == 0 else []))
                    p.add("tensor", lambda e, h=h, mc=mc: e.matmul(
                        psSz[:, 0:1], kh[h % 2][:, mc * 128:(mc + 1) * 128],
                        onesb[:, 0:1],
                        start=(mc == 0), stop=(mc == TC - 1)),
                        incs=([("pe", 1)] if mc == TC - 1 else []))
                st_mm[h] = p.n("pe")
                p.add("vector", lambda e, h=h: e.tensor_copy(
                    out=Sloc[:, h * 132:h * 132 + 128], in_=psS[:, 0:128]),
                    waits=[("pe", st_mm[h])])
                p.add("vector", lambda e, h=h: e.tensor_copy(
                    out=Sloc[:, h * 132 + 128:h * 132 + 129], in_=psSz[:, 0:1]),
                    incs=[("dve", 1)])
                ev_ok = p.n("dve")

            cctok = p.dma("gpsimd", "ccw", lambda e: e.dma_start(
                out=cc_in[0, :].rearrange("(pp c) -> pp c", pp=128), in_=Sloc[:, :]),
                waits=[("dve", ev_ok)])
            p.add("gpsimd", lambda e: e.collective_compute(
                "AllGather", OP.bypass,
                replica_groups=[list(range(NCORES))],
                ins=[cc_in[:, :]], outs=[cc_out[:, :]]),
                waits=[cctok], incs=[("cc", 1)])
            cc_ok = p.n("cc")
            sin_ok = None
            for j in range(NCORES):
                w = [("cc", cc_ok)]
                if j >= 2:
                    w.append(("dve", p.n("dve")))
                ldtok = p.dma("sync", "rk", lambda e, j=j: e.dma_start(
                    out=rkb[j % 2][:, :], in_=cc_out[j, :].rearrange("(pp c) -> pp c", pp=128)),
                    waits=w, slot=j % 2)
                if j == 0:
                    p.add("vector", lambda e, j=j: e.tensor_scalar(
                        out=S_run[:, :], in0=rkb[j % 2][:, :], scalar1=premask[:, j:j + 1],
                        scalar2=None, op0=OP.mult),
                        waits=[ldtok], incs=[("dve", 1)])
                else:
                    p.add("vector", lambda e, j=j: e.tensor_scalar(
                        out=tmpS[:, :], in0=rkb[j % 2][:, :], scalar1=premask[:, j:j + 1],
                        scalar2=None, op0=OP.mult),
                        waits=[ldtok])
                    p.add("vector", lambda e: e.tensor_tensor(
                        out=S_run[:, :], in0=tmpS[:, :], in1=S_run[:, :], op=OP.add),
                        incs=[("dve", 1)])
            sin_ok = p.n("dve")
            p.add("scalar", lambda e: e.copy(out=S_bf[:, :], in_=S_run[:, :]),
                  waits=[("dve", sin_ok)], incs=[("act", 1)])
            sbf_ok = {(-1, h): p.n("act") for h in range(H)}
            sc23.close()
            wout_sb = E(nc.sbuf_tensor("wout_sb", [128, FC * D], BF16))
            o_buf = E(nc.sbuf_tensor("o_buf", [128, H * 256], BF16))
            qch = [E(nc.sbuf_tensor(f"qch{i}", [128, 256], BF16)) for i in range(2)]
            kch = [E(nc.sbuf_tensor(f"kch{i}", [128, 256], BF16)) for i in range(2)]
            vch = [E(nc.sbuf_tensor(f"vch{i}", [128, 256], BF16)) for i in range(2)]
            qfm = [E(nc.sbuf_tensor(f"qfm{i}", [128, 256], BF16)) for i in range(2)]
            kfm = [E(nc.sbuf_tensor(f"kfm{i}", [128, 256], BF16)) for i in range(2)]
            p2 = E(nc.sbuf_tensor("p2t", [128, 2 * 256], BF16))
            dr_t = E(nc.sbuf_tensor("dr_t", [1, 512], F32))
            dsb = dr_t[:, 0:256]
            rden = dr_t[:, 256:512]
            bsb = E(nc.sbuf_tensor("bsb", [128, 256], F32))
            spare = E(nc.sbuf_tensor("spareA", [128, 2], F32))
            psT2 = E(nc.psum_tensor("psT2", [128, 1024], BF16))
            psA = [E(nc.psum_tensor(f"psA{i}", [128, 512], F32)) for i in range(2)]
            psUt = E(nc.psum_tensor("psUt", [128, 512], F32))
            psUs = psUt[:, 0:128]
            psDZ = E(nc.psum_tensor("psDZ", [128, 512], F32))
            psD = psDZ[0:1, 0:256]
            psUz = psDZ[:, 256:257]
            psPt = E(nc.psum_tensor("psPt", [128, 512], F32))
            psP0 = psPt[:, 0:256]
            psN = psPt[:, 256:512]

            for ec in range(FC):
                wout_ok = p.dma("sync", "wo", lambda e, ec=ec: e.dma_start(
                    out=wout_sb[:, ec * D:(ec + 1) * D],
                    in_=woutT_sc[ec * 128:(ec + 1) * 128, :]))

            o_ok = {}
            xadd_ok = 0
            for c in range(NCH):
                for h in range(H):
                    if drain:
                        drain()
                    i = c * H + h
                    w = []
                    if i >= 2:
                        w.append(("pe", ld_reuse[i - 2]))
                    chtoks = []
                    for nmx, (dst, srcm) in (("chq", (qch, q_sc)), ("chk", (kch, k_sc)), ("chv", (vch, v_sc))):
                        chtoks.append(p.dma("sync", nmx, lambda e, dst=dst, srcm=srcm, c=c, h=h, i=i: e.dma_start(
                            out=dst[i % 2][:, :].rearrange("p (i d) -> p i d", i=2),
                            in_=srcm[c * 256:(c + 1) * 256, h * 128:(h + 1) * 128]
                                .rearrange("(i p) d -> p i d", p=128)),
                            waits=(w if dst is qch else []), slot=i % 2))
                    w = list(chtoks)
                    if i >= 2:
                        w.append(("act", fm_ok[i - 2]))
                    for ii in range(2):
                        p.add("tensor", lambda e, i=i, ii=ii: e.transpose(
                            psT2[:, ii * 128:(ii + 1) * 128],
                            qch[i % 2][:, ii * 128:(ii + 1) * 128], idb[:, :]),
                            waits=(w if ii == 0 else []))
                    for ii in range(2):
                        p.add("tensor", lambda e, i=i, ii=ii: e.transpose(
                            psT2[:, 256 + ii * 128: 256 + (ii + 1) * 128],
                            kch[i % 2][:, ii * 128:(ii + 1) * 128], idb[:, :]),
                            incs=([("pe", 1)] if ii == 1 else []))
                    tr_ok = p.n("pe")
                    p.add("scalar", lambda e, i=i: e.copy(out=qfm[i % 2][:, :], in_=psT2[:, 0:256]),
                          waits=[("pe", tr_ok)])
                    p.add("scalar", lambda e, i=i: e.copy(out=kfm[i % 2][:, :], in_=psT2[:, 256:512]),
                          incs=[("act", 1)])
                    fm_ok[i] = p.n("act")
                    w = [("act", fm_ok[i])]
                    if i >= 1:
                        w.append(("dve", p2_ok[i - 1]))
                        w.append(("dve", onum_ok[i - 1]))
                    for mc in range(2):
                        p.add("tensor", lambda e, i=i, mc=mc: e.matmul(
                            psA[mc][:, 0:256], kfm[i % 2][:, mc * 128:(mc + 1) * 128],
                            qfm[i % 2][:, :], start=True, stop=True),
                            waits=(w if mc == 0 else []),
                            incs=([("pe", 1)] if mc == 1 else []))
                    sc_ok = p.n("pe")
                    if drain:
                        drain()
                    for mc in range(2):
                        p.add("vector", lambda e, mc=mc: e.tensor_tensor(
                            out=p2[:, mc * 256:(mc + 1) * 256],
                            in0=maska[:, mc * 256:(mc + 1) * 256], in1=psA[mc][:, 0:256], op=OP.mult),
                            waits=([("pe", sc_ok)] if mc == 0 else []),
                            incs=([("dve", 1)] if mc == 1 else []))
                    p2_ok[i] = p.n("dve")
                    w = [("dve", p2_ok[i]), ("act", sbf_ok[(c - 1, h)])]
                    if i >= 1:
                        w.append(("dve", den_ok[i - 1]))
                        w.append(("dve", supd_ok[i - 1]))
                    p.add("tensor", lambda e, i=i: e.matmul(
                        psD, onesb[:, 0:1], p2[:, 0:256], start=True, stop=False), waits=w)
                    p.add("tensor", lambda e, i=i: e.matmul(
                        psD, onesb[:, 0:1], p2[:, 256:512], start=False, stop=False))
                    p.add("tensor", lambda e, i=i, h=h: e.matmul(
                        psD, S_bf[:, h * 132 + 128:h * 132 + 129], qfm[i % 2][:, :],
                        start=False, stop=True), incs=[("pe", 1)])
                    dmm_ok = p.n("pe")
                    w = []
                    if i >= 1:
                        w.append(("dve", onum_ok[i - 1]))
                    if h == 0 and c >= 1:
                        w.append(("dve", proj_ok[FC - 1]))
                    p.add("tensor", lambda e, i=i, h=h: e.matmul(
                        psN[:, :], S_bf[:, h * 132:h * 132 + 128], qfm[i % 2][:, :],
                        start=True, stop=False), waits=w)
                    for mc in range(2):
                        p.add("tensor", lambda e, i=i, mc=mc: e.matmul(
                            psN[:, :], vch[i % 2][:, mc * 128:(mc + 1) * 128],
                            p2[:, mc * 256:(mc + 1) * 256], start=False, stop=(mc == 1)),
                            incs=([("pe", 1)] if mc == 1 else []))
                    nmm_ok = p.n("pe")
                    if drain:
                        drain()
                    p.add("vector", lambda e: e.tensor_scalar(
                        out=dsb, in0=psD, scalar1=1e-6, scalar2=None, op0=OP.max),
                        waits=[("pe", dmm_ok)], incs=[("dve", 1)])
                    den_ok[i] = p.n("dve")
                    p.add("vector", lambda e: e.tensor_copy(out=spare[:, :], in_=onesb[:, :]))
                    p.add("vector", lambda e: e.reciprocal(out=rden, in_=dsb),
                          incs=[("dve", 1)])
                    rd_ok = p.n("dve")
                    w = [("dve", rd_ok)]
                    p.add("tensor", lambda e: e.matmul(
                        psA[0][:, 256:512], onesrf[0:1, :], rden, start=True, stop=True),
                        waits=w, incs=[("pe", 1)])
                    bmm_ok = p.n("pe")
                    wb = [("pe", bmm_ok)]
                    if i >= 1:
                        wb.append(("dve", onum_ok[i - 1]))
                    p.add("scalar", lambda e: e.copy(out=bsb[:, :], in_=psA[0][:, 256:512]),
                          waits=wb, incs=[("act", 1)])
                    bsb_ok[i] = ("act", p.n("act"))
                    if drain:
                        drain()
                    p.add("vector", lambda e, h=h: e.tensor_tensor(
                        out=o_buf[:, h * 256:(h + 1) * 256], in0=bsb[:, :], in1=psN[:, :], op=OP.mult),
                        waits=[bsb_ok[i], ("pe", nmm_ok)], incs=[("dve", 1)])
                    onum_ok[i] = p.n("dve")
                    o_ok[(c, h)] = p.n("dve")
                    w = [("dve", onum_ok[i])]
                    if i >= 1:
                        w.append(("dve", supd_ok[i - 1]))
                    for mc in range(2):
                        p.add("tensor", lambda e, i=i, mc=mc: e.matmul(
                            psUs, kch[i % 2][:, mc * 128:(mc + 1) * 128],
                            vch[i % 2][:, mc * 128:(mc + 1) * 128],
                            start=(mc == 0), stop=(mc == 1)),
                            waits=(w if mc == 0 else []))
                        p.add("tensor", lambda e, i=i, mc=mc: e.matmul(
                            psUz, kch[i % 2][:, mc * 128:(mc + 1) * 128],
                            onesb[:, 0:1], start=(mc == 0), stop=(mc == 1)),
                            incs=([("pe", 1)] if mc == 1 else []))
                    umm_ok = p.n("pe")
                    ld_reuse[i] = umm_ok
                    p.add("vector", lambda e, h=h: e.tensor_tensor(
                        out=S_run[:, h * 132:h * 132 + 128],
                        in0=S_run[:, h * 132:h * 132 + 128], in1=psUs, op=OP.add),
                        waits=[("pe", umm_ok)])
                    p.add("vector", lambda e, h=h: e.tensor_tensor(
                        out=S_run[:, h * 132 + 128:h * 132 + 129],
                        in0=S_run[:, h * 132 + 128:h * 132 + 129], in1=psUz, op=OP.add),
                        incs=[("dve", 1)])
                    supd_ok[i] = p.n("dve")
                    p.add("scalar", lambda e, h=h: e.copy(
                        out=S_bf[:, h * 132:h * 132 + 129], in_=S_run[:, h * 132:h * 132 + 129]),
                        waits=[("dve", supd_ok[i])], incs=[("act", 1)])
                    sbf_ok[(c, h)] = p.n("act")

                w_all_o = [("dve", o_ok[(c, H - 1)]), wout_ok]
                for hc in range(FC):
                    w = list(w_all_o) if hc == 0 else []
                    if hc >= 1:
                        w.append(("dve", proj_ok[hc - 1]))
                    for ec in range(FC):
                        p.add("tensor", lambda e, hc=hc, ec=ec: e.matmul(
                            psP0, wout_sb[:, ec * D + hc * 128: ec * D + (hc + 1) * 128],
                            o_buf[:, ec * 256:(ec + 1) * 256],
                            start=(ec == 0), stop=(ec == FC - 1)),
                            waits=(w if ec == 0 else []),
                            incs=([("pe", 1)] if ec == FC - 1 else []))
                    pmm_ok = p.n("pe")
                    p.add("vector", lambda e, hc=hc, c=c: e.tensor_tensor(
                        out=X[:, hc * T + c * 256: hc * T + (c + 1) * 256],
                        in0=X[:, hc * T + c * 256: hc * T + (c + 1) * 256],
                        in1=psP0, op=OP.add),
                        waits=[("pe", pmm_ok)], incs=[("dve", 1)])
                    proj_ok[hc] = p.n("dve")
                ld_reuse[c * H + H - 1] = p.n("pe")
            if "sdbg_sc" in DEBUG_SCRATCH:
                p.dma("gpsimd", "dbg", lambda e: e.dma_start(out=sdbg_sc, in_=S_run[:, :]),
                      waits=[("dve", p.n("dve")), ("act", p.n("act")), ("pe", p.n("pe"))])
                p.dma("gpsimd", "dbg", lambda e: e.dma_start(out=odbg_sc, in_=o_buf[:, :]))
            sc.close()

        st_mm = {}
        fm_ok = {}
        p2_ok = {}
        den_ok = {}
        onum_ok = {}
        bsb_ok = {}
        supd_ok = {}
        ld_reuse = {}
        proj_ok = {}

        class Decoder:
            def __init__(self):
                self.sc = ExitStack()
                E = self.sc.enter_context
                self.pk = [E(nc.sbuf_tensor(f"pk{i}", [128, 1024], I32)) for i in range(2)]
                self.m32 = [E(nc.sbuf_tensor(f"m32{i}", [128, 1024], I32)) for i in range(2)]
                self.s32 = [E(nc.sbuf_tensor(f"s32{i}", [128, 1024], I32)) for i in range(2)]
                self.c32 = [E(nc.sbuf_tensor(f"c32{i}", [128, 1024], I32)) for i in range(2)]
                self.dec = [E(nc.sbuf_tensor(f"dec{i}", [128, 2048], BF16)) for i in range(2)]
                self.scf = [E(nc.sbuf_tensor(f"scf{i}", [128, 64], F32)) for i in range(2)]
                self.scb = [E(nc.sbuf_tensor(f"scb{i}", [128, 64], BF16)) for i in range(2)]
                self.scfD = E(nc.sbuf_tensor("scfD", [128, 512], F32))
                self.scbD = [E(nc.sbuf_tensor(f"scbD{i}", [128, 512], BF16)) for i in range(2)]
                self.wstg = [E(nc.sbuf_tensor(f"dstg{i}", [128, 512], BF16)) for i in range(2)]
                self.psDC = [E(nc.psum_tensor(f"psDC{i}", [128, 1024], BF16)) for i in range(2)]
                self.units = ([("g", nb, 0) for nb in range(128)] +
                              [("u", nb, 0) for nb in range(128)] +
                              [("d", hb, sb) for hb in range(16) for sb in range(8)])
                self.n = len(self.units)
                self.e1 = 0
                self.e2 = 0
                self.e3 = 0
                self.u_ok = {}
                self.s_ok = {}
                self.c_ok = {}
                self.or_ok = {}
                self.scale_ok = {}
                self.scv = {}
                self.cvt_done = {}
                self.dcvt = {}
                self.dscale_prev = {}
                self.trall = {}
                self.ev_done = {}
                self.wr_done = {}
                self.wr_unit = {}

            def stage1(self):
                i = self.e1
                if i >= self.n:
                    return
                kind, rb, sb = self.units[i]
                pksrc = {"g": gpk_in, "u": upk_in, "d": dpk_in}[kind]
                w = [("dve", self.s_ok.get(i - 2, 0))]
                if kind == "d":
                    tok = p.dma("sync", "pk", lambda e, pksrc=pksrc, rb=rb, sb=sb, i=i: e.dma_start(
                        out=self.pk[i % 2][:, :],
                        in_=pksrc[rb * 128:(rb + 1) * 128, sb * 1024:(sb + 1) * 1024]),
                        waits=[x for x in w if x[1] > 0], slot=i % 2)
                else:
                    tok = p.dma("sync", "pk", lambda e, pksrc=pksrc, rb=rb, i=i: e.dma_start(
                        out=self.pk[i % 2][:, :], in_=pksrc[rb * 128:(rb + 1) * 128, :]),
                        waits=[x for x in w if x[1] > 0], slot=i % 2)
                if kind != "d":
                    slot = i % 2
                    w2 = [("act", self.cvt_done.get(i - 2, 0))]
                    sl = p.dma("sync", "scg", lambda e, kind=kind, rb=rb, slot=slot: e.dma_start(
                        out=self.scf[slot][:, 0:64],
                        in_=(gsc_in if kind == "g" else usc_in)[rb * 128:(rb + 1) * 128, :]),
                        waits=[x for x in w2 if x[1] > 0], slot=slot)
                    w3 = [sl, ("pool", self.scale_ok.get(i - 2, 0))]
                    p.add("scalar", lambda e, slot=slot: e.copy(
                        out=self.scb[slot][:, 0:64], in_=self.scf[slot][:, 0:64]),
                        waits=[x for x in w3 if not (isinstance(x[1], int) and x[1] <= 0)],
                        incs=[("act", 1)])
                    self.cvt_done[i] = p.n("act")
                    self.scv[i] = (self.scb[slot][:, 0:64], ("act", self.cvt_done[i]))
                else:
                    if sb == 0:
                        slot = rb % 2
                        w2 = [("act", self.dcvt.get(rb - 1, 0))]
                        sl = p.dma("sync", "scd", lambda e, rb=rb: e.dma_start(
                            out=self.scfD[:, :], in_=dsc_in[rb * 128:(rb + 1) * 128, :]),
                            waits=[x for x in w2 if x[1] > 0], slot=slot)
                        w3 = [sl, ("pool", self.dscale_prev.get(rb - 2, 0))]
                        p.add("scalar", lambda e, slot=slot: e.copy(
                            out=self.scbD[slot][:, :], in_=self.scfD[:, :]),
                            waits=[x for x in w3 if not (isinstance(x[1], int) and x[1] <= 0)],
                            incs=[("act", 1)])
                        self.dcvt[rb] = p.n("act")
                    self.scv[i] = (self.scbD[rb % 2][:, sb * 64:(sb + 1) * 64],
                                   ("act", self.dcvt[rb]))
                p.add("gpsimd", lambda e, i=i: e.tensor_scalar(
                    out=self.pk[i % 2][:, :], in0=self.pk[i % 2][:, :],
                    scalar1=4097, scalar2=None, op0=OP.mult),
                    waits=[tok], incs=[("pool", 1)])
                self.u_ok[i] = p.n("pool")
                w = [("pool", self.u_ok[i])]
                p.add("vector", lambda e, i=i: e.tensor_scalar(
                    out=self.m32[i % 2][:, :], in0=self.pk[i % 2][:, :],
                    scalar1=0x00070007, scalar2=None, op0=OP.bitwise_and),
                    waits=[x for x in w if x[1] > 0])
                p.add("vector", lambda e, i=i: e.tensor_scalar(
                    out=self.s32[i % 2][:, :], in0=self.pk[i % 2][:, :],
                    scalar1=0x00080008, scalar2=12,
                    op0=OP.bitwise_and, op1=OP.logical_shift_left),
                    incs=[("dve", 1)])
                self.s_ok[i] = p.n("dve")
                self.e1 = i + 1

            def stage2(self):
                i = self.e2
                if i >= self.n or i >= self.e1:
                    return
                p.add("vector", lambda e, i=i: e.tensor_scalar(
                    out=self.c32[i % 2][:, :].bitcast(I16), in0=self.m32[i % 2][:, :].bitcast(I16),
                    scalar1=0x40, scalar2=0x3F00, op0=OP.mult, op1=OP.add),
                    incs=[("dve", 1)])
                self.c_ok[i] = p.n("dve")
                self.e2 = i + 1

            def stage3(self):
                i = self.e3
                if i >= self.n or i >= self.e2:
                    return
                kind, rb, sb = self.units[i]
                p.add("vector", lambda e, i=i: e.tensor_scalar(
                    out=self.m32[i % 2][:, :].bitcast(BF16), in0=self.c32[i % 2][:, :].bitcast(BF16),
                    scalar1=-1.0, scalar2=0.0, op0=OP.add, op1=OP.min))
                p.add("vector", lambda e, i=i: e.tensor_tensor(
                    out=self.c32[i % 2][:, :].bitcast(BF16), in0=self.m32[i % 2][:, :].bitcast(BF16),
                    in1=self.c32[i % 2][:, :].bitcast(BF16), op=OP.add))
                p.add("vector", lambda e, i=i: e.tensor_tensor(
                    out=self.c32[i % 2][:, :], in0=self.s32[i % 2][:, :], in1=self.c32[i % 2][:, :],
                    op=OP.bitwise_or), incs=[("dve", 1)])
                self.or_ok[i] = p.n("dve")
                scsl, stok = self.scv[i]
                sc3 = scsl.unsqueeze(2).broadcast_to([128, 64, 32])
                w = [("dve", self.or_ok[i]), stok, ("pe", self.trall.get(i - 2, 0))]
                p.add("gpsimd", lambda e, i=i, sc3=sc3: e.tensor_tensor(
                    out=self.dec[i % 2][:, :].rearrange("p (b k) -> p b k", b=64),
                    in0=sc3, in1=self.c32[i % 2][:, :].bitcast(BF16).rearrange("p (b k) -> p b k", b=64),
                    op=OP.mult),
                    waits=[x for x in w if not (isinstance(x[1], int) and x[1] <= 0)],
                    incs=[("pool", 1)])
                self.scale_ok[i] = p.n("pool")
                if kind == "d" and sb == 7:
                    self.dscale_prev[rb] = self.scale_ok[i]
                dstt = {"g": wg_sc, "u": wu_sc, "d": wd_sc}[kind]
                rowblk = rb if kind != "d" else sb * 16 + rb
                for g in range(4):
                    lv = i * 4 + g
                    w = [("pool", self.scale_ok[i]), CONSTS_TOK] if g == 0 else []
                    if lv >= 2:
                        w.append(("act", self.ev_done[lv - 2]))
                    for j in range(4):
                        kc = g * 4 + j
                        p.add("tensor", lambda e, i=i, g=g, j=j, kc=kc: e.transpose(
                            self.psDC[g % 2][:, j * 128:(j + 1) * 128],
                            self.dec[i % 2][:, kc * 128:(kc + 1) * 128], idb[:, :]),
                            waits=(w if j == 0 else []),
                            incs=([("pe", 1)] if j == 3 else []))
                    tr_ok = p.n("pe")
                    w2 = [("pe", tr_ok)]
                    if lv >= 2:
                        w2.append(self.wr_done[lv - 2])
                    p.add("scalar", lambda e, g=g: e.copy(
                        out=self.wstg[g % 2][:, :], in_=self.psDC[g % 2][:, 0:512]),
                        waits=w2, incs=[("act", 1)])
                    self.ev_done[lv] = p.n("act")
                    self.wr_done[lv] = p.dma("scalar", "dw",
                        lambda e, dstt=dstt, rowblk=rowblk, g=g: e.dma_start(
                            out=dstt[rowblk * 128:(rowblk + 1) * 128, g * 512:(g + 1) * 512],
                            in_=self.wstg[g % 2][:, :]),
                        waits=[("act", self.ev_done[lv])], slot=lv % 2)
                self.trall[i] = p.n("pe")
                self.wr_unit[(kind, rowblk)] = self.wr_done[i * 4 + 3]
                self.e3 = i + 1

            def drain(self, k):
                for _ in range(k):
                    self.stage3()
                    self.stage2()
                    self.stage1()

            def flush(self):
                while self.e3 < self.n:
                    self.stage1()
                    self.stage2()
                    self.stage3()
                self.sc.close()


        def ph_attn_b(Xln):
            sc = ExitStack()
            E = sc.enter_context
            gsc_sb = E(nc.sbuf_tensor("gsc_sb", [128, 256], BF16))
            p1s = E(nc.sbuf_tensor("p1s", [128, 2 * 256], BF16))
            p1n = E(nc.sbuf_tensor("p1n", [128, 2 * 256], BF16))
            p2g = E(nc.sbuf_tensor("p2g", [128, 768], BF16))
            nmx = E(nc.sbuf_tensor("nmx", [128, 2], F32))
            denb = E(nc.sbuf_tensor("denb", [128, 2], F32))
            rdb = E(nc.sbuf_tensor("rdb", [128, 2], F32))
            abdbg = E(nc.sbuf_tensor("abdbg", [128, 2048], F32))
            psS1 = [E(nc.psum_tensor(f"psS1{i}", [128, 512], F32)) for i in range(2)]
            psP2 = E(nc.psum_tensor("psP2", [128, 1024], BF16))
            psO = E(nc.psum_tensor("psO", [128, 512], F32))

            gsc_ok = 0
            exp_ok = {}
            evac_ok = 0
            xadd_ok = 0
            prev_sc = 0
            for s in range(NSEG):
                for h in range(H):
                    i = s * H + h
                    par = h % 2
                    base = h * T + s * 512 + par

                    def gv(lo, cnt, base=base):
                        st = base + 2 * lo
                        return Xln[:, st: st + 2 * cnt - 1: 2]

                    w = [("pe", prev_sc)] if i >= 1 else []
                    p.add("vector", lambda e, gv=gv: e.tensor_scalar(
                        out=gsc_sb[:, :], in0=gv(0, 256), scalar1=BSCALE, scalar2=None, op0=OP.mult),
                        waits=w, incs=[("dve", 1)])
                    gsc_ok = p.n("dve")
                    w = [("dve", gsc_ok)]
                    if i >= 1:
                        w.append(("act", exp_ok[i - 1]))
                    for lc in range(2):
                        p.add("tensor", lambda e, lc=lc, gv=gv: e.matmul(
                            psS1[lc][:, 0:256], gsc_sb[:, lc * 128:(lc + 1) * 128],
                            gv(0, 256), start=True, stop=True),
                            waits=(w if lc == 0 else []),
                            incs=([("pe", 1)] if lc == 1 else []))
                    sc_ok = p.n("pe")
                    prev_sc = sc_ok
                    for lc in range(2):
                        p.add("vector", lambda e, lc=lc: e.reduce_max(
                            out=nmx[:, lc:lc + 1], in_=psS1[lc][:, 0:256], axis=AX.X, negate=True),
                            waits=([("pe", sc_ok)] if lc == 0 else []),
                            incs=([("dve", 1)] if lc == 1 else []))
                    nmx_ok = p.n("dve")
                    for lc in range(2):
                        p.add("scalar", lambda e, lc=lc: e.activation(
                            out=p1s[:, lc * 256:(lc + 1) * 256], in_=psS1[lc][:, 0:256],
                            func=AF.Exp, bias=nmx[:, lc:lc + 1]),
                            waits=([("dve", nmx_ok)] if lc == 0 else []),
                            incs=([("act", 1)] if lc == 1 else []))
                    exp_ok[i] = p.n("act")
                    for lc in range(2):
                        p.add("vector", lambda e, lc=lc: e.reduce_sum(
                            out=denb[:, lc:lc + 1], in_=p1s[:, lc * 256:(lc + 1) * 256], axis=AX.X),
                            waits=([("act", exp_ok[i])] if lc == 0 else []))
                    p.add("vector", lambda e: e.reciprocal(out=rdb[:, :], in_=denb[:, :]))
                    p.add("vector", lambda e: e.tensor_copy(out=nmx[:, :], in_=denb[:, :]))
                    p.add("vector", lambda e: e.tensor_copy(out=nmx[:, :], in_=denb[:, :]))
                    for lc in range(2):
                        p.add("vector", lambda e, lc=lc: e.tensor_scalar(
                            out=p1n[:, lc * 256:(lc + 1) * 256], in0=p1s[:, lc * 256:(lc + 1) * 256],
                            scalar1=rdb[:, lc:lc + 1], scalar2=None, op0=OP.mult),
                            incs=([("dve", 1)] if lc == 1 else []))
                    p1n_ok = p.n("dve")
                    w = [("dve", p1n_ok)]
                    if i >= 1:
                        w.append(("act", evac_ok))
                    for mc in range(2):
                        for lc in range(2):
                            p.add("tensor", lambda e, mc=mc, lc=lc: e.transpose(
                                psP2[:, mc * 256 + lc * 128: mc * 256 + (lc + 1) * 128],
                                p1n[:, lc * 256 + mc * 128: lc * 256 + (mc + 1) * 128], idb[:, :]),
                                waits=(w if (mc == 0 and lc == 0) else []))
                    for mc in range(2):
                        p.add("tensor", lambda e, mc=mc, gv=gv: e.transpose(
                            psP2[:, 512 + mc * 128: 512 + (mc + 1) * 128],
                            gv(mc * 128, 128), idb[:, :]),
                            incs=([("pe", 1)] if mc == 1 else []))
                    tr_ok = p.n("pe")
                    w2 = [("pe", tr_ok)]
                    if i >= 1:
                        w2.append(("dve", xadd_ok))
                    p.add("scalar", lambda e: e.copy(out=p2g[:, :], in_=psP2[:, 0:768]),
                          waits=w2, incs=[("act", 1)])
                    evac_ok = p.n("act")
                    for mc in range(2):
                        wav = [("act", evac_ok)]
                        if i >= 1:
                            wav.append(("dve", xadd_ok))
                        p.add("tensor", lambda e, mc=mc: e.matmul(
                            psO[:, 0:256], p2g[:, 512 + mc * 128: 512 + (mc + 1) * 128],
                            p2g[:, mc * 256:(mc + 1) * 256], start=(mc == 0), stop=(mc == 1)),
                            waits=(wav if mc == 0 else []),
                            incs=([("pe", 1)] if mc == 1 else []))
                    av_ok = p.n("pe")
                    if i == 0 and "abdbg_sc" in DEBUG_SCRATCH:
                        p.add("vector", lambda e: e.tensor_copy(out=abdbg[:, 0:256], in_=gsc_sb[:, :]),
                              waits=[("pe", av_ok)])
                        p.add("vector", lambda e: e.tensor_copy(out=abdbg[:, 256:512], in_=p1n[:, 0:256]))
                        p.add("vector", lambda e: e.tensor_copy(out=abdbg[:, 512:768], in_=p1n[:, 256:512]))
                        p.add("vector", lambda e: e.tensor_copy(out=abdbg[:, 768:770], in_=denb[:, :]))
                        p.add("vector", lambda e: e.tensor_copy(out=abdbg[:, 770:772], in_=rdb[:, :]))
                        p.add("vector", lambda e: e.tensor_copy(out=abdbg[:, 772:1540], in_=p2g[:, :]))
                        p.add("vector", lambda e: e.tensor_copy(out=abdbg[:, 1540:1796], in_=psO[:, 0:256]),
                              incs=[("dve", 1)])
                        p.dma("gpsimd", "dbg", lambda e: e.dma_start(out=abdbg_sc[:, :], in_=abdbg[:, :]),
                              waits=[("dve", p.n("dve"))])
                    p.add("vector", lambda e, base=base: e.tensor_tensor(
                        out=Xs_strided(base), in0=Xs_strided(base), in1=psO[:, 0:256], op=OP.add),
                        waits=[("pe", av_ok)], incs=[("dve", 1)])
                    xadd_ok = p.n("dve")
            sc.close()

        def Xs_strided(base):
            return X[:, base: base + 511: 2]

        def ph_out():
            sc = ExitStack()
            E = sc.enter_context
            yst = [E(nc.sbuf_tensor(f"yst{i}", [128, D], F32)) for i in range(2)]
            psY = [E(nc.psum_tensor(f"psY{i}", [128, 512], F32)) for i in range(2)]
            evy = {}
            wry = {}
            for tc in range(TC):
                for g in range(4):
                    lin = tc * 4 + g
                    w = []
                    if lin >= 2:
                        w.append(("dve", evy[lin - 2]))
                    if g == 0 and tc >= 2:
                        w.append(wry[tc - 2])
                    for j in range(4):
                        fc = g * 4 + j
                        p.add("tensor", lambda e, tc=tc, g=g, j=j, fc=fc: e.transpose(
                            psY[g % 2][:, j * 128:(j + 1) * 128],
                            X[:, fc * T + tc * 128: fc * T + tc * 128 + 128], idf[:, :]),
                            waits=(w if j == 0 else []),
                            incs=([("pe", 1)] if j == 3 else []))
                    pe_ok = p.n("pe")
                    p.add("vector", lambda e, tc=tc, g=g: e.tensor_copy(
                        out=yst[tc % 2][:, :].rearrange("p (g j f) -> p g j f", g=4, j=4)[:, g, :, :],
                        in_=psY[g % 2][:, :].rearrange("p (j f) -> p j f", j=4)),
                        waits=[("pe", pe_ok)], incs=[("dve", 1)])
                    evy[lin] = p.n("dve")
                wry[tc] = p.dma("gpsimd", "yout", lambda e, tc=tc: e.dma_start(
                    out=y_out[tc * 128:(tc + 1) * 128, :], in_=yst[tc % 2][:, :]),
                    waits=[("dve", evy[tc * 4 + 3])], slot=tc % 2)
            sc.close()

        def ph_ffn(Xln, tagn):
            sc = ExitStack()
            E = sc.enter_context
            wgb = [E(nc.sbuf_tensor(f"wgb{tagn}{i}", [128, 2048], BF16)) for i in range(2)]
            wub = [E(nc.sbuf_tensor(f"wub{tagn}{i}", [128, 2048], BF16)) for i in range(2)]
            wdb = [E(nc.sbuf_tensor(f"wdb{tagn}{i}", [128, 2048], BF16)) for i in range(2)]
            inter = E(nc.sbuf_tensor(f"inter{tagn}", [128, 16 * 1024], BF16))
            sqb = E(nc.sbuf_tensor(f"sqb{tagn}", [128, 1024], F32))
            cgb = E(nc.sbuf_tensor(f"cgb{tagn}", [128, 1024], F32))
            sgb = E(nc.sbuf_tensor(f"sgb{tagn}", [128, 1024], F32))
            glb = E(nc.sbuf_tensor(f"glb{tagn}", [128, 1024], BF16))
            usb = E(nc.sbuf_tensor(f"usb{tagn}", [128, 1024], BF16))
            psG = E(nc.psum_tensor(f"psG{tagn}", [128, 1024], F32))
            psUp = E(nc.psum_tensor(f"psU{tagn}", [128, 1024], F32))
            psDn = [E(nc.psum_tensor(f"psDn{tagn}{i}", [128, 512], F32)) for i in range(2)]

            gld = {}
            uld = {}
            dld = {}
            gmm = {}
            umm = {}
            inter_ok = {}
            gl_ok = {}
            ue_ok = {}
            dadd = {}
            for ng in range(8):
                for ncx in range(16):
                    nb = ng * 16 + ncx
                    w = [("pe", gmm.get(nb - 2, 0))]
                    gld[nb] = p.dma("sync", "wgb", lambda e, nb=nb: e.dma_start(
                        out=wgb[nb % 2][:, :], in_=wg_sc[nb * 128:(nb + 1) * 128, :]),
                        waits=[x for x in w if x[1] > 0], slot=nb % 2)
                    w = [("pe", umm.get(nb - 2, 0))]
                    uld[nb] = p.dma("sync", "wub", lambda e, nb=nb: e.dma_start(
                        out=wub[nb % 2][:, :], in_=wu_sc[nb * 128:(nb + 1) * 128, :]),
                        waits=[x for x in w if x[1] > 0], slot=nb % 2)
                    w = [gld[nb]]
                    if nb >= 1:
                        w.append(("dve", gl_ok[nb - 1]))
                    for kc in range(16):
                        for half in range(2):
                            p.add("tensor", lambda e, nb=nb, kc=kc, half=half: e.matmul(
                                psG[:, half * 512:(half + 1) * 512],
                                wgb[nb % 2][:, kc * 128:(kc + 1) * 128],
                                Xln[:, kc * T + half * 512: kc * T + (half + 1) * 512],
                                start=(kc == 0), stop=(kc == 15)),
                                waits=(w if kc == 0 and half == 0 else []),
                                incs=([("pe", 1)] if kc == 15 and half == 1 else []))
                    gmm[nb] = p.n("pe")
                    w = [uld[nb]]
                    if nb >= 1:
                        w.append(("act", ue_ok[nb - 1]))
                    for kc in range(16):
                        for half in range(2):
                            p.add("tensor", lambda e, nb=nb, kc=kc, half=half: e.matmul(
                                psUp[:, half * 512:(half + 1) * 512],
                                wub[nb % 2][:, kc * 128:(kc + 1) * 128],
                                Xln[:, kc * T + half * 512: kc * T + (half + 1) * 512],
                                start=(kc == 0), stop=(kc == 15)),
                                waits=(w if kc == 0 and half == 0 else []),
                                incs=([("pe", 1)] if kc == 15 and half == 1 else []))
                    umm[nb] = p.n("pe")
                    w = [("pe", gmm[nb])]
                    if nb >= 1:
                        w.append(("dve", cg_prev[nb - 1]))
                    p.add("scalar", lambda e: e.activation(
                        out=sqb[:, :], in_=psG[:, :], func=AF.Square),
                        waits=w, incs=[("act", 1)])
                    sq_ok = p.n("act")
                    p.add("vector", lambda e: e.tensor_scalar(
                        out=cgb[:, :], in0=sqb[:, :], scalar1=GELU_A, scalar2=1.0,
                        op0=OP.mult, op1=OP.add),
                        waits=[("act", sq_ok)])
                    p.add("vector", lambda e: e.tensor_tensor(
                        out=cgb[:, :], in0=cgb[:, :], in1=psG[:, :], op=OP.mult),
                        incs=[("dve", 1)])
                    cg_ok = p.n("dve")
                    cg_prev[nb] = cg_ok
                    w = [("dve", cg_ok)]
                    if nb >= 1:
                        w.append(("dve", gl_ok[nb - 1]))
                    p.add("scalar", lambda e: e.activation(
                        out=sgb[:, :], in_=cgb[:, :], func=AF.Sigmoid, scale=np.float32(GELU_S)),
                        waits=w, incs=[("act", 1)])
                    sg_ok = p.n("act")
                    w = [("pe", umm[nb])]
                    if nb >= 1:
                        w.append(("pool", inter_ok[nb - 1]))
                    p.add("scalar", lambda e: e.copy(out=usb[:, :], in_=psUp[:, :]),
                          waits=w, incs=[("act", 1)])
                    ue_ok[nb] = p.n("act")
                    w = [("act", sg_ok)]
                    if nb >= 1:
                        w.append(("pool", inter_ok[nb - 1]))
                    p.add("vector", lambda e: e.tensor_tensor(
                        out=glb[:, :], in0=sgb[:, :], in1=psG[:, :], op=OP.mult),
                        waits=w, incs=[("dve", 1)])
                    gl_ok[nb] = p.n("dve")
                    w = [("act", ue_ok[nb]), ("dve", gl_ok[nb])]
                    if ng >= 1 and ncx == 0:
                        w.append(("pe", dmmprev[(ng - 1, 15)]))
                    p.add("gpsimd", lambda e, ncx=ncx: e.tensor_tensor(
                        out=inter[:, ncx * 1024:(ncx + 1) * 1024], in0=usb[:, :], in1=glb[:, :],
                        op=OP.mult),
                        waits=w, incs=[("pool", 1)])
                    inter_ok[nb] = p.n("pool")
                for hc in range(16):
                    if hc >= 2:
                        w = [("pe", dmmprev[(ng, hc - 2)])]
                    elif ng >= 1:
                        w = [("pe", dmmprev[(ng - 1, 14 + hc)])]
                    else:
                        w = []
                    dld[(ng, hc)] = p.dma("sync", "wdb", lambda e, ng=ng, hc=hc: e.dma_start(
                        out=wdb[hc % 2][:, :], in_=wd_sc[(ng * 16 + hc) * 128:(ng * 16 + hc + 1) * 128, :]),
                        waits=w, slot=hc % 2)
                    w = [dld[(ng, hc)], ("pool", inter_ok[ng * 16 + 15])]
                    for th in range(2):
                        wth = list(w) if th == 0 else []
                        idx = hc * 2 + th
                        if idx >= 2:
                            wth.append(("dve", dadd[(ng, idx - 2)]))
                        elif ng >= 1:
                            wth.append(("dve", dadd[(ng - 1, 30 + idx)]))
                        for ncx in range(16):
                            p.add("tensor", lambda e, ng=ng, hc=hc, th=th, ncx=ncx: e.matmul(
                                psDn[(hc * 2 + th) % 2][:, :],
                                wdb[hc % 2][:, ncx * 128:(ncx + 1) * 128],
                                inter[:, ncx * 1024 + th * 512: ncx * 1024 + (th + 1) * 512],
                                start=(ncx == 0), stop=(ncx == 15)),
                                waits=(wth if ncx == 0 else []),
                                incs=([("pe", 1)] if ncx == 15 else []))
                        dm_ok = p.n("pe")
                        p.add("vector", lambda e, hc=hc, th=th, ng=ng, idx=idx: e.tensor_tensor(
                            out=X[:, hc * T + th * 512: hc * T + (th + 1) * 512],
                            in0=X[:, hc * T + th * 512: hc * T + (th + 1) * 512],
                            in1=psDn[idx % 2][:, :], op=OP.add),
                            waits=[("pe", dm_ok)], incs=[("dve", 1)])
                        dadd[(ng, idx)] = p.n("dve")
                    dmmprev[(ng, hc)] = p.n("pe")
            sc.close()
        dmmprev = {}
        cg_prev = {}

        def ph_attn_b(Xln):
            sc = ExitStack()
            E = sc.enter_context
            gsc_sb = E(nc.sbuf_tensor("gsc_sb", [128, 256], BF16))
            p1s = E(nc.sbuf_tensor("p1s", [128, 2 * 256], BF16))
            p1n = E(nc.sbuf_tensor("p1n", [128, 2 * 256], BF16))
            p2g = E(nc.sbuf_tensor("p2g", [128, 768], BF16))
            nmx = E(nc.sbuf_tensor("nmx", [128, 2], F32))
            denb = E(nc.sbuf_tensor("denb", [128, 2], F32))
            rdb = E(nc.sbuf_tensor("rdb", [128, 2], F32))
            abdbg = E(nc.sbuf_tensor("abdbg", [128, 2048], F32))
            psS1 = [E(nc.psum_tensor(f"psS1{i}", [128, 512], F32)) for i in range(2)]
            psP2 = E(nc.psum_tensor("psP2", [128, 1024], BF16))
            psO = E(nc.psum_tensor("psO", [128, 512], F32))

            gsc_ok = 0
            exp_ok = {}
            evac_ok = 0
            xadd_ok = 0
            prev_sc = 0
            for s in range(NSEG):
                for h in range(H):
                    i = s * H + h
                    par = h % 2
                    base = h * T + s * 512 + par

                    def gv(lo, cnt, base=base):
                        st = base + 2 * lo
                        return Xln[:, st: st + 2 * cnt - 1: 2]

                    w = [("pe", prev_sc)] if i >= 1 else []
                    p.add("vector", lambda e, gv=gv: e.tensor_scalar(
                        out=gsc_sb[:, :], in0=gv(0, 256), scalar1=BSCALE, scalar2=None, op0=OP.mult),
                        waits=w, incs=[("dve", 1)])
                    gsc_ok = p.n("dve")
                    w = [("dve", gsc_ok)]
                    if i >= 1:
                        w.append(("act", exp_ok[i - 1]))
                    for lc in range(2):
                        p.add("tensor", lambda e, lc=lc, gv=gv: e.matmul(
                            psS1[lc][:, 0:256], gsc_sb[:, lc * 128:(lc + 1) * 128],
                            gv(0, 256), start=True, stop=True),
                            waits=(w if lc == 0 else []),
                            incs=([("pe", 1)] if lc == 1 else []))
                    sc_ok = p.n("pe")
                    prev_sc = sc_ok
                    for lc in range(2):
                        p.add("vector", lambda e, lc=lc: e.reduce_max(
                            out=nmx[:, lc:lc + 1], in_=psS1[lc][:, 0:256], axis=AX.X, negate=True),
                            waits=([("pe", sc_ok)] if lc == 0 else []),
                            incs=([("dve", 1)] if lc == 1 else []))
                    nmx_ok = p.n("dve")
                    for lc in range(2):
                        p.add("scalar", lambda e, lc=lc: e.activation(
                            out=p1s[:, lc * 256:(lc + 1) * 256], in_=psS1[lc][:, 0:256],
                            func=AF.Exp, bias=nmx[:, lc:lc + 1]),
                            waits=([("dve", nmx_ok)] if lc == 0 else []),
                            incs=([("act", 1)] if lc == 1 else []))
                    exp_ok[i] = p.n("act")
                    for lc in range(2):
                        p.add("vector", lambda e, lc=lc: e.reduce_sum(
                            out=denb[:, lc:lc + 1], in_=p1s[:, lc * 256:(lc + 1) * 256], axis=AX.X),
                            waits=([("act", exp_ok[i])] if lc == 0 else []))
                    p.add("vector", lambda e: e.reciprocal(out=rdb[:, :], in_=denb[:, :]))
                    p.add("vector", lambda e: e.tensor_copy(out=nmx[:, :], in_=denb[:, :]))
                    p.add("vector", lambda e: e.tensor_copy(out=nmx[:, :], in_=denb[:, :]))
                    for lc in range(2):
                        p.add("vector", lambda e, lc=lc: e.tensor_scalar(
                            out=p1n[:, lc * 256:(lc + 1) * 256], in0=p1s[:, lc * 256:(lc + 1) * 256],
                            scalar1=rdb[:, lc:lc + 1], scalar2=None, op0=OP.mult),
                            incs=([("dve", 1)] if lc == 1 else []))
                    p1n_ok = p.n("dve")
                    w = [("dve", p1n_ok)]
                    if i >= 1:
                        w.append(("act", evac_ok))
                    for mc in range(2):
                        for lc in range(2):
                            p.add("tensor", lambda e, mc=mc, lc=lc: e.transpose(
                                psP2[:, mc * 256 + lc * 128: mc * 256 + (lc + 1) * 128],
                                p1n[:, lc * 256 + mc * 128: lc * 256 + (mc + 1) * 128], idb[:, :]),
                                waits=(w if (mc == 0 and lc == 0) else []))
                    for mc in range(2):
                        p.add("tensor", lambda e, mc=mc, gv=gv: e.transpose(
                            psP2[:, 512 + mc * 128: 512 + (mc + 1) * 128],
                            gv(mc * 128, 128), idb[:, :]),
                            incs=([("pe", 1)] if mc == 1 else []))
                    tr_ok = p.n("pe")
                    w2 = [("pe", tr_ok)]
                    if i >= 1:
                        w2.append(("dve", xadd_ok))
                    p.add("scalar", lambda e: e.copy(out=p2g[:, :], in_=psP2[:, 0:768]),
                          waits=w2, incs=[("act", 1)])
                    evac_ok = p.n("act")
                    for mc in range(2):
                        wav = [("act", evac_ok)]
                        if i >= 1:
                            wav.append(("dve", xadd_ok))
                        p.add("tensor", lambda e, mc=mc: e.matmul(
                            psO[:, 0:256], p2g[:, 512 + mc * 128: 512 + (mc + 1) * 128],
                            p2g[:, mc * 256:(mc + 1) * 256], start=(mc == 0), stop=(mc == 1)),
                            waits=(wav if mc == 0 else []),
                            incs=([("pe", 1)] if mc == 1 else []))
                    av_ok = p.n("pe")
                    if i == 0 and "abdbg_sc" in DEBUG_SCRATCH:
                        p.add("vector", lambda e: e.tensor_copy(out=abdbg[:, 0:256], in_=gsc_sb[:, :]),
                              waits=[("pe", av_ok)])
                        p.add("vector", lambda e: e.tensor_copy(out=abdbg[:, 256:512], in_=p1n[:, 0:256]))
                        p.add("vector", lambda e: e.tensor_copy(out=abdbg[:, 512:768], in_=p1n[:, 256:512]))
                        p.add("vector", lambda e: e.tensor_copy(out=abdbg[:, 768:770], in_=denb[:, :]))
                        p.add("vector", lambda e: e.tensor_copy(out=abdbg[:, 770:772], in_=rdb[:, :]))
                        p.add("vector", lambda e: e.tensor_copy(out=abdbg[:, 772:1540], in_=p2g[:, :]))
                        p.add("vector", lambda e: e.tensor_copy(out=abdbg[:, 1540:1796], in_=psO[:, 0:256]),
                              incs=[("dve", 1)])
                        p.dma("gpsimd", "dbg", lambda e: e.dma_start(out=abdbg_sc[:, :], in_=abdbg[:, :]),
                              waits=[("dve", p.n("dve"))])
                    p.add("vector", lambda e, base=base: e.tensor_tensor(
                        out=Xs_strided(base), in0=Xs_strided(base), in1=psO[:, 0:256], op=OP.add),
                        waits=[("pe", av_ok)], incs=[("dve", 1)])
                    xadd_ok = p.n("dve")
            sc.close()

        def Xs_strided(base):
            return X[:, base: base + 511: 2]

        def ph_out():
            sc = ExitStack()
            E = sc.enter_context
            yst = [E(nc.sbuf_tensor(f"yst{i}", [128, D], F32)) for i in range(2)]
            psY = [E(nc.psum_tensor(f"psY{i}", [128, 512], F32)) for i in range(2)]
            evy = {}
            wry = {}
            for tc in range(TC):
                for g in range(4):
                    lin = tc * 4 + g
                    w = []
                    if lin >= 2:
                        w.append(("dve", evy[lin - 2]))
                    if g == 0 and tc >= 2:
                        w.append(wry[tc - 2])
                    for j in range(4):
                        fc = g * 4 + j
                        p.add("tensor", lambda e, tc=tc, g=g, j=j, fc=fc: e.transpose(
                            psY[g % 2][:, j * 128:(j + 1) * 128],
                            X[:, fc * T + tc * 128: fc * T + tc * 128 + 128], idf[:, :]),
                            waits=(w if j == 0 else []),
                            incs=([("pe", 1)] if j == 3 else []))
                    pe_ok = p.n("pe")
                    p.add("vector", lambda e, tc=tc, g=g: e.tensor_copy(
                        out=yst[tc % 2][:, :].rearrange("p (g j f) -> p g j f", g=4, j=4)[:, g, :, :],
                        in_=psY[g % 2][:, :].rearrange("p (j f) -> p j f", j=4)),
                        waits=[("pe", pe_ok)], incs=[("dve", 1)])
                    evy[lin] = p.n("dve")
                wry[tc] = p.dma("gpsimd", "yout", lambda e, tc=tc: e.dma_start(
                    out=y_out[tc * 128:(tc + 1) * 128, :], in_=yst[tc % 2][:, :]),
                    waits=[("dve", evy[tc * 4 + 3])], slot=tc % 2)
            sc.close()

        DECODE_ON = STOP_AFTER in (None, "attnb", "ffn2", "ffn1", "decode") and "decode" not in DEBUG_SKIP
        dec_obj = Decoder() if DECODE_ON else None

        def drain2():
            if dec_obj is not None:
                dec_obj.drain(1)
        cos_sc = ExitStack()
        cose = cos_sc.enter_context(nc.sbuf_tensor("cose", [128, TC * 64], F32))
        coso = cos_sc.enter_context(nc.sbuf_tensor("coso", [128, TC * 64], F32))
        sine = cos_sc.enter_context(nc.sbuf_tensor("sine", [128, TC * 64], F32))
        sino = cos_sc.enter_context(nc.sbuf_tensor("sino", [128, TC * 64], F32))
        for dst, src in ((idb, idb_in), (idf, idf_in), (premask, premask_in), (onesb, ones_in), (lnc, lnc_in)):
            p.dma("sync", "const", lambda e, d=dst, s=src: e.dma_start(out=d[:, :], in_=s))
        for dst, src in ((cose, cose_in), (coso, coso_in), (sine, sine_in), (sino, sino_in)):
            p.dma("sync", "const", lambda e, d=dst, s=src: e.dma_start(
                out=d[:, :].rearrange("p (tc j) -> p tc j", tc=TC),
                in_=s.rearrange("(tc p) j -> p tc j", p=128)))
        p.dma("sync", "const", lambda e: e.dma_start(
            out=maska[:, :].rearrange("p (m l) -> p m l", m=2),
            in_=maska_in.rearrange("(m p) l -> p m l", p=128)))
        p.dma("sync", "const", lambda e: e.dma_start(out=ones_row[:, :], in_=onesr_in))
        CONSTS_TOK = p.dma("sync", "const", lambda e: e.dma_start(out=onesrf[:, :], in_=onesrf_in))
        Xsc = ExitStack()
        Xln1 = Xsc.enter_context(nc.sbuf_tensor("Xln1", [128, FC * T], BF16))
        ph_ln("1", Xln1, first=True, drain=drain2)
        barrier()
        if "xdbg_sc" in DEBUG_SCRATCH:
            p.dma("gpsimd", "dbg", lambda e: e.dma_start(out=xdbg_sc, in_=X[:, :]))
            p.dma("gpsimd", "dbg", lambda e: e.dma_start(out=xlndbg_sc, in_=Xln1[:, :]))
            barrier()
        if STOP_AFTER not in ("ln1",) and "wt" not in DEBUG_SKIP:
            ph_wt(drain=drain2)
            barrier()
        if STOP_AFTER not in ("ln1", "wt") and "qkv" not in DEBUG_SKIP:
            ph_qkv(Xln1, drain=drain2)
            barrier()
        Xsc.close()
        cos_sc.close()
        if STOP_AFTER not in ("ln1", "wt", "qkv") and "attna" not in DEBUG_SKIP:
            ph_attn_a(drain=drain2)
            barrier()
            if "xdbg_sc" in DEBUG_SCRATCH and STOP_AFTER == "attna":
                p.dma("gpsimd", "dbg", lambda e: e.dma_start(out=xdbg_sc, in_=X[:, :]))
                barrier()
        if dec_obj is not None:
            dec_obj.flush()
            barrier()
        if STOP_AFTER not in ("ln1", "wt", "qkv", "attna", "decode"):
            Xsc2 = ExitStack()
            Xln2 = Xsc2.enter_context(nc.sbuf_tensor("Xln2", [128, FC * T], BF16))
            ph_ln("2", Xln2, first=False)
            barrier()
            if STOP_AFTER == "ln2":
                if "xlndbg_sc" in DEBUG_SCRATCH:
                    p.dma("gpsimd", "dbg", lambda e: e.dma_start(out=xlndbg_sc, in_=Xln2[:, :]))
                    barrier()
            else:
                ph_ffn(Xln2, 1)
                barrier()
            Xsc2.close()
        if STOP_AFTER in (None, "attnb", "ffn2"):
            Xsc3 = ExitStack()
            Xln3 = Xsc3.enter_context(nc.sbuf_tensor("Xln3", [128, FC * T], BF16))
            ph_ln("3", Xln3, first=False)
            barrier()
            ph_attn_b(Xln3)
            Xsc3.close()
            barrier()
            if "xdbg_sc" in DEBUG_SCRATCH and STOP_AFTER == "attnb":
                p.dma("gpsimd", "dbg", lambda e: e.dma_start(out=xdbg_sc, in_=X[:, :]))
                barrier()
        if STOP_AFTER in (None, "ffn2"):
            Xsc4 = ExitStack()
            Xln4 = Xsc4.enter_context(nc.sbuf_tensor("Xln4", [128, FC * T], BF16))
            ph_ln("4", Xln4, first=False)
            barrier()
            ph_ffn(Xln4, 2)
            Xsc4.close()
            barrier()
        if STOP_AFTER is None:
            ph_out()
            if "xdbg_sc" in DEBUG_SCRATCH and STOP_AFTER == "ffn1":
                p.dma("gpsimd", "dbg", lambda e: e.dma_start(out=xdbg_sc, in_=X[:, :]))
                barrier()


        p.add("gpsimd", None, waits=p.all_sems(), incs=[("fin", 1)])

        block = es.enter_context(nc.Block())
        p.emit(block)
    return nc


def host_consts(core_id):
    pos0 = T * core_id
    invf = (10000.0 ** (-np.arange(0, HD, 2, dtype=np.float64) / HD))
    pos = np.arange(pos0, pos0 + T, dtype=np.float64)
    f = np.outer(pos, invf)
    emb = np.concatenate([f, f], axis=-1)
    cos = np.cos(emb).astype(np.float32)
    sin = np.sin(emb).astype(np.float32)
    return {
        "cose_in": np.ascontiguousarray(cos[:, 0::2]),
        "coso_in": np.ascontiguousarray(cos[:, 1::2]),
        "sine_in": np.ascontiguousarray(sin[:, 0::2]),
        "sino_in": np.ascontiguousarray(sin[:, 1::2]),
        "idb_in": np.eye(128, dtype=np.float32).astype(BF),
        "idf_in": np.eye(128, dtype=np.float32),
        "maska_in": np.concatenate([
            (np.arange(256)[None, :] >= np.arange(128)[:, None]).astype(np.float32),
            (np.arange(256)[None, :] >= (np.arange(128) + 128)[:, None]).astype(np.float32),
        ], axis=0),
        "ones_in": np.ones((128, 2), dtype=np.float32).astype(BF),
        "onesr_in": np.ones((1, 128), dtype=np.float32).astype(BF),
        "onesrf_in": np.ones((1, 128), dtype=np.float32),
        "lnc_in": np.broadcast_to(np.array([1.0 / D, -1.0], np.float32), (128, 2)).copy(),
        "premask_in": np.broadcast_to(
            (np.arange(8)[None, :] < core_id).astype(np.float32), (128, 8)).copy(),
    }


def make_in_maps(inputs):
    x = np.ascontiguousarray(np.asarray(inputs["L0_attn_a_x"], dtype=np.float32)[0])
    shared = {
        "wqkv_in": np.ascontiguousarray(np.asarray(inputs["L0_attn_a_W_qkv"], dtype=np.float32)),
        "wout_in": np.ascontiguousarray(np.asarray(inputs["L0_attn_a_W_out"], dtype=np.float32)),
        "gpk_in": np.ascontiguousarray(np.asarray(inputs["L0_ffn_gate_W_packed"], dtype=np.int32)),
        "gsc_in": np.ascontiguousarray(np.asarray(inputs["L0_ffn_gate_scales"], dtype=np.float32)),
        "upk_in": np.ascontiguousarray(np.asarray(inputs["L0_ffn_up_W_packed"], dtype=np.int32)),
        "usc_in": np.ascontiguousarray(np.asarray(inputs["L0_ffn_up_scales"], dtype=np.float32)),
        "dpk_in": np.ascontiguousarray(np.asarray(inputs["L0_ffn_down_W_packed"], dtype=np.int32)),
        "dsc_in": np.ascontiguousarray(np.asarray(inputs["L0_ffn_down_scales"], dtype=np.float32)),
    }
    in_maps = []
    for i in range(NCORES):
        m = {"x_in": np.ascontiguousarray(x[i * T:(i + 1) * T])}
        m.update(shared)
        m.update(host_consts(i))
        in_maps.append(m)
    return in_maps


def kernel(L0_attn_a_x, L0_attn_a_W_qkv, L0_attn_a_W_out,
           L0_ffn_gate_W_packed, L0_ffn_gate_scales,
           L0_ffn_up_W_packed, L0_ffn_up_scales,
           L0_ffn_down_W_packed, L0_ffn_down_scales):
    in_maps = make_in_maps(dict(
        L0_attn_a_x=L0_attn_a_x, L0_attn_a_W_qkv=L0_attn_a_W_qkv,
        L0_attn_a_W_out=L0_attn_a_W_out,
        L0_ffn_gate_W_packed=L0_ffn_gate_W_packed, L0_ffn_gate_scales=L0_ffn_gate_scales,
        L0_ffn_up_W_packed=L0_ffn_up_W_packed, L0_ffn_up_scales=L0_ffn_up_scales,
        L0_ffn_down_W_packed=L0_ffn_down_W_packed, L0_ffn_down_scales=L0_ffn_down_scales))
    nc = build()
    res = run_bass_kernel_spmd(nc, in_maps, core_ids=list(range(NCORES)))
    out = np.concatenate([res.results[i]["y_out"] for i in range(NCORES)], axis=0)
    return out[None, :, :]


# version 05
# speedup vs baseline: 1.0383x; 1.0383x over previous
import numpy as np
import ml_dtypes
from contextlib import ExitStack

import concourse.bass as bass
import concourse.mybir as mybir
from concourse.bass_utils import run_bass_kernel_spmd

F32 = mybir.dt.float32
BF16 = mybir.dt.bfloat16
I32 = mybir.dt.int32
I16 = mybir.dt.int16
AF = mybir.ActivationFunctionType
OP = mybir.AluOpType
AX = mybir.AxisListType
BF = ml_dtypes.bfloat16

NCORES = 8
T = 1024
TC = 8
D = 2048
FC = 16
H = 16
HD = 128
CH = 256
NCH = 4
SEG = 512
NSEG = 2
NMLP = 16384
EPS = 1e-5
GELU_A = 0.044715
GELU_S = 1.5957691216057308
BSCALE = 0.08838834764831845

ENGINES = ("sync", "tensor", "vector", "scalar", "gpsimd")

STOP_AFTER = None
DEBUG_SCRATCH = ()
DEBUG_SKIP = ()
RACE_DETECT = True
LN_PASS_A_ONLY = False


class Prog:

    def __init__(self, nc, es):
        self.nc = nc
        self.es = es
        self.ops = {e: [] for e in ENGINES}
        self.sem_h = {}
        self.sem_n = {}

    def sem(self, name):
        if name not in self.sem_h:
            self.sem_h[name] = self.es.enter_context(self.nc.semaphore(name))
            self.sem_n[name] = 0
        return name

    def n(self, name):
        return self.sem_n[name]

    def dma(self, eng, stream, fn, waits=(), slot=0):
        s = self.sem(f"{stream}.{slot}")
        self.add(eng, fn, waits=waits, incs=[(s, 16)])
        return (s, self.n(s))

    def all_sems(self):
        return [(s, self.sem_n[s]) for s in self.sem_n]

    def add(self, eng, fn=None, waits=(), incs=()):
        waits = [(self.sem_h[s], v) for s, v in waits if v > 0]
        incs_h = [(self.sem_h[s], k) for s, k in incs]
        for s, k in incs:
            self.sem_n[s] += k

        def emit(e):
            for h, v in waits:
                e.wait_ge(h, v)
            if fn is not None:
                ins = fn(e)
                for h, k in incs_h:
                    ins.then_inc(h, k, skip_validation=(k == 16))
            else:
                for h, k in incs_h:
                    e.sem_inc(h, k)
        self.ops[eng].append(emit)

    def emit(self, block):
        for ename in ENGINES:
            lst = self.ops[ename]
            if not lst:
                continue
            sec = getattr(block, ename)

            def make(lst):
                def _(e):
                    for op in lst:
                        op(e)
                return _
            sec(make(lst))


def r3(ap, b=4):
    return ap.rearrange("p (b d) -> p b d", b=b)


def build():
    nc = bass.Bass("TRN2", target_bir_lowering=False, detect_race_conditions=RACE_DETECT)

    def din(name, shape, dt):
        return nc.dram_tensor(name, shape, dt, kind="ExternalInput").ap()

    x_in = din("x_in", [T, D], F32)
    wqkv_in = din("wqkv_in", [3 * D, D], F32)
    wout_in = din("wout_in", [D, D], F32)
    gpk_in = din("gpk_in", [NMLP, D // 2], I32)
    gsc_in = din("gsc_in", [NMLP, 64], F32)
    upk_in = din("upk_in", [NMLP, D // 2], I32)
    usc_in = din("usc_in", [NMLP, 64], F32)
    dpk_in = din("dpk_in", [D, NMLP // 2], I32)
    dsc_in = din("dsc_in", [D, 512], F32)
    idb_in = din("idb_in", [128, 128], BF16)
    idf_in = din("idf_in", [128, 128], F32)
    cose_in = din("cose_in", [T, 64], F32)
    coso_in = din("coso_in", [T, 64], F32)
    sine_in = din("sine_in", [T, 64], F32)
    sino_in = din("sino_in", [T, 64], F32)
    maska_in = din("maska_in", [256, 256], F32)
    ones_in = din("ones_in", [128, 2], BF16)
    onesr_in = din("onesr_in", [1, 128], BF16)
    onesrf_in = din("onesrf_in", [1, 128], F32)
    premask_in = din("premask_in", [128, 8], F32)
    lnc_in = din("lnc_in", [128, 2], F32)

    y_out = nc.dram_tensor("y_out", [T, D], F32, kind="ExternalOutput").ap()

    def dscr(name, shape, dt):
        kind = "ExternalOutput" if name in DEBUG_SCRATCH else "Internal"
        return nc.dram_tensor(name, shape, dt, kind=kind).ap()

    wqkvT_sc = dscr("wqkvT_sc", [D, 3 * D], BF16)
    woutT_sc = dscr("woutT_sc", [D, D], BF16)
    q_sc = dscr("q_sc", [T, D], BF16)
    k_sc = dscr("k_sc", [T, D], BF16)
    v_sc = dscr("v_sc", [T, D], BF16)
    wg_sc = dscr("wg_sc", [128 * 128, 16 * 128], BF16)
    wu_sc = dscr("wu_sc", [128 * 128, 16 * 128], BF16)
    wd_sc = dscr("wd_sc", [128 * 128, 16 * 128], BF16)
    cc_in = dscr("cc_in", [1, 128 * H * 132], F32)
    cc_out = dscr("cc_out", [NCORES, 128 * H * 132], F32)
    xdbg_sc = dscr("xdbg_sc", [128, FC * T], F32)
    sdbg_sc = dscr("sdbg_sc", [128, H * 132], F32)
    abdbg_sc = dscr("abdbg_sc", [128, 2048], F32)
    odbg_sc = dscr("odbg_sc", [128, H * 256], BF16)
    xlndbg_sc = dscr("xlndbg_sc", [128, FC * T], BF16)

    with ExitStack() as es:
        p = Prog(nc, es)
        for s in ("pe", "dve", "act", "pool", "cc", "fin",
                  "b_sync", "b_tensor", "b_vector", "b_scalar", "b_gpsimd"):
            p.sem(s)
        EB = es.enter_context

        def barrier():
            for e in ENGINES:
                p.add(e, None, incs=[(f"b_{e}", 1)])
            w = p.all_sems()
            for e in ENGINES:
                p.add(e, None, waits=w)

        X = EB(nc.sbuf_tensor("X", [128, FC * T], F32))
        idb = EB(nc.sbuf_tensor("idb", [128, 128], BF16))
        idf = EB(nc.sbuf_tensor("idf", [128, 128], F32))
        maska = EB(nc.sbuf_tensor("maska", [128, 2 * 256], F32))
        onesb = EB(nc.sbuf_tensor("onesb", [128, 2], BF16))
        ones_row = EB(nc.sbuf_tensor("ones_row", [1, 128], BF16))
        onesrf = EB(nc.sbuf_tensor("onesrf", [1, 128], F32))
        premask = EB(nc.sbuf_tensor("premask", [128, 8], F32))
        lnc = EB(nc.sbuf_tensor("lnc", [128, 2], F32))


        def ph_ln(tag, Xln, first, drain=None):
            sc = ExitStack()
            E = sc.enter_context
            xtm = [E(nc.sbuf_tensor(f"xtm{tag}{i}", [128, D], F32)) for i in range(2)]
            xlt = [E(nc.sbuf_tensor(f"xlt{tag}{i}", [128, D], BF16)) for i in range(2)]
            junk = E(nc.sbuf_tensor(f"junk{tag}", [128, D], BF16))
            st = {nm: E(nc.sbuf_tensor(f"{nm}{tag}", [128, TC], F32))
                  for nm in ("sx", "sxx", "mu", "musq", "var", "srt", "rstd", "nmr")}
            psX = [E(nc.psum_tensor(f"psX{tag}{i}", [128, 512], F32)) for i in range(2)]
            psL = [E(nc.psum_tensor(f"psL{tag}{i}", [128, 1024], BF16)) for i in range(2)]

            def make_xtm(tc, evX, post):
                if first:
                    w = [x for x in post.get(("xtmfree", tc - 2), []) if x[1] > 0]
                    tok = p.dma("sync", "xld", lambda e, tc=tc: e.dma_start(
                        out=xtm[tc % 2][:, :], in_=x_in[tc * 128:(tc + 1) * 128, :]),
                        waits=w, slot=tc % 2)
                    return [tok, CONSTS_TOK]
                else:
                    last = []
                    for g in range(4):
                        lin = tc * 4 + g
                        w = []
                        if ("psX", lin - 2) in post:
                            w.append(post[("psX", lin - 2)])
                        if g == 0:
                            w += post.get(("xtmfree", tc - 2), [])
                        w = [x for x in w if x[1] > 0]
                        for j in range(4):
                            fc = g * 4 + j
                            p.add("tensor", lambda e, tc=tc, g=g, j=j, fc=fc: e.transpose(
                                psX[g % 2][:, j * 128:(j + 1) * 128],
                                X[:, fc * T + tc * 128: fc * T + tc * 128 + 128], idf[:, :]),
                                waits=(w if j == 0 else []),
                                incs=([("pe", 1)] if j == 3 else []))
                        pe_ok = p.n("pe")
                        p.add("scalar", lambda e, tc=tc, g=g: e.copy(
                            out=xtm[tc % 2][:, :].rearrange("p (g j f) -> p g j f", g=4, j=4)[:, g, :, :],
                            in_=psX[g % 2][:, :].rearrange("p (j f) -> p j f", j=4)),
                            waits=[("pe", pe_ok)], incs=[("act", 1)])
                        post[("psX", lin)] = ("act", p.n("act"))
                    return [("act", p.n("act"))]

            post = {}
            for tc in range(TC):
                if drain:
                    drain()
                rdy = make_xtm(tc, None, post)
                xs = xtm[tc % 2][:, :]
                p.add("vector", lambda e, xs=xs, tc=tc: e.reduce_sum(
                    out=st["sx"][:, tc:tc + 1], in_=xs, axis=AX.X),
                    waits=rdy, incs=[("dve", 1)])
                rsum_ok = p.n("dve")
                p.add("scalar", lambda e, xs=xs, tc=tc: e.activation(
                    out=junk[:, :], in_=xs, func=AF.Square, accum_out=st["sxx"][:, tc:tc + 1]),
                    waits=rdy, incs=[("act", 1)])
                post[("xtmfree", tc)] = [("act", p.n("act")), ("dve", rsum_ok)]
            sxA, sxxA = p.n("dve"), p.n("act")

            lnc0b = lnc[:, 0:1].broadcast_to([128, TC])
            lnc1b = lnc[:, 1:2].broadcast_to([128, TC])
            p.add("gpsimd", lambda e: e.tensor_tensor(
                out=st["mu"][:, :], in0=lnc0b, in1=st["sx"][:, :], op=OP.mult),
                waits=[("dve", sxA)])
            p.add("gpsimd", lambda e: e.tensor_tensor(
                out=st["musq"][:, :], in0=st["mu"][:, :], in1=st["mu"][:, :], op=OP.mult))
            p.add("gpsimd", lambda e: e.tensor_tensor(
                out=st["var"][:, :], in0=lnc0b, in1=st["sxx"][:, :], op=OP.mult),
                waits=[("act", sxxA)])
            p.add("gpsimd", lambda e: e.tensor_tensor(
                out=st["var"][:, :], in0=st["var"][:, :], in1=st["musq"][:, :], op=OP.subtract),
                incs=[("pool", 1)])
            var_ok = p.n("pool")
            p.add("scalar", lambda e: e.activation(
                out=st["srt"][:, :], in_=st["var"][:, :], func=AF.Sqrt,
                bias=np.float32(EPS), scale=np.float32(1.0)),
                waits=[("pool", var_ok)], incs=[("act", 1)])
            srt_ok = p.n("act")
            p.add("vector", lambda e: e.reciprocal(out=st["rstd"][:, :], in_=st["srt"][:, :]),
                  waits=[("act", srt_ok)], incs=[("dve", 1)])
            rstd_ok = p.n("dve")
            p.add("gpsimd", lambda e: e.tensor_tensor(
                out=st["musq"][:, :], in0=lnc1b, in1=st["mu"][:, :], op=OP.mult))
            p.add("gpsimd", lambda e: e.tensor_tensor(
                out=st["nmr"][:, :], in0=st["musq"][:, :], in1=st["rstd"][:, :], op=OP.mult),
                waits=[("dve", rstd_ok)], incs=[("pool", 1)])
            nmr_ok = p.n("pool")

            if LN_PASS_A_ONLY and not first:
                sc.close()
                return
            passA_tail = [("act", p.n("act")), ("dve", p.n("dve"))]
            post = {("psX", -2): passA_tail[0], ("psX", -1): passA_tail[1],
                    ("xtmfree", -2): passA_tail, ("xtmfree", -1): passA_tail}
            aff = {}
            evL = {}
            for tc in range(TC):
                if drain:
                    drain()
                rdy = make_xtm(tc, None, post)
                if first:
                    for g in range(4):
                        lin = tc * 4 + g
                        w = list(rdy) if g == 0 else []
                        if ("psX", lin - 2) in post:
                            w.append(post[("psX", lin - 2)])
                        for j in range(4):
                            fc = g * 4 + j
                            p.add("tensor", lambda e, tc=tc, g=g, j=j, fc=fc: e.transpose(
                                psX[g % 2][:, j * 128:(j + 1) * 128],
                                xtm[tc % 2][:, fc * 128:(fc + 1) * 128], idf[:, :]),
                                waits=(w if j == 0 else []),
                                incs=([("pe", 1)] if j == 3 else []))
                        pe_ok = p.n("pe")
                        p.add("vector", lambda e, tc=tc, g=g: e.tensor_copy(
                            out=X[:, :].rearrange("p (f t) -> p f t", f=FC)[:, g * 4:(g + 1) * 4, tc * 128:(tc + 1) * 128],
                            in_=psX[g % 2][:, :].rearrange("p (j f) -> p j f", j=4)),
                            waits=[("pe", pe_ok)], incs=[("dve", 1)])
                        post[("psX", lin)] = ("dve", p.n("dve"))

                waff = list(rdy) + [("pool", nmr_ok)]
                if tc >= 2:
                    waff.append(("act", evL[(tc - 2) * 4 + 3]))
                p.add("vector", lambda e, tc=tc: e.tensor_scalar(
                    out=xlt[tc % 2][:, :], in0=xtm[tc % 2][:, :],
                    scalar1=st["rstd"][:, tc:tc + 1], scalar2=st["nmr"][:, tc:tc + 1],
                    op0=OP.mult, op1=OP.add),
                    waits=waff, incs=[("dve", 1)])
                aff[tc] = p.n("dve")
                post[("xtmfree", tc)] = [("dve", aff[tc])]

                for g in range(4):
                    lin = tc * 4 + g
                    w = [("dve", aff[tc])] if g == 0 else []
                    if lin >= 2:
                        w.append(("act", evL[lin - 2]))
                    for j in range(4):
                        fc = g * 4 + j
                        p.add("tensor", lambda e, tc=tc, g=g, j=j, fc=fc: e.transpose(
                            psL[g % 2][:, j * 128:(j + 1) * 128],
                            xlt[tc % 2][:, fc * 128:(fc + 1) * 128], idb[:, :]),
                            waits=(w if j == 0 else []),
                            incs=([("pe", 1)] if j == 3 else []))
                    pe_ok = p.n("pe")
                    p.add("scalar", lambda e, Xln=Xln, tc=tc, g=g: e.copy(
                        out=Xln[:, :].rearrange("p (f t) -> p f t", f=FC)[:, g * 4:(g + 1) * 4, tc * 128:(tc + 1) * 128],
                        in_=psL[g % 2][:, 0:512].rearrange("p (j f) -> p j f", j=4)),
                        waits=[("pe", pe_ok)], incs=[("act", 1)])
                    evL[lin] = p.n("act")
            sc.close()

        def ph_wt(drain=None):
            sc = ExitStack()
            E = sc.enter_context
            wraw = [E(nc.sbuf_tensor(f"wraw{i}", [128, D], F32)) for i in range(2)]
            wbf = [E(nc.sbuf_tensor(f"wbf{i}", [128, D], BF16)) for i in range(4)]
            wst = [E(nc.sbuf_tensor(f"wstg{i}", [128, 512], BF16)) for i in range(2)]
            psT = [E(nc.psum_tensor(f"psWT{i}", [128, 1024], BF16)) for i in range(2)]

            cvt = {}
            evc = {}
            dmo = {}
            units = [("qkv", og) for og in range(12)] + [("out", og) for og in range(4)]
            for u, (kind, og) in enumerate(units):
                if drain:
                    drain()
                src = wqkv_in if kind == "qkv" else wout_in
                dst = wqkvT_sc if kind == "qkv" else woutT_sc
                for j in range(4):
                    if j < 2:
                        w = [("act", cvt[(u - 1, j + 2)])] if u >= 1 else []
                    else:
                        w = [("act", cvt[(u, j - 2)])]
                    tok = p.dma("sync", "wraw", lambda e, src=src, og=og, j=j: e.dma_start(
                        out=wraw[j % 2][:, :], in_=src[og * 512 + j * 128: og * 512 + (j + 1) * 128, :]),
                        waits=w, slot=j % 2)
                    w2 = [tok]
                    if u >= 1:
                        w2.append(("pe", pe_units[u - 1]))
                    p.add("scalar", lambda e, j=j: e.copy(out=wbf[j][:, :], in_=wraw[j % 2][:, :]),
                          waits=w2, incs=[("act", 1)])
                    cvt[(u, j)] = p.n("act")
                for kc in range(16):
                    lin = u * 16 + kc
                    w = [("act", cvt[(u, 3)])] if kc == 0 else []
                    if lin >= 2:
                        w.append(("act", evc[lin - 2]))
                    for j in range(4):
                        p.add("tensor", lambda e, kc=kc, j=j: e.transpose(
                            psT[kc % 2][:, j * 128:(j + 1) * 128],
                            wbf[j][:, kc * 128:(kc + 1) * 128], idb[:, :]),
                            waits=(w if j == 0 else []),
                            incs=([("pe", 1)] if j == 3 else []))
                    pe_ok = p.n("pe")
                    w2 = [("pe", pe_ok)]
                    if lin >= 2:
                        w2.append(dmo[lin - 2])
                    p.add("scalar", lambda e, kc=kc: e.copy(out=wst[kc % 2][:, :], in_=psT[kc % 2][:, 0:512]),
                          waits=w2, incs=[("act", 1)])
                    evc[lin] = p.n("act")
                    dmo[lin] = p.dma("gpsimd", "wtw", lambda e, dst=dst, kc=kc, og=og: e.dma_start(
                        out=dst[kc * 128:(kc + 1) * 128, og * 512:(og + 1) * 512], in_=wst[kc % 2][:, :]),
                        waits=[("act", evc[lin])], slot=kc % 2)
                pe_units[u] = p.n("pe")
            sc.close()
        pe_units = {}

        def ph_qkv(Xln, drain=None):
            sc = ExitStack()
            E = sc.enter_context
            wqT = [E(nc.sbuf_tensor(f"wqT{i}", [128, 16 * 512], BF16)) for i in range(2)]
            stg = [E(nc.sbuf_tensor(f"qstg{i}", [128, 512], BF16)) for i in range(3)]
            ra = E(nc.sbuf_tensor("ra", [128, 256], F32))
            rb = E(nc.sbuf_tensor("rb", [128, 256], F32))
            rc = E(nc.sbuf_tensor("rcc", [128, 256], F32))
            rd = E(nc.sbuf_tensor("rdd", [128, 256], F32))
            qp = E(nc.sbuf_tensor("qp", [128, 512], F32))
            me = E(nc.sbuf_tensor("me", [128, 512], F32))
            ex = E(nc.sbuf_tensor("ex", [128, 512], F32))
            rl = E(nc.sbuf_tensor("rl", [128, 512], F32))
            psQ = [E(nc.psum_tensor(f"psQ{i}", [128, 512], F32)) for i in range(2)]

            psum_free = {}
            stage_ok = {}
            spill_ok = {}
            wq_ok = {}
            mm_done = {}
            for og in range(12):
                w = [("pe", mm_done[(og - 2) * 8 + 7])] if og >= 2 else []
                wq_ok[og] = p.dma("sync", "wqT", lambda e, og=og: e.dma_start(
                    out=wqT[og % 2][:, :].rearrange("p (kc f) -> p kc f", kc=16),
                    in_=wqkvT_sc[:, og * 512:(og + 1) * 512].rearrange("(kc p) f -> p kc f", p=128)),
                    waits=w, slot=og % 2)
                kind = "q" if og < 4 else ("k" if og < 8 else "v")
                for tc in range(TC):
                    if drain:
                        drain()
                    i = og * 8 + tc
                    w = [wq_ok[og]] if tc == 0 else []
                    if i >= 2:
                        w.append(psum_free[i - 2])
                    for kc in range(16):
                        p.add("tensor", lambda e, og=og, tc=tc, kc=kc, i=i: e.matmul(
                            psQ[i % 2][:, :],
                            Xln[:, kc * T + tc * 128: kc * T + tc * 128 + 128],
                            wqT[og % 2][:, kc * 512:(kc + 1) * 512],
                            start=(kc == 0), stop=(kc == 15)),
                            waits=(w if kc == 0 else []),
                            incs=([("pe", 1)] if kc == 15 else []))
                    mm_done[i] = p.n("pe")
                    stb = stg[i % 3]
                    wstg = [spill_ok[i - 3]] if i >= 3 else []
                    if kind == "v":
                        p.add("scalar", lambda e, stb=stb, i=i: e.copy(out=stb[:, :], in_=psQ[i % 2][:, :]),
                              waits=[("pe", mm_done[i])] + wstg, incs=[("act", 1)])
                        psum_free[i] = ("act", p.n("act"))
                        stage_ok[i] = ("act", p.n("act"))
                    else:
                        ps = psQ[i % 2][:, :]
                        pse = r3(ps)[:, :, 0::2]
                        pso = r3(ps)[:, :, 1::2]
                        ce = cose[:, tc * 64:(tc + 1) * 64].unsqueeze(1).broadcast_to([128, 4, 64])
                        co = coso[:, tc * 64:(tc + 1) * 64].unsqueeze(1).broadcast_to([128, 4, 64])
                        se = sine[:, tc * 64:(tc + 1) * 64].unsqueeze(1).broadcast_to([128, 4, 64])
                        so = sino[:, tc * 64:(tc + 1) * 64].unsqueeze(1).broadcast_to([128, 4, 64])
                        p.add("vector", lambda e, ce=ce, pse=pse: e.tensor_tensor(
                            out=r3(ra[:, :], 4), in0=ce, in1=pse, op=OP.mult),
                            waits=[("pe", mm_done[i])])
                        p.add("vector", lambda e, se=se, pso=pso: e.tensor_tensor(
                            out=r3(rb[:, :], 4), in0=se, in1=pso, op=OP.mult))
                        p.add("vector", lambda e, co=co, pso=pso: e.tensor_tensor(
                            out=r3(rc[:, :], 4), in0=co, in1=pso, op=OP.mult))
                        p.add("vector", lambda e, so=so, pse=pse: e.tensor_tensor(
                            out=r3(rd[:, :], 4), in0=so, in1=pse, op=OP.mult),
                            incs=[("dve", 1)])
                        psum_free[i] = ("dve", p.n("dve"))
                        p.add("vector", lambda e: e.tensor_tensor(
                            out=r3(qp[:, :])[:, :, 0::2], in0=r3(ra[:, :], 4), in1=r3(rb[:, :], 4),
                            op=OP.subtract))
                        p.add("vector", lambda e: e.tensor_tensor(
                            out=r3(qp[:, :])[:, :, 1::2], in0=r3(rc[:, :], 4), in1=r3(rd[:, :], 4),
                            op=OP.add))
                        p.add("vector", lambda e: e.tensor_scalar(
                            out=me[:, :], in0=qp[:, :], scalar1=0.0, scalar2=None, op0=OP.min),
                            incs=[("dve", 1)])
                        me_ok = p.n("dve")
                        p.add("scalar", lambda e: e.activation(out=ex[:, :], in_=me[:, :], func=AF.Exp),
                              waits=[("dve", me_ok)], incs=[("act", 1)])
                        ex_ok = p.n("act")
                        p.add("vector", lambda e: e.tensor_relu(out=rl[:, :], in_=qp[:, :]))
                        p.add("vector", lambda e, stb=stb: e.tensor_tensor(
                            out=stb[:, :], in0=ex[:, :], in1=rl[:, :], op=OP.add),
                            waits=[("act", ex_ok)] + wstg, incs=[("dve", 1)])
                        stage_ok[i] = ("dve", p.n("dve"))
                    dstm = {"q": q_sc, "k": k_sc, "v": v_sc}[kind]
                    col = (og % 4) * 512
                    spill_ok[i] = p.dma("sync", "spill", lambda e, dstm=dstm, tc=tc, col=col, stb=stb: e.dma_start(
                        out=dstm[tc * 128:(tc + 1) * 128, col:col + 512], in_=stb[:, :]),
                        waits=[stage_ok[i]], slot=i % 3)
            sc.close()

        def ph_attn_a(drain=None):
            sc = ExitStack()
            E = sc.enter_context
            S_run = E(nc.sbuf_tensor("S_run", [128, H * 132], F32))
            S_bf = E(nc.sbuf_tensor("S_bf", [128, H * 132], BF16))
            sc23 = ExitStack()
            E2 = sc23.enter_context
            Sloc = E2(nc.sbuf_tensor("Sloc", [128, H * 132], F32))
            kh = [E2(nc.sbuf_tensor(f"khA{i}", [128, TC * 128], BF16)) for i in range(2)]
            vh = [E2(nc.sbuf_tensor(f"vhA{i}", [128, TC * 128], BF16)) for i in range(2)]
            rkb = [E2(nc.sbuf_tensor(f"rkb{i}", [128, H * 132], F32)) for i in range(2)]
            tmpS = E2(nc.sbuf_tensor("tmpS", [128, H * 132], F32))
            psS = E2(nc.psum_tensor("psS", [128, 512], F32))
            psSz = E2(nc.psum_tensor("psSz", [128, 512], F32))

            ev_ok = 0
            for h in range(H):
                if drain:
                    drain()
                w = []
                if h >= 2:
                    w.append(("pe", st_mm[h - 2]))
                t1 = p.dma("sync", "kh", lambda e, h=h: e.dma_start(
                    out=kh[h % 2][:, :].rearrange("p (tc d) -> p tc d", tc=TC),
                    in_=k_sc[:, h * 128:(h + 1) * 128].rearrange("(tc p) d -> p tc d", p=128)),
                    waits=w, slot=h % 2)
                t2 = p.dma("sync", "vh", lambda e, h=h: e.dma_start(
                    out=vh[h % 2][:, :].rearrange("p (tc d) -> p tc d", tc=TC),
                    in_=v_sc[:, h * 128:(h + 1) * 128].rearrange("(tc p) d -> p tc d", p=128)),
                    slot=h % 2)
                w = [t1, t2]
                if h >= 1:
                    w.append(("dve", ev_ok))
                for mc in range(TC):
                    p.add("tensor", lambda e, h=h, mc=mc: e.matmul(
                        psS[:, 0:128], kh[h % 2][:, mc * 128:(mc + 1) * 128],
                        vh[h % 2][:, mc * 128:(mc + 1) * 128],
                        start=(mc == 0), stop=(mc == TC - 1)),
                        waits=(w if mc == 0 else []))
                    p.add("tensor", lambda e, h=h, mc=mc: e.matmul(
                        psSz[:, 0:1], kh[h % 2][:, mc * 128:(mc + 1) * 128],
                        onesb[:, 0:1],
                        start=(mc == 0), stop=(mc == TC - 1)),
                        incs=([("pe", 1)] if mc == TC - 1 else []))
                st_mm[h] = p.n("pe")
                p.add("vector", lambda e, h=h: e.tensor_copy(
                    out=Sloc[:, h * 132:h * 132 + 128], in_=psS[:, 0:128]),
                    waits=[("pe", st_mm[h])])
                p.add("vector", lambda e, h=h: e.tensor_copy(
                    out=Sloc[:, h * 132 + 128:h * 132 + 129], in_=psSz[:, 0:1]),
                    incs=[("dve", 1)])
                ev_ok = p.n("dve")

            cctok = p.dma("gpsimd", "ccw", lambda e: e.dma_start(
                out=cc_in[0, :].rearrange("(pp c) -> pp c", pp=128), in_=Sloc[:, :]),
                waits=[("dve", ev_ok)])
            p.add("gpsimd", lambda e: e.collective_compute(
                "AllGather", OP.bypass,
                replica_groups=[list(range(NCORES))],
                ins=[cc_in[:, :]], outs=[cc_out[:, :]]),
                waits=[cctok], incs=[("cc", 1)])
            cc_ok = p.n("cc")
            sin_ok = None
            for j in range(NCORES):
                w = [("cc", cc_ok)]
                if j >= 2:
                    w.append(("dve", p.n("dve")))
                ldtok = p.dma("sync", "rk", lambda e, j=j: e.dma_start(
                    out=rkb[j % 2][:, :], in_=cc_out[j, :].rearrange("(pp c) -> pp c", pp=128)),
                    waits=w, slot=j % 2)
                if j == 0:
                    p.add("vector", lambda e, j=j: e.tensor_scalar(
                        out=S_run[:, :], in0=rkb[j % 2][:, :], scalar1=premask[:, j:j + 1],
                        scalar2=None, op0=OP.mult),
                        waits=[ldtok], incs=[("dve", 1)])
                else:
                    p.add("vector", lambda e, j=j: e.tensor_scalar(
                        out=tmpS[:, :], in0=rkb[j % 2][:, :], scalar1=premask[:, j:j + 1],
                        scalar2=None, op0=OP.mult),
                        waits=[ldtok])
                    p.add("vector", lambda e: e.tensor_tensor(
                        out=S_run[:, :], in0=tmpS[:, :], in1=S_run[:, :], op=OP.add),
                        incs=[("dve", 1)])
            sin_ok = p.n("dve")
            p.add("scalar", lambda e: e.copy(out=S_bf[:, :], in_=S_run[:, :]),
                  waits=[("dve", sin_ok)], incs=[("act", 1)])
            sbf_ok = {(-1, h): p.n("act") for h in range(H)}
            sc23.close()
            wout_sb = E(nc.sbuf_tensor("wout_sb", [128, FC * D], BF16))
            o_buf = E(nc.sbuf_tensor("o_buf", [128, H * 256], BF16))
            qch = [E(nc.sbuf_tensor(f"qch{i}", [128, 256], BF16)) for i in range(2)]
            kch = [E(nc.sbuf_tensor(f"kch{i}", [128, 256], BF16)) for i in range(2)]
            vch = [E(nc.sbuf_tensor(f"vch{i}", [128, 256], BF16)) for i in range(2)]
            qfm = [E(nc.sbuf_tensor(f"qfm{i}", [128, 256], BF16)) for i in range(2)]
            kfm = [E(nc.sbuf_tensor(f"kfm{i}", [128, 256], BF16)) for i in range(2)]
            p2 = E(nc.sbuf_tensor("p2t", [128, 2 * 256], BF16))
            dr_t = E(nc.sbuf_tensor("dr_t", [1, 512], F32))
            dsb = dr_t[:, 0:256]
            rden = dr_t[:, 256:512]
            bsb = E(nc.sbuf_tensor("bsb", [128, 256], F32))
            spare = E(nc.sbuf_tensor("spareA", [128, 2], F32))
            psT2 = E(nc.psum_tensor("psT2", [128, 1024], BF16))
            psA = [E(nc.psum_tensor(f"psA{i}", [128, 512], F32)) for i in range(2)]
            psUt = E(nc.psum_tensor("psUt", [128, 512], F32))
            psUs = psUt[:, 0:128]
            psDZ = E(nc.psum_tensor("psDZ", [128, 512], F32))
            psD = psDZ[0:1, 0:256]
            psUz = psDZ[:, 256:257]
            psPt = E(nc.psum_tensor("psPt", [128, 512], F32))
            psP0 = psPt[:, 0:256]
            psN = psPt[:, 256:512]

            for ec in range(FC):
                wout_ok = p.dma("sync", "wo", lambda e, ec=ec: e.dma_start(
                    out=wout_sb[:, ec * D:(ec + 1) * D],
                    in_=woutT_sc[ec * 128:(ec + 1) * 128, :]))

            o_ok = {}
            xadd_ok = 0
            for c in range(NCH):
                for h in range(H):
                    if drain:
                        drain()
                    i = c * H + h
                    w = []
                    if i >= 2:
                        w.append(("pe", ld_reuse[i - 2]))
                    chtoks = []
                    for nmx, (dst, srcm) in (("chq", (qch, q_sc)), ("chk", (kch, k_sc)), ("chv", (vch, v_sc))):
                        chtoks.append(p.dma("sync", nmx, lambda e, dst=dst, srcm=srcm, c=c, h=h, i=i: e.dma_start(
                            out=dst[i % 2][:, :].rearrange("p (i d) -> p i d", i=2),
                            in_=srcm[c * 256:(c + 1) * 256, h * 128:(h + 1) * 128]
                                .rearrange("(i p) d -> p i d", p=128)),
                            waits=(w if dst is qch else []), slot=i % 2))
                    w = list(chtoks)
                    if i >= 2:
                        w.append(("act", fm_ok[i - 2]))
                    for ii in range(2):
                        p.add("tensor", lambda e, i=i, ii=ii: e.transpose(
                            psT2[:, ii * 128:(ii + 1) * 128],
                            qch[i % 2][:, ii * 128:(ii + 1) * 128], idb[:, :]),
                            waits=(w if ii == 0 else []))
                    for ii in range(2):
                        p.add("tensor", lambda e, i=i, ii=ii: e.transpose(
                            psT2[:, 256 + ii * 128: 256 + (ii + 1) * 128],
                            kch[i % 2][:, ii * 128:(ii + 1) * 128], idb[:, :]),
                            incs=([("pe", 1)] if ii == 1 else []))
                    tr_ok = p.n("pe")
                    p.add("scalar", lambda e, i=i: e.copy(out=qfm[i % 2][:, :], in_=psT2[:, 0:256]),
                          waits=[("pe", tr_ok)])
                    p.add("scalar", lambda e, i=i: e.copy(out=kfm[i % 2][:, :], in_=psT2[:, 256:512]),
                          incs=[("act", 1)])
                    fm_ok[i] = p.n("act")
                    w = [("act", fm_ok[i])]
                    if i >= 1:
                        w.append(("dve", p2_ok[i - 1]))
                        w.append(("dve", onum_ok[i - 1]))
                    for mc in range(2):
                        p.add("tensor", lambda e, i=i, mc=mc: e.matmul(
                            psA[mc][:, 0:256], kfm[i % 2][:, mc * 128:(mc + 1) * 128],
                            qfm[i % 2][:, :], start=True, stop=True),
                            waits=(w if mc == 0 else []),
                            incs=([("pe", 1)] if mc == 1 else []))
                    sc_ok = p.n("pe")
                    if drain:
                        drain()
                    for mc in range(2):
                        p.add("vector", lambda e, mc=mc: e.tensor_tensor(
                            out=p2[:, mc * 256:(mc + 1) * 256],
                            in0=maska[:, mc * 256:(mc + 1) * 256], in1=psA[mc][:, 0:256], op=OP.mult),
                            waits=([("pe", sc_ok)] if mc == 0 else []),
                            incs=([("dve", 1)] if mc == 1 else []))
                    p2_ok[i] = p.n("dve")
                    w = [("dve", p2_ok[i]), ("act", sbf_ok[(c - 1, h)])]
                    if i >= 1:
                        w.append(("dve", den_ok[i - 1]))
                        w.append(("dve", supd_ok[i - 1]))
                    p.add("tensor", lambda e, i=i: e.matmul(
                        psD, onesb[:, 0:1], p2[:, 0:256], start=True, stop=False), waits=w)
                    p.add("tensor", lambda e, i=i: e.matmul(
                        psD, onesb[:, 0:1], p2[:, 256:512], start=False, stop=False))
                    p.add("tensor", lambda e, i=i, h=h: e.matmul(
                        psD, S_bf[:, h * 132 + 128:h * 132 + 129], qfm[i % 2][:, :],
                        start=False, stop=True), incs=[("pe", 1)])
                    dmm_ok = p.n("pe")
                    w = []
                    if i >= 1:
                        w.append(("dve", onum_ok[i - 1]))
                    if h == 0 and c >= 1:
                        w.append(("dve", proj_ok[FC - 1]))
                    p.add("tensor", lambda e, i=i, h=h: e.matmul(
                        psN[:, :], S_bf[:, h * 132:h * 132 + 128], qfm[i % 2][:, :],
                        start=True, stop=False), waits=w)
                    for mc in range(2):
                        p.add("tensor", lambda e, i=i, mc=mc: e.matmul(
                            psN[:, :], vch[i % 2][:, mc * 128:(mc + 1) * 128],
                            p2[:, mc * 256:(mc + 1) * 256], start=False, stop=(mc == 1)),
                            incs=([("pe", 1)] if mc == 1 else []))
                    nmm_ok = p.n("pe")
                    if drain:
                        drain()
                    p.add("vector", lambda e: e.tensor_scalar(
                        out=dsb, in0=psD, scalar1=1e-6, scalar2=None, op0=OP.max),
                        waits=[("pe", dmm_ok)], incs=[("dve", 1)])
                    den_ok[i] = p.n("dve")
                    p.add("vector", lambda e: e.tensor_copy(out=spare[:, :], in_=onesb[:, :]))
                    p.add("vector", lambda e: e.reciprocal(out=rden, in_=dsb),
                          incs=[("dve", 1)])
                    rd_ok = p.n("dve")
                    w = [("dve", rd_ok)]
                    p.add("tensor", lambda e: e.matmul(
                        psA[0][:, 256:512], onesrf[0:1, :], rden, start=True, stop=True),
                        waits=w, incs=[("pe", 1)])
                    bmm_ok = p.n("pe")
                    wb = [("pe", bmm_ok)]
                    if i >= 1:
                        wb.append(("dve", onum_ok[i - 1]))
                    p.add("scalar", lambda e: e.copy(out=bsb[:, :], in_=psA[0][:, 256:512]),
                          waits=wb, incs=[("act", 1)])
                    bsb_ok[i] = ("act", p.n("act"))
                    if drain:
                        drain()
                    p.add("vector", lambda e, h=h: e.tensor_tensor(
                        out=o_buf[:, h * 256:(h + 1) * 256], in0=bsb[:, :], in1=psN[:, :], op=OP.mult),
                        waits=[bsb_ok[i], ("pe", nmm_ok)], incs=[("dve", 1)])
                    onum_ok[i] = p.n("dve")
                    o_ok[(c, h)] = p.n("dve")
                    w = [("dve", onum_ok[i])]
                    if i >= 1:
                        w.append(("dve", supd_ok[i - 1]))
                    for mc in range(2):
                        p.add("tensor", lambda e, i=i, mc=mc: e.matmul(
                            psUs, kch[i % 2][:, mc * 128:(mc + 1) * 128],
                            vch[i % 2][:, mc * 128:(mc + 1) * 128],
                            start=(mc == 0), stop=(mc == 1)),
                            waits=(w if mc == 0 else []))
                        p.add("tensor", lambda e, i=i, mc=mc: e.matmul(
                            psUz, kch[i % 2][:, mc * 128:(mc + 1) * 128],
                            onesb[:, 0:1], start=(mc == 0), stop=(mc == 1)),
                            incs=([("pe", 1)] if mc == 1 else []))
                    umm_ok = p.n("pe")
                    ld_reuse[i] = umm_ok
                    p.add("vector", lambda e, h=h: e.tensor_tensor(
                        out=S_run[:, h * 132:h * 132 + 128],
                        in0=S_run[:, h * 132:h * 132 + 128], in1=psUs, op=OP.add),
                        waits=[("pe", umm_ok)])
                    p.add("vector", lambda e, h=h: e.tensor_tensor(
                        out=S_run[:, h * 132 + 128:h * 132 + 129],
                        in0=S_run[:, h * 132 + 128:h * 132 + 129], in1=psUz, op=OP.add),
                        incs=[("dve", 1)])
                    supd_ok[i] = p.n("dve")
                    p.add("scalar", lambda e, h=h: e.copy(
                        out=S_bf[:, h * 132:h * 132 + 129], in_=S_run[:, h * 132:h * 132 + 129]),
                        waits=[("dve", supd_ok[i])], incs=[("act", 1)])
                    sbf_ok[(c, h)] = p.n("act")

                w_all_o = [("dve", o_ok[(c, H - 1)]), wout_ok]
                for hc in range(FC):
                    w = list(w_all_o) if hc == 0 else []
                    if hc >= 1:
                        w.append(("dve", proj_ok[hc - 1]))
                    for ec in range(FC):
                        p.add("tensor", lambda e, hc=hc, ec=ec: e.matmul(
                            psP0, wout_sb[:, ec * D + hc * 128: ec * D + (hc + 1) * 128],
                            o_buf[:, ec * 256:(ec + 1) * 256],
                            start=(ec == 0), stop=(ec == FC - 1)),
                            waits=(w if ec == 0 else []),
                            incs=([("pe", 1)] if ec == FC - 1 else []))
                    pmm_ok = p.n("pe")
                    p.add("vector", lambda e, hc=hc, c=c: e.tensor_tensor(
                        out=X[:, hc * T + c * 256: hc * T + (c + 1) * 256],
                        in0=X[:, hc * T + c * 256: hc * T + (c + 1) * 256],
                        in1=psP0, op=OP.add),
                        waits=[("pe", pmm_ok)], incs=[("dve", 1)])
                    proj_ok[hc] = p.n("dve")
                ld_reuse[c * H + H - 1] = p.n("pe")
            if "sdbg_sc" in DEBUG_SCRATCH:
                p.dma("gpsimd", "dbg", lambda e: e.dma_start(out=sdbg_sc, in_=S_run[:, :]),
                      waits=[("dve", p.n("dve")), ("act", p.n("act")), ("pe", p.n("pe"))])
                p.dma("gpsimd", "dbg", lambda e: e.dma_start(out=odbg_sc, in_=o_buf[:, :]))
            sc.close()

        st_mm = {}
        fm_ok = {}
        p2_ok = {}
        den_ok = {}
        onum_ok = {}
        bsb_ok = {}
        supd_ok = {}
        ld_reuse = {}
        proj_ok = {}

        class Decoder:
            def __init__(self):
                self.sc = ExitStack()
                E = self.sc.enter_context
                self.pk = [E(nc.sbuf_tensor(f"pk{i}", [128, 1024], I32)) for i in range(2)]
                self.m32 = [E(nc.sbuf_tensor(f"m32{i}", [128, 1024], I32)) for i in range(2)]
                self.s32 = [E(nc.sbuf_tensor(f"s32{i}", [128, 1024], I32)) for i in range(2)]
                self.c32 = [E(nc.sbuf_tensor(f"c32{i}", [128, 1024], I32)) for i in range(2)]
                self.dec = [E(nc.sbuf_tensor(f"dec{i}", [128, 2048], BF16)) for i in range(2)]
                self.scf = [E(nc.sbuf_tensor(f"scf{i}", [128, 64], F32)) for i in range(2)]
                self.scb = [E(nc.sbuf_tensor(f"scb{i}", [128, 64], BF16)) for i in range(2)]
                self.scfD = E(nc.sbuf_tensor("scfD", [128, 512], F32))
                self.scbD = [E(nc.sbuf_tensor(f"scbD{i}", [128, 512], BF16)) for i in range(2)]
                self.wstg = [E(nc.sbuf_tensor(f"dstg{i}", [128, 512], BF16)) for i in range(2)]
                self.psDC = [E(nc.psum_tensor(f"psDC{i}", [128, 1024], BF16)) for i in range(2)]
                self.units = ([("g", nb, 0) for nb in range(128)] +
                              [("u", nb, 0) for nb in range(128)] +
                              [("d", hb, sb) for hb in range(16) for sb in range(8)])
                self.n = len(self.units)
                self.e1 = 0
                self.e2 = 0
                self.e3 = 0
                self.u_ok = {}
                self.s_ok = {}
                self.c_ok = {}
                self.or_ok = {}
                self.scale_ok = {}
                self.scv = {}
                self.cvt_done = {}
                self.dcvt = {}
                self.dscale_prev = {}
                self.trall = {}
                self.ev_done = {}
                self.wr_done = {}
                self.wr_unit = {}

            def stage1(self):
                i = self.e1
                if i >= self.n:
                    return
                kind, rb, sb = self.units[i]
                pksrc = {"g": gpk_in, "u": upk_in, "d": dpk_in}[kind]
                w = [("dve", self.s_ok.get(i - 2, 0))]
                if kind == "d":
                    tok = p.dma("sync", "pk", lambda e, pksrc=pksrc, rb=rb, sb=sb, i=i: e.dma_start(
                        out=self.pk[i % 2][:, :],
                        in_=pksrc[rb * 128:(rb + 1) * 128, sb * 1024:(sb + 1) * 1024]),
                        waits=[x for x in w if x[1] > 0], slot=i % 2)
                else:
                    tok = p.dma("sync", "pk", lambda e, pksrc=pksrc, rb=rb, i=i: e.dma_start(
                        out=self.pk[i % 2][:, :], in_=pksrc[rb * 128:(rb + 1) * 128, :]),
                        waits=[x for x in w if x[1] > 0], slot=i % 2)
                if kind != "d":
                    slot = i % 2
                    w2 = [("act", self.cvt_done.get(i - 2, 0))]
                    sl = p.dma("sync", "scg", lambda e, kind=kind, rb=rb, slot=slot: e.dma_start(
                        out=self.scf[slot][:, 0:64],
                        in_=(gsc_in if kind == "g" else usc_in)[rb * 128:(rb + 1) * 128, :]),
                        waits=[x for x in w2 if x[1] > 0], slot=slot)
                    w3 = [sl, ("dve", self.scale_ok.get(i - 2, 0))]
                    p.add("scalar", lambda e, slot=slot: e.copy(
                        out=self.scb[slot][:, 0:64], in_=self.scf[slot][:, 0:64]),
                        waits=[x for x in w3 if not (isinstance(x[1], int) and x[1] <= 0)],
                        incs=[("act", 1)])
                    self.cvt_done[i] = p.n("act")
                    self.scv[i] = (self.scb[slot][:, 0:64], ("act", self.cvt_done[i]))
                else:
                    if sb == 0:
                        slot = rb % 2
                        w2 = [("act", self.dcvt.get(rb - 1, 0))]
                        sl = p.dma("sync", "scd", lambda e, rb=rb: e.dma_start(
                            out=self.scfD[:, :], in_=dsc_in[rb * 128:(rb + 1) * 128, :]),
                            waits=[x for x in w2 if x[1] > 0], slot=slot)
                        w3 = [sl, ("dve", self.dscale_prev.get(rb - 2, 0))]
                        p.add("scalar", lambda e, slot=slot: e.copy(
                            out=self.scbD[slot][:, :], in_=self.scfD[:, :]),
                            waits=[x for x in w3 if not (isinstance(x[1], int) and x[1] <= 0)],
                            incs=[("act", 1)])
                        self.dcvt[rb] = p.n("act")
                    self.scv[i] = (self.scbD[rb % 2][:, sb * 64:(sb + 1) * 64],
                                   ("act", self.dcvt[rb]))
                p.add("gpsimd", lambda e, i=i: e.tensor_scalar(
                    out=self.pk[i % 2][:, :], in0=self.pk[i % 2][:, :],
                    scalar1=4097, scalar2=None, op0=OP.mult),
                    waits=[tok], incs=[("pool", 1)])
                self.u_ok[i] = p.n("pool")
                w = [("pool", self.u_ok[i])]
                p.add("vector", lambda e, i=i: e.tensor_scalar(
                    out=self.m32[i % 2][:, :], in0=self.pk[i % 2][:, :],
                    scalar1=0x00070007, scalar2=None, op0=OP.bitwise_and),
                    waits=[x for x in w if x[1] > 0])
                p.add("vector", lambda e, i=i: e.tensor_scalar(
                    out=self.s32[i % 2][:, :], in0=self.pk[i % 2][:, :],
                    scalar1=0x00080008, scalar2=12,
                    op0=OP.bitwise_and, op1=OP.logical_shift_left),
                    incs=[("dve", 1)])
                self.s_ok[i] = p.n("dve")
                self.e1 = i + 1

            def stage2(self):
                i = self.e2
                if i >= self.n or i >= self.e1:
                    return
                p.add("vector", lambda e, i=i: e.tensor_scalar(
                    out=self.c32[i % 2][:, :].bitcast(I16), in0=self.m32[i % 2][:, :].bitcast(I16),
                    scalar1=0x40, scalar2=0x3F00, op0=OP.mult, op1=OP.add),
                    incs=[("dve", 1)])
                self.c_ok[i] = p.n("dve")
                self.e2 = i + 1

            def stage3(self):
                i = self.e3
                if i >= self.n or i >= self.e2:
                    return
                kind, rb, sb = self.units[i]
                p.add("vector", lambda e, i=i: e.tensor_scalar(
                    out=self.m32[i % 2][:, :].bitcast(BF16), in0=self.c32[i % 2][:, :].bitcast(BF16),
                    scalar1=-1.0, scalar2=0.0, op0=OP.add, op1=OP.min))
                p.add("vector", lambda e, i=i: e.tensor_tensor(
                    out=self.c32[i % 2][:, :].bitcast(BF16), in0=self.m32[i % 2][:, :].bitcast(BF16),
                    in1=self.c32[i % 2][:, :].bitcast(BF16), op=OP.add))
                p.add("vector", lambda e, i=i: e.tensor_tensor(
                    out=self.c32[i % 2][:, :], in0=self.s32[i % 2][:, :], in1=self.c32[i % 2][:, :],
                    op=OP.bitwise_or), incs=[("dve", 1)])
                self.or_ok[i] = p.n("dve")
                scsl, stok = self.scv[i]
                sc3 = scsl.unsqueeze(2).broadcast_to([128, 64, 32])
                w = [stok, ("pe", self.trall.get(i - 2, 0))]
                p.add("vector", lambda e, i=i, sc3=sc3: e.tensor_tensor(
                    out=self.dec[i % 2][:, :].rearrange("p (b k) -> p b k", b=64),
                    in0=sc3, in1=self.c32[i % 2][:, :].bitcast(BF16).rearrange("p (b k) -> p b k", b=64),
                    op=OP.mult),
                    waits=[x for x in w if not (isinstance(x[1], int) and x[1] <= 0)],
                    incs=[("dve", 1)])
                self.scale_ok[i] = p.n("dve")
                if kind == "d" and sb == 7:
                    self.dscale_prev[rb] = self.scale_ok[i]
                dstt = {"g": wg_sc, "u": wu_sc, "d": wd_sc}[kind]
                rowblk = rb if kind != "d" else sb * 16 + rb
                for g in range(4):
                    lv = i * 4 + g
                    w = [("dve", self.scale_ok[i]), CONSTS_TOK] if g == 0 else []
                    if lv >= 2:
                        w.append(("act", self.ev_done[lv - 2]))
                    for j in range(4):
                        kc = g * 4 + j
                        p.add("tensor", lambda e, i=i, g=g, j=j, kc=kc: e.transpose(
                            self.psDC[g % 2][:, j * 128:(j + 1) * 128],
                            self.dec[i % 2][:, kc * 128:(kc + 1) * 128], idb[:, :]),
                            waits=(w if j == 0 else []),
                            incs=([("pe", 1)] if j == 3 else []))
                    tr_ok = p.n("pe")
                    w2 = [("pe", tr_ok)]
                    if lv >= 2:
                        w2.append(self.wr_done[lv - 2])
                    p.add("scalar", lambda e, g=g: e.copy(
                        out=self.wstg[g % 2][:, :], in_=self.psDC[g % 2][:, 0:512]),
                        waits=w2, incs=[("act", 1)])
                    self.ev_done[lv] = p.n("act")
                    self.wr_done[lv] = p.dma("scalar", "dw",
                        lambda e, dstt=dstt, rowblk=rowblk, g=g: e.dma_start(
                            out=dstt[rowblk * 128:(rowblk + 1) * 128, g * 512:(g + 1) * 512],
                            in_=self.wstg[g % 2][:, :]),
                        waits=[("act", self.ev_done[lv])], slot=lv % 2)
                self.trall[i] = p.n("pe")
                self.wr_unit[(kind, rowblk)] = self.wr_done[i * 4 + 3]
                self.e3 = i + 1

            def drain(self, k):
                for _ in range(k):
                    self.stage3()
                    self.stage2()
                    self.stage1()

            def flush(self):
                while self.e3 < self.n:
                    self.stage1()
                    self.stage2()
                    self.stage3()
                self.sc.close()


        def ph_attn_b(Xln):
            sc = ExitStack()
            E = sc.enter_context
            gsc_sb = E(nc.sbuf_tensor("gsc_sb", [128, 256], BF16))
            p1s = E(nc.sbuf_tensor("p1s", [128, 2 * 256], BF16))
            p1n = E(nc.sbuf_tensor("p1n", [128, 2 * 256], BF16))
            p2g = E(nc.sbuf_tensor("p2g", [128, 768], BF16))
            nmx = E(nc.sbuf_tensor("nmx", [128, 2], F32))
            denb = E(nc.sbuf_tensor("denb", [128, 2], F32))
            rdb = E(nc.sbuf_tensor("rdb", [128, 2], F32))
            abdbg = E(nc.sbuf_tensor("abdbg", [128, 2048], F32))
            psS1 = [E(nc.psum_tensor(f"psS1{i}", [128, 512], F32)) for i in range(2)]
            psP2 = E(nc.psum_tensor("psP2", [128, 1024], BF16))
            psO = E(nc.psum_tensor("psO", [128, 512], F32))

            gsc_ok = 0
            exp_ok = {}
            evac_ok = 0
            xadd_ok = 0
            prev_sc = 0
            for s in range(NSEG):
                for h in range(H):
                    i = s * H + h
                    par = h % 2
                    base = h * T + s * 512 + par

                    def gv(lo, cnt, base=base):
                        st = base + 2 * lo
                        return Xln[:, st: st + 2 * cnt - 1: 2]

                    w = [("pe", prev_sc)] if i >= 1 else []
                    p.add("vector", lambda e, gv=gv: e.tensor_scalar(
                        out=gsc_sb[:, :], in0=gv(0, 256), scalar1=BSCALE, scalar2=None, op0=OP.mult),
                        waits=w, incs=[("dve", 1)])
                    gsc_ok = p.n("dve")
                    w = [("dve", gsc_ok)]
                    if i >= 1:
                        w.append(("act", exp_ok[i - 1]))
                    for lc in range(2):
                        p.add("tensor", lambda e, lc=lc, gv=gv: e.matmul(
                            psS1[lc][:, 0:256], gsc_sb[:, lc * 128:(lc + 1) * 128],
                            gv(0, 256), start=True, stop=True),
                            waits=(w if lc == 0 else []),
                            incs=([("pe", 1)] if lc == 1 else []))
                    sc_ok = p.n("pe")
                    prev_sc = sc_ok
                    for lc in range(2):
                        p.add("vector", lambda e, lc=lc: e.reduce_max(
                            out=nmx[:, lc:lc + 1], in_=psS1[lc][:, 0:256], axis=AX.X, negate=True),
                            waits=([("pe", sc_ok)] if lc == 0 else []),
                            incs=([("dve", 1)] if lc == 1 else []))
                    nmx_ok = p.n("dve")
                    for lc in range(2):
                        p.add("scalar", lambda e, lc=lc: e.activation(
                            out=p1s[:, lc * 256:(lc + 1) * 256], in_=psS1[lc][:, 0:256],
                            func=AF.Exp, bias=nmx[:, lc:lc + 1]),
                            waits=([("dve", nmx_ok)] if lc == 0 else []),
                            incs=([("act", 1)] if lc == 1 else []))
                    exp_ok[i] = p.n("act")
                    for lc in range(2):
                        p.add("vector", lambda e, lc=lc: e.reduce_sum(
                            out=denb[:, lc:lc + 1], in_=p1s[:, lc * 256:(lc + 1) * 256], axis=AX.X),
                            waits=([("act", exp_ok[i])] if lc == 0 else []))
                    p.add("vector", lambda e: e.reciprocal(out=rdb[:, :], in_=denb[:, :]))
                    p.add("vector", lambda e: e.tensor_copy(out=nmx[:, :], in_=denb[:, :]))
                    p.add("vector", lambda e: e.tensor_copy(out=nmx[:, :], in_=denb[:, :]))
                    for lc in range(2):
                        p.add("vector", lambda e, lc=lc: e.tensor_scalar(
                            out=p1n[:, lc * 256:(lc + 1) * 256], in0=p1s[:, lc * 256:(lc + 1) * 256],
                            scalar1=rdb[:, lc:lc + 1], scalar2=None, op0=OP.mult),
                            incs=([("dve", 1)] if lc == 1 else []))
                    p1n_ok = p.n("dve")
                    w = [("dve", p1n_ok)]
                    if i >= 1:
                        w.append(("act", evac_ok))
                    for mc in range(2):
                        for lc in range(2):
                            p.add("tensor", lambda e, mc=mc, lc=lc: e.transpose(
                                psP2[:, mc * 256 + lc * 128: mc * 256 + (lc + 1) * 128],
                                p1n[:, lc * 256 + mc * 128: lc * 256 + (mc + 1) * 128], idb[:, :]),
                                waits=(w if (mc == 0 and lc == 0) else []))
                    for mc in range(2):
                        p.add("tensor", lambda e, mc=mc, gv=gv: e.transpose(
                            psP2[:, 512 + mc * 128: 512 + (mc + 1) * 128],
                            gv(mc * 128, 128), idb[:, :]),
                            incs=([("pe", 1)] if mc == 1 else []))
                    tr_ok = p.n("pe")
                    w2 = [("pe", tr_ok)]
                    if i >= 1:
                        w2.append(("dve", xadd_ok))
                    p.add("scalar", lambda e: e.copy(out=p2g[:, :], in_=psP2[:, 0:768]),
                          waits=w2, incs=[("act", 1)])
                    evac_ok = p.n("act")
                    for mc in range(2):
                        wav = [("act", evac_ok)]
                        if i >= 1:
                            wav.append(("dve", xadd_ok))
                        p.add("tensor", lambda e, mc=mc: e.matmul(
                            psO[:, 0:256], p2g[:, 512 + mc * 128: 512 + (mc + 1) * 128],
                            p2g[:, mc * 256:(mc + 1) * 256], start=(mc == 0), stop=(mc == 1)),
                            waits=(wav if mc == 0 else []),
                            incs=([("pe", 1)] if mc == 1 else []))
                    av_ok = p.n("pe")
                    if i == 0 and "abdbg_sc" in DEBUG_SCRATCH:
                        p.add("vector", lambda e: e.tensor_copy(out=abdbg[:, 0:256], in_=gsc_sb[:, :]),
                              waits=[("pe", av_ok)])
                        p.add("vector", lambda e: e.tensor_copy(out=abdbg[:, 256:512], in_=p1n[:, 0:256]))
                        p.add("vector", lambda e: e.tensor_copy(out=abdbg[:, 512:768], in_=p1n[:, 256:512]))
                        p.add("vector", lambda e: e.tensor_copy(out=abdbg[:, 768:770], in_=denb[:, :]))
                        p.add("vector", lambda e: e.tensor_copy(out=abdbg[:, 770:772], in_=rdb[:, :]))
                        p.add("vector", lambda e: e.tensor_copy(out=abdbg[:, 772:1540], in_=p2g[:, :]))
                        p.add("vector", lambda e: e.tensor_copy(out=abdbg[:, 1540:1796], in_=psO[:, 0:256]),
                              incs=[("dve", 1)])
                        p.dma("gpsimd", "dbg", lambda e: e.dma_start(out=abdbg_sc[:, :], in_=abdbg[:, :]),
                              waits=[("dve", p.n("dve"))])
                    p.add("vector", lambda e, base=base: e.tensor_tensor(
                        out=Xs_strided(base), in0=Xs_strided(base), in1=psO[:, 0:256], op=OP.add),
                        waits=[("pe", av_ok)], incs=[("dve", 1)])
                    xadd_ok = p.n("dve")
            sc.close()

        def Xs_strided(base):
            return X[:, base: base + 511: 2]

        def ph_out():
            sc = ExitStack()
            E = sc.enter_context
            yst = [E(nc.sbuf_tensor(f"yst{i}", [128, D], F32)) for i in range(2)]
            psY = [E(nc.psum_tensor(f"psY{i}", [128, 512], F32)) for i in range(2)]
            evy = {}
            wry = {}
            for tc in range(TC):
                for g in range(4):
                    lin = tc * 4 + g
                    w = []
                    if lin >= 2:
                        w.append(("dve", evy[lin - 2]))
                    if g == 0 and tc >= 2:
                        w.append(wry[tc - 2])
                    for j in range(4):
                        fc = g * 4 + j
                        p.add("tensor", lambda e, tc=tc, g=g, j=j, fc=fc: e.transpose(
                            psY[g % 2][:, j * 128:(j + 1) * 128],
                            X[:, fc * T + tc * 128: fc * T + tc * 128 + 128], idf[:, :]),
                            waits=(w if j == 0 else []),
                            incs=([("pe", 1)] if j == 3 else []))
                    pe_ok = p.n("pe")
                    p.add("vector", lambda e, tc=tc, g=g: e.tensor_copy(
                        out=yst[tc % 2][:, :].rearrange("p (g j f) -> p g j f", g=4, j=4)[:, g, :, :],
                        in_=psY[g % 2][:, :].rearrange("p (j f) -> p j f", j=4)),
                        waits=[("pe", pe_ok)], incs=[("dve", 1)])
                    evy[lin] = p.n("dve")
                wry[tc] = p.dma("gpsimd", "yout", lambda e, tc=tc: e.dma_start(
                    out=y_out[tc * 128:(tc + 1) * 128, :], in_=yst[tc % 2][:, :]),
                    waits=[("dve", evy[tc * 4 + 3])], slot=tc % 2)
            sc.close()

        def ph_ffn(Xln, tagn):
            sc = ExitStack()
            E = sc.enter_context
            wgb = [E(nc.sbuf_tensor(f"wgb{tagn}{i}", [128, 2048], BF16)) for i in range(2)]
            wub = [E(nc.sbuf_tensor(f"wub{tagn}{i}", [128, 2048], BF16)) for i in range(2)]
            wdb = [E(nc.sbuf_tensor(f"wdb{tagn}{i}", [128, 2048], BF16)) for i in range(2)]
            inter = E(nc.sbuf_tensor(f"inter{tagn}", [128, 16 * 1024], BF16))
            sqb = E(nc.sbuf_tensor(f"sqb{tagn}", [128, 1024], F32))
            cgb = E(nc.sbuf_tensor(f"cgb{tagn}", [128, 1024], F32))
            sgb = E(nc.sbuf_tensor(f"sgb{tagn}", [128, 1024], F32))
            glb = E(nc.sbuf_tensor(f"glb{tagn}", [128, 1024], BF16))
            usb = E(nc.sbuf_tensor(f"usb{tagn}", [128, 1024], BF16))
            psG = E(nc.psum_tensor(f"psG{tagn}", [128, 1024], F32))
            psUp = E(nc.psum_tensor(f"psU{tagn}", [128, 1024], F32))
            psDn = [E(nc.psum_tensor(f"psDn{tagn}{i}", [128, 512], F32)) for i in range(2)]

            gld = {}
            uld = {}
            dld = {}
            gmm = {}
            umm = {}
            inter_ok = {}
            gl_ok = {}
            ue_ok = {}
            dadd = {}
            for ng in range(8):
                for ncx in range(16):
                    nb = ng * 16 + ncx
                    w = [("pe", gmm.get(nb - 2, 0))]
                    gld[nb] = p.dma("sync", "wgb", lambda e, nb=nb: e.dma_start(
                        out=wgb[nb % 2][:, :], in_=wg_sc[nb * 128:(nb + 1) * 128, :]),
                        waits=[x for x in w if x[1] > 0], slot=nb % 2)
                    w = [("pe", umm.get(nb - 2, 0))]
                    uld[nb] = p.dma("sync", "wub", lambda e, nb=nb: e.dma_start(
                        out=wub[nb % 2][:, :], in_=wu_sc[nb * 128:(nb + 1) * 128, :]),
                        waits=[x for x in w if x[1] > 0], slot=nb % 2)
                    w = [gld[nb]]
                    if nb >= 1:
                        w.append(("dve", gl_ok[nb - 1]))
                    for kc in range(16):
                        for half in range(2):
                            p.add("tensor", lambda e, nb=nb, kc=kc, half=half: e.matmul(
                                psG[:, half * 512:(half + 1) * 512],
                                wgb[nb % 2][:, kc * 128:(kc + 1) * 128],
                                Xln[:, kc * T + half * 512: kc * T + (half + 1) * 512],
                                start=(kc == 0), stop=(kc == 15)),
                                waits=(w if kc == 0 and half == 0 else []),
                                incs=([("pe", 1)] if kc == 15 and half == 1 else []))
                    gmm[nb] = p.n("pe")
                    w = [uld[nb]]
                    if nb >= 1:
                        w.append(("act", ue_ok[nb - 1]))
                    for kc in range(16):
                        for half in range(2):
                            p.add("tensor", lambda e, nb=nb, kc=kc, half=half: e.matmul(
                                psUp[:, half * 512:(half + 1) * 512],
                                wub[nb % 2][:, kc * 128:(kc + 1) * 128],
                                Xln[:, kc * T + half * 512: kc * T + (half + 1) * 512],
                                start=(kc == 0), stop=(kc == 15)),
                                waits=(w if kc == 0 and half == 0 else []),
                                incs=([("pe", 1)] if kc == 15 and half == 1 else []))
                    umm[nb] = p.n("pe")
                    w = [("pe", gmm[nb])]
                    if nb >= 1:
                        w.append(("dve", cg_prev[nb - 1]))
                    p.add("scalar", lambda e: e.activation(
                        out=sqb[:, :], in_=psG[:, :], func=AF.Square),
                        waits=w, incs=[("act", 1)])
                    sq_ok = p.n("act")
                    p.add("vector", lambda e: e.tensor_scalar(
                        out=cgb[:, :], in0=sqb[:, :], scalar1=GELU_A, scalar2=1.0,
                        op0=OP.mult, op1=OP.add),
                        waits=[("act", sq_ok)])
                    p.add("vector", lambda e: e.tensor_tensor(
                        out=cgb[:, :], in0=cgb[:, :], in1=psG[:, :], op=OP.mult),
                        incs=[("dve", 1)])
                    cg_ok = p.n("dve")
                    cg_prev[nb] = cg_ok
                    w = [("dve", cg_ok)]
                    if nb >= 1:
                        w.append(("dve", gl_ok[nb - 1]))
                    p.add("scalar", lambda e: e.activation(
                        out=sgb[:, :], in_=cgb[:, :], func=AF.Sigmoid, scale=np.float32(GELU_S)),
                        waits=w, incs=[("act", 1)])
                    sg_ok = p.n("act")
                    w = [("pe", umm[nb])]
                    if nb >= 1:
                        w.append(("pool", inter_ok[nb - 1]))
                    p.add("scalar", lambda e: e.copy(out=usb[:, :], in_=psUp[:, :]),
                          waits=w, incs=[("act", 1)])
                    ue_ok[nb] = p.n("act")
                    w = [("act", sg_ok)]
                    if nb >= 1:
                        w.append(("pool", inter_ok[nb - 1]))
                    p.add("vector", lambda e: e.tensor_tensor(
                        out=glb[:, :], in0=sgb[:, :], in1=psG[:, :], op=OP.mult),
                        waits=w, incs=[("dve", 1)])
                    gl_ok[nb] = p.n("dve")
                    w = [("act", ue_ok[nb]), ("dve", gl_ok[nb])]
                    if ng >= 1 and ncx == 0:
                        w.append(("pe", dmmprev[(ng - 1, 15)]))
                    p.add("gpsimd", lambda e, ncx=ncx: e.tensor_tensor(
                        out=inter[:, ncx * 1024:(ncx + 1) * 1024], in0=usb[:, :], in1=glb[:, :],
                        op=OP.mult),
                        waits=w, incs=[("pool", 1)])
                    inter_ok[nb] = p.n("pool")
                for hc in range(16):
                    if hc >= 2:
                        w = [("pe", dmmprev[(ng, hc - 2)])]
                    elif ng >= 1:
                        w = [("pe", dmmprev[(ng - 1, 14 + hc)])]
                    else:
                        w = []
                    dld[(ng, hc)] = p.dma("sync", "wdb", lambda e, ng=ng, hc=hc: e.dma_start(
                        out=wdb[hc % 2][:, :], in_=wd_sc[(ng * 16 + hc) * 128:(ng * 16 + hc + 1) * 128, :]),
                        waits=w, slot=hc % 2)
                    w = [dld[(ng, hc)], ("pool", inter_ok[ng * 16 + 15])]
                    for th in range(2):
                        wth = list(w) if th == 0 else []
                        idx = hc * 2 + th
                        if idx >= 2:
                            wth.append(("dve", dadd[(ng, idx - 2)]))
                        elif ng >= 1:
                            wth.append(("dve", dadd[(ng - 1, 30 + idx)]))
                        for ncx in range(16):
                            p.add("tensor", lambda e, ng=ng, hc=hc, th=th, ncx=ncx: e.matmul(
                                psDn[(hc * 2 + th) % 2][:, :],
                                wdb[hc % 2][:, ncx * 128:(ncx + 1) * 128],
                                inter[:, ncx * 1024 + th * 512: ncx * 1024 + (th + 1) * 512],
                                start=(ncx == 0), stop=(ncx == 15)),
                                waits=(wth if ncx == 0 else []),
                                incs=([("pe", 1)] if ncx == 15 else []))
                        dm_ok = p.n("pe")
                        p.add("vector", lambda e, hc=hc, th=th, ng=ng, idx=idx: e.tensor_tensor(
                            out=X[:, hc * T + th * 512: hc * T + (th + 1) * 512],
                            in0=X[:, hc * T + th * 512: hc * T + (th + 1) * 512],
                            in1=psDn[idx % 2][:, :], op=OP.add),
                            waits=[("pe", dm_ok)], incs=[("dve", 1)])
                        dadd[(ng, idx)] = p.n("dve")
                    dmmprev[(ng, hc)] = p.n("pe")
            sc.close()
        dmmprev = {}
        cg_prev = {}

        def ph_attn_b(Xln):
            sc = ExitStack()
            E = sc.enter_context
            gsc_sb = E(nc.sbuf_tensor("gsc_sb", [128, 256], BF16))
            p1s = E(nc.sbuf_tensor("p1s", [128, 2 * 256], BF16))
            p1n = E(nc.sbuf_tensor("p1n", [128, 2 * 256], BF16))
            p2g = E(nc.sbuf_tensor("p2g", [128, 768], BF16))
            nmx = E(nc.sbuf_tensor("nmx", [128, 2], F32))
            denb = E(nc.sbuf_tensor("denb", [128, 2], F32))
            rdb = E(nc.sbuf_tensor("rdb", [128, 2], F32))
            abdbg = E(nc.sbuf_tensor("abdbg", [128, 2048], F32))
            psS1 = [E(nc.psum_tensor(f"psS1{i}", [128, 512], F32)) for i in range(2)]
            psP2 = E(nc.psum_tensor("psP2", [128, 1024], BF16))
            psO = E(nc.psum_tensor("psO", [128, 512], F32))

            gsc_ok = 0
            exp_ok = {}
            evac_ok = 0
            xadd_ok = 0
            prev_sc = 0
            for s in range(NSEG):
                for h in range(H):
                    i = s * H + h
                    par = h % 2
                    base = h * T + s * 512 + par

                    def gv(lo, cnt, base=base):
                        st = base + 2 * lo
                        return Xln[:, st: st + 2 * cnt - 1: 2]

                    w = [("pe", prev_sc)] if i >= 1 else []
                    p.add("vector", lambda e, gv=gv: e.tensor_scalar(
                        out=gsc_sb[:, :], in0=gv(0, 256), scalar1=BSCALE, scalar2=None, op0=OP.mult),
                        waits=w, incs=[("dve", 1)])
                    gsc_ok = p.n("dve")
                    w = [("dve", gsc_ok)]
                    if i >= 1:
                        w.append(("act", exp_ok[i - 1]))
                    for lc in range(2):
                        p.add("tensor", lambda e, lc=lc, gv=gv: e.matmul(
                            psS1[lc][:, 0:256], gsc_sb[:, lc * 128:(lc + 1) * 128],
                            gv(0, 256), start=True, stop=True),
                            waits=(w if lc == 0 else []),
                            incs=([("pe", 1)] if lc == 1 else []))
                    sc_ok = p.n("pe")
                    prev_sc = sc_ok
                    for lc in range(2):
                        p.add("vector", lambda e, lc=lc: e.reduce_max(
                            out=nmx[:, lc:lc + 1], in_=psS1[lc][:, 0:256], axis=AX.X, negate=True),
                            waits=([("pe", sc_ok)] if lc == 0 else []),
                            incs=([("dve", 1)] if lc == 1 else []))
                    nmx_ok = p.n("dve")
                    for lc in range(2):
                        p.add("scalar", lambda e, lc=lc: e.activation(
                            out=p1s[:, lc * 256:(lc + 1) * 256], in_=psS1[lc][:, 0:256],
                            func=AF.Exp, bias=nmx[:, lc:lc + 1]),
                            waits=([("dve", nmx_ok)] if lc == 0 else []),
                            incs=([("act", 1)] if lc == 1 else []))
                    exp_ok[i] = p.n("act")
                    for lc in range(2):
                        p.add("vector", lambda e, lc=lc: e.reduce_sum(
                            out=denb[:, lc:lc + 1], in_=p1s[:, lc * 256:(lc + 1) * 256], axis=AX.X),
                            waits=([("act", exp_ok[i])] if lc == 0 else []))
                    p.add("vector", lambda e: e.reciprocal(out=rdb[:, :], in_=denb[:, :]))
                    p.add("vector", lambda e: e.tensor_copy(out=nmx[:, :], in_=denb[:, :]))
                    p.add("vector", lambda e: e.tensor_copy(out=nmx[:, :], in_=denb[:, :]))
                    for lc in range(2):
                        p.add("vector", lambda e, lc=lc: e.tensor_scalar(
                            out=p1n[:, lc * 256:(lc + 1) * 256], in0=p1s[:, lc * 256:(lc + 1) * 256],
                            scalar1=rdb[:, lc:lc + 1], scalar2=None, op0=OP.mult),
                            incs=([("dve", 1)] if lc == 1 else []))
                    p1n_ok = p.n("dve")
                    w = [("dve", p1n_ok)]
                    if i >= 1:
                        w.append(("act", evac_ok))
                    for mc in range(2):
                        for lc in range(2):
                            p.add("tensor", lambda e, mc=mc, lc=lc: e.transpose(
                                psP2[:, mc * 256 + lc * 128: mc * 256 + (lc + 1) * 128],
                                p1n[:, lc * 256 + mc * 128: lc * 256 + (mc + 1) * 128], idb[:, :]),
                                waits=(w if (mc == 0 and lc == 0) else []))
                    for mc in range(2):
                        p.add("tensor", lambda e, mc=mc, gv=gv: e.transpose(
                            psP2[:, 512 + mc * 128: 512 + (mc + 1) * 128],
                            gv(mc * 128, 128), idb[:, :]),
                            incs=([("pe", 1)] if mc == 1 else []))
                    tr_ok = p.n("pe")
                    w2 = [("pe", tr_ok)]
                    if i >= 1:
                        w2.append(("dve", xadd_ok))
                    p.add("scalar", lambda e: e.copy(out=p2g[:, :], in_=psP2[:, 0:768]),
                          waits=w2, incs=[("act", 1)])
                    evac_ok = p.n("act")
                    for mc in range(2):
                        wav = [("act", evac_ok)]
                        if i >= 1:
                            wav.append(("dve", xadd_ok))
                        p.add("tensor", lambda e, mc=mc: e.matmul(
                            psO[:, 0:256], p2g[:, 512 + mc * 128: 512 + (mc + 1) * 128],
                            p2g[:, mc * 256:(mc + 1) * 256], start=(mc == 0), stop=(mc == 1)),
                            waits=(wav if mc == 0 else []),
                            incs=([("pe", 1)] if mc == 1 else []))
                    av_ok = p.n("pe")
                    if i == 0 and "abdbg_sc" in DEBUG_SCRATCH:
                        p.add("vector", lambda e: e.tensor_copy(out=abdbg[:, 0:256], in_=gsc_sb[:, :]),
                              waits=[("pe", av_ok)])
                        p.add("vector", lambda e: e.tensor_copy(out=abdbg[:, 256:512], in_=p1n[:, 0:256]))
                        p.add("vector", lambda e: e.tensor_copy(out=abdbg[:, 512:768], in_=p1n[:, 256:512]))
                        p.add("vector", lambda e: e.tensor_copy(out=abdbg[:, 768:770], in_=denb[:, :]))
                        p.add("vector", lambda e: e.tensor_copy(out=abdbg[:, 770:772], in_=rdb[:, :]))
                        p.add("vector", lambda e: e.tensor_copy(out=abdbg[:, 772:1540], in_=p2g[:, :]))
                        p.add("vector", lambda e: e.tensor_copy(out=abdbg[:, 1540:1796], in_=psO[:, 0:256]),
                              incs=[("dve", 1)])
                        p.dma("gpsimd", "dbg", lambda e: e.dma_start(out=abdbg_sc[:, :], in_=abdbg[:, :]),
                              waits=[("dve", p.n("dve"))])
                    p.add("vector", lambda e, base=base: e.tensor_tensor(
                        out=Xs_strided(base), in0=Xs_strided(base), in1=psO[:, 0:256], op=OP.add),
                        waits=[("pe", av_ok)], incs=[("dve", 1)])
                    xadd_ok = p.n("dve")
            sc.close()

        def Xs_strided(base):
            return X[:, base: base + 511: 2]

        def ph_out():
            sc = ExitStack()
            E = sc.enter_context
            yst = [E(nc.sbuf_tensor(f"yst{i}", [128, D], F32)) for i in range(2)]
            psY = [E(nc.psum_tensor(f"psY{i}", [128, 512], F32)) for i in range(2)]
            evy = {}
            wry = {}
            for tc in range(TC):
                for g in range(4):
                    lin = tc * 4 + g
                    w = []
                    if lin >= 2:
                        w.append(("dve", evy[lin - 2]))
                    if g == 0 and tc >= 2:
                        w.append(wry[tc - 2])
                    for j in range(4):
                        fc = g * 4 + j
                        p.add("tensor", lambda e, tc=tc, g=g, j=j, fc=fc: e.transpose(
                            psY[g % 2][:, j * 128:(j + 1) * 128],
                            X[:, fc * T + tc * 128: fc * T + tc * 128 + 128], idf[:, :]),
                            waits=(w if j == 0 else []),
                            incs=([("pe", 1)] if j == 3 else []))
                    pe_ok = p.n("pe")
                    p.add("vector", lambda e, tc=tc, g=g: e.tensor_copy(
                        out=yst[tc % 2][:, :].rearrange("p (g j f) -> p g j f", g=4, j=4)[:, g, :, :],
                        in_=psY[g % 2][:, :].rearrange("p (j f) -> p j f", j=4)),
                        waits=[("pe", pe_ok)], incs=[("dve", 1)])
                    evy[lin] = p.n("dve")
                wry[tc] = p.dma("gpsimd", "yout", lambda e, tc=tc: e.dma_start(
                    out=y_out[tc * 128:(tc + 1) * 128, :], in_=yst[tc % 2][:, :]),
                    waits=[("dve", evy[tc * 4 + 3])], slot=tc % 2)
            sc.close()

        DECODE_ON = STOP_AFTER in (None, "attnb", "ffn2", "ffn1", "decode") and "decode" not in DEBUG_SKIP
        dec_obj = Decoder() if DECODE_ON else None

        def drain2():
            if dec_obj is not None:
                dec_obj.drain(1)
        cos_sc = ExitStack()
        cose = cos_sc.enter_context(nc.sbuf_tensor("cose", [128, TC * 64], F32))
        coso = cos_sc.enter_context(nc.sbuf_tensor("coso", [128, TC * 64], F32))
        sine = cos_sc.enter_context(nc.sbuf_tensor("sine", [128, TC * 64], F32))
        sino = cos_sc.enter_context(nc.sbuf_tensor("sino", [128, TC * 64], F32))
        for dst, src in ((idb, idb_in), (idf, idf_in), (premask, premask_in), (onesb, ones_in), (lnc, lnc_in)):
            p.dma("sync", "const", lambda e, d=dst, s=src: e.dma_start(out=d[:, :], in_=s))
        for dst, src in ((cose, cose_in), (coso, coso_in), (sine, sine_in), (sino, sino_in)):
            p.dma("sync", "const", lambda e, d=dst, s=src: e.dma_start(
                out=d[:, :].rearrange("p (tc j) -> p tc j", tc=TC),
                in_=s.rearrange("(tc p) j -> p tc j", p=128)))
        p.dma("sync", "const", lambda e: e.dma_start(
            out=maska[:, :].rearrange("p (m l) -> p m l", m=2),
            in_=maska_in.rearrange("(m p) l -> p m l", p=128)))
        p.dma("sync", "const", lambda e: e.dma_start(out=ones_row[:, :], in_=onesr_in))
        CONSTS_TOK = p.dma("sync", "const", lambda e: e.dma_start(out=onesrf[:, :], in_=onesrf_in))
        Xsc = ExitStack()
        Xln1 = Xsc.enter_context(nc.sbuf_tensor("Xln1", [128, FC * T], BF16))
        ph_ln("1", Xln1, first=True, drain=drain2)
        barrier()
        if "xdbg_sc" in DEBUG_SCRATCH:
            p.dma("gpsimd", "dbg", lambda e: e.dma_start(out=xdbg_sc, in_=X[:, :]))
            p.dma("gpsimd", "dbg", lambda e: e.dma_start(out=xlndbg_sc, in_=Xln1[:, :]))
            barrier()
        if STOP_AFTER not in ("ln1",) and "wt" not in DEBUG_SKIP:
            ph_wt(drain=drain2)
            barrier()
        if STOP_AFTER not in ("ln1", "wt") and "qkv" not in DEBUG_SKIP:
            ph_qkv(Xln1, drain=drain2)
            barrier()
        Xsc.close()
        cos_sc.close()
        if STOP_AFTER not in ("ln1", "wt", "qkv") and "attna" not in DEBUG_SKIP:
            ph_attn_a(drain=drain2)
            barrier()
            if "xdbg_sc" in DEBUG_SCRATCH and STOP_AFTER == "attna":
                p.dma("gpsimd", "dbg", lambda e: e.dma_start(out=xdbg_sc, in_=X[:, :]))
                barrier()
        if dec_obj is not None:
            dec_obj.flush()
            barrier()
        if STOP_AFTER not in ("ln1", "wt", "qkv", "attna", "decode"):
            Xsc2 = ExitStack()
            Xln2 = Xsc2.enter_context(nc.sbuf_tensor("Xln2", [128, FC * T], BF16))
            ph_ln("2", Xln2, first=False)
            barrier()
            if STOP_AFTER == "ln2":
                if "xlndbg_sc" in DEBUG_SCRATCH:
                    p.dma("gpsimd", "dbg", lambda e: e.dma_start(out=xlndbg_sc, in_=Xln2[:, :]))
                    barrier()
            else:
                ph_ffn(Xln2, 1)
                barrier()
            Xsc2.close()
        if STOP_AFTER in (None, "attnb", "ffn2"):
            Xsc3 = ExitStack()
            Xln3 = Xsc3.enter_context(nc.sbuf_tensor("Xln3", [128, FC * T], BF16))
            ph_ln("3", Xln3, first=False)
            barrier()
            ph_attn_b(Xln3)
            Xsc3.close()
            barrier()
            if "xdbg_sc" in DEBUG_SCRATCH and STOP_AFTER == "attnb":
                p.dma("gpsimd", "dbg", lambda e: e.dma_start(out=xdbg_sc, in_=X[:, :]))
                barrier()
        if STOP_AFTER in (None, "ffn2"):
            Xsc4 = ExitStack()
            Xln4 = Xsc4.enter_context(nc.sbuf_tensor("Xln4", [128, FC * T], BF16))
            ph_ln("4", Xln4, first=False)
            barrier()
            ph_ffn(Xln4, 2)
            Xsc4.close()
            barrier()
        if STOP_AFTER is None:
            ph_out()
            if "xdbg_sc" in DEBUG_SCRATCH and STOP_AFTER == "ffn1":
                p.dma("gpsimd", "dbg", lambda e: e.dma_start(out=xdbg_sc, in_=X[:, :]))
                barrier()


        p.add("gpsimd", None, waits=p.all_sems(), incs=[("fin", 1)])

        block = es.enter_context(nc.Block())
        p.emit(block)
    return nc


def host_consts(core_id):
    pos0 = T * core_id
    invf = (10000.0 ** (-np.arange(0, HD, 2, dtype=np.float64) / HD))
    pos = np.arange(pos0, pos0 + T, dtype=np.float64)
    f = np.outer(pos, invf)
    emb = np.concatenate([f, f], axis=-1)
    cos = np.cos(emb).astype(np.float32)
    sin = np.sin(emb).astype(np.float32)
    return {
        "cose_in": np.ascontiguousarray(cos[:, 0::2]),
        "coso_in": np.ascontiguousarray(cos[:, 1::2]),
        "sine_in": np.ascontiguousarray(sin[:, 0::2]),
        "sino_in": np.ascontiguousarray(sin[:, 1::2]),
        "idb_in": np.eye(128, dtype=np.float32).astype(BF),
        "idf_in": np.eye(128, dtype=np.float32),
        "maska_in": np.concatenate([
            (np.arange(256)[None, :] >= np.arange(128)[:, None]).astype(np.float32),
            (np.arange(256)[None, :] >= (np.arange(128) + 128)[:, None]).astype(np.float32),
        ], axis=0),
        "ones_in": np.ones((128, 2), dtype=np.float32).astype(BF),
        "onesr_in": np.ones((1, 128), dtype=np.float32).astype(BF),
        "onesrf_in": np.ones((1, 128), dtype=np.float32),
        "lnc_in": np.broadcast_to(np.array([1.0 / D, -1.0], np.float32), (128, 2)).copy(),
        "premask_in": np.broadcast_to(
            (np.arange(8)[None, :] < core_id).astype(np.float32), (128, 8)).copy(),
    }


def make_in_maps(inputs):
    x = np.ascontiguousarray(np.asarray(inputs["L0_attn_a_x"], dtype=np.float32)[0])
    shared = {
        "wqkv_in": np.ascontiguousarray(np.asarray(inputs["L0_attn_a_W_qkv"], dtype=np.float32)),
        "wout_in": np.ascontiguousarray(np.asarray(inputs["L0_attn_a_W_out"], dtype=np.float32)),
        "gpk_in": np.ascontiguousarray(np.asarray(inputs["L0_ffn_gate_W_packed"], dtype=np.int32)),
        "gsc_in": np.ascontiguousarray(np.asarray(inputs["L0_ffn_gate_scales"], dtype=np.float32)),
        "upk_in": np.ascontiguousarray(np.asarray(inputs["L0_ffn_up_W_packed"], dtype=np.int32)),
        "usc_in": np.ascontiguousarray(np.asarray(inputs["L0_ffn_up_scales"], dtype=np.float32)),
        "dpk_in": np.ascontiguousarray(np.asarray(inputs["L0_ffn_down_W_packed"], dtype=np.int32)),
        "dsc_in": np.ascontiguousarray(np.asarray(inputs["L0_ffn_down_scales"], dtype=np.float32)),
    }
    in_maps = []
    for i in range(NCORES):
        m = {"x_in": np.ascontiguousarray(x[i * T:(i + 1) * T])}
        m.update(shared)
        m.update(host_consts(i))
        in_maps.append(m)
    return in_maps


def kernel(L0_attn_a_x, L0_attn_a_W_qkv, L0_attn_a_W_out,
           L0_ffn_gate_W_packed, L0_ffn_gate_scales,
           L0_ffn_up_W_packed, L0_ffn_up_scales,
           L0_ffn_down_W_packed, L0_ffn_down_scales):
    in_maps = make_in_maps(dict(
        L0_attn_a_x=L0_attn_a_x, L0_attn_a_W_qkv=L0_attn_a_W_qkv,
        L0_attn_a_W_out=L0_attn_a_W_out,
        L0_ffn_gate_W_packed=L0_ffn_gate_W_packed, L0_ffn_gate_scales=L0_ffn_gate_scales,
        L0_ffn_up_W_packed=L0_ffn_up_W_packed, L0_ffn_up_scales=L0_ffn_up_scales,
        L0_ffn_down_W_packed=L0_ffn_down_W_packed, L0_ffn_down_scales=L0_ffn_down_scales))
    nc = build()
    res = run_bass_kernel_spmd(nc, in_maps, core_ids=list(range(NCORES)))
    out = np.concatenate([res.results[i]["y_out"] for i in range(NCORES)], axis=0)
    return out[None, :, :]


# version 06
# speedup vs baseline: 1.5654x; 1.5076x over previous
import numpy as np
import ml_dtypes
from contextlib import ExitStack

import concourse.bass as bass
import concourse.mybir as mybir
from concourse.bass_utils import run_bass_kernel_spmd

F32 = mybir.dt.float32
BF16 = mybir.dt.bfloat16
I32 = mybir.dt.int32
I16 = mybir.dt.int16
AF = mybir.ActivationFunctionType
OP = mybir.AluOpType
AX = mybir.AxisListType
BF = ml_dtypes.bfloat16

NCORES = 8
T = 1024
TC = 8
D = 2048
FC = 16
H = 16
HD = 128
CH = 256
NCH = 4
SEG = 512
NSEG = 2
NMLP = 16384
EPS = 1e-5
GELU_A = 0.044715
GELU_S = 1.5957691216057308
BSCALE = 0.08838834764831845

ENGINES = ("sync", "tensor", "vector", "scalar", "gpsimd")

STOP_AFTER = None
DEBUG_SCRATCH = ()
DEBUG_SKIP = ()
RACE_DETECT = True
LN_PASS_A_ONLY = False


class Prog:

    def __init__(self, nc, es):
        self.nc = nc
        self.es = es
        self.ops = {e: [] for e in ENGINES}
        self.sem_h = {}
        self.sem_n = {}

    def sem(self, name):
        if name not in self.sem_h:
            self.sem_h[name] = self.es.enter_context(self.nc.semaphore(name))
            self.sem_n[name] = 0
        return name

    def n(self, name):
        return self.sem_n[name]

    def dma(self, eng, stream, fn, waits=(), slot=0):
        s = self.sem(f"{stream}.{slot}")
        self.add(eng, fn, waits=waits, incs=[(s, 16)])
        return (s, self.n(s))

    def all_sems(self):
        return [(s, self.sem_n[s]) for s in self.sem_n]

    def add(self, eng, fn=None, waits=(), incs=()):
        waits = [(self.sem_h[s], v) for s, v in waits if v > 0]
        incs_h = [(self.sem_h[s], k) for s, k in incs]
        for s, k in incs:
            self.sem_n[s] += k

        def emit(e):
            for h, v in waits:
                e.wait_ge(h, v)
            if fn is not None:
                ins = fn(e)
                for h, k in incs_h:
                    ins.then_inc(h, k, skip_validation=(k == 16))
            else:
                for h, k in incs_h:
                    e.sem_inc(h, k)
        self.ops[eng].append(emit)

    def emit(self, block):
        for ename in ENGINES:
            lst = self.ops[ename]
            if not lst:
                continue
            sec = getattr(block, ename)

            def make(lst):
                def _(e):
                    for op in lst:
                        op(e)
                return _
            sec(make(lst))


def r3(ap, b=4):
    return ap.rearrange("p (b d) -> p b d", b=b)


def build():
    nc = bass.Bass("TRN2", target_bir_lowering=False, detect_race_conditions=RACE_DETECT)

    def din(name, shape, dt):
        return nc.dram_tensor(name, shape, dt, kind="ExternalInput").ap()

    x_in = din("x_in", [T, D], F32)
    wqkv_in = din("wqkv_in", [3 * D, D], F32)
    wout_in = din("wout_in", [D, D], F32)
    gpk_in = din("gpk_in", [NMLP, D // 2], I32)
    gsc_in = din("gsc_in", [NMLP, 64], F32)
    upk_in = din("upk_in", [NMLP, D // 2], I32)
    usc_in = din("usc_in", [NMLP, 64], F32)
    dpk_in = din("dpk_in", [D, NMLP // 2], I32)
    dsc_in = din("dsc_in", [D, 512], F32)
    idb_in = din("idb_in", [128, 128], BF16)
    idf_in = din("idf_in", [128, 128], F32)
    cose_in = din("cose_in", [T, 64], F32)
    coso_in = din("coso_in", [T, 64], F32)
    sine_in = din("sine_in", [T, 64], F32)
    sino_in = din("sino_in", [T, 64], F32)
    maska_in = din("maska_in", [256, 256], F32)
    ones_in = din("ones_in", [128, 2], BF16)
    onesr_in = din("onesr_in", [1, 128], BF16)
    onesrf_in = din("onesrf_in", [1, 128], F32)
    premask_in = din("premask_in", [128, 8], F32)
    mypk_in = din("mypk_in", [48 * 128, 1024], I32)
    mysc_in = din("mysc_in", [48 * 128, 64], F32)
    lnc_in = din("lnc_in", [128, 2], F32)

    y_out = nc.dram_tensor("y_out", [T, D], F32, kind="ExternalOutput").ap()

    def dscr(name, shape, dt):
        kind = "ExternalOutput" if name in DEBUG_SCRATCH else "Internal"
        return nc.dram_tensor(name, shape, dt, kind=kind).ap()

    wqkvT_sc = dscr("wqkvT_sc", [D, 3 * D], BF16)
    woutT_sc = dscr("woutT_sc", [D, D], BF16)
    q_sc = dscr("q_sc", [T, D], BF16)
    k_sc = dscr("k_sc", [T, D], BF16)
    v_sc = dscr("v_sc", [T, D], BF16)
    ccw_in = dscr("ccw_in", [48 * 128, 2048], BF16)
    ccd_out = nc.dram_tensor("ccd_out", [8 * 48 * 128, 2048], BF16,
                             kind="ExternalOutput" if "ccd_out" in DEBUG_SCRATCH else "Internal",
                             addr_space="Shared").ap()
    cc_in = dscr("cc_in", [1, 128 * H * 132], F32)
    cc_out = dscr("cc_out", [NCORES, 128 * H * 132], F32)
    xdbg_sc = dscr("xdbg_sc", [128, FC * T], F32)
    sdbg_sc = dscr("sdbg_sc", [128, H * 132], F32)
    abdbg_sc = dscr("abdbg_sc", [128, 2048], F32)
    odbg_sc = dscr("odbg_sc", [128, H * 256], BF16)
    xlndbg_sc = dscr("xlndbg_sc", [128, FC * T], BF16)

    with ExitStack() as es:
        p = Prog(nc, es)
        for s in ("pe", "dve", "act", "pool", "cc", "fin",
                  "b_sync", "b_tensor", "b_vector", "b_scalar", "b_gpsimd"):
            p.sem(s)
        EB = es.enter_context

        def barrier():
            for e in ENGINES:
                p.add(e, None, incs=[(f"b_{e}", 1)])
            w = p.all_sems()
            for e in ENGINES:
                p.add(e, None, waits=w)

        X = EB(nc.sbuf_tensor("X", [128, FC * T], F32))
        idb = EB(nc.sbuf_tensor("idb", [128, 128], BF16))
        idf = EB(nc.sbuf_tensor("idf", [128, 128], F32))
        maska = EB(nc.sbuf_tensor("maska", [128, 2 * 256], F32))
        onesb = EB(nc.sbuf_tensor("onesb", [128, 2], BF16))
        ones_row = EB(nc.sbuf_tensor("ones_row", [1, 128], BF16))
        onesrf = EB(nc.sbuf_tensor("onesrf", [1, 128], F32))
        premask = EB(nc.sbuf_tensor("premask", [128, 8], F32))
        lnc = EB(nc.sbuf_tensor("lnc", [128, 2], F32))


        def ph_ln(tag, Xln, first, drain=None):
            sc = ExitStack()
            E = sc.enter_context
            xtm = [E(nc.sbuf_tensor(f"xtm{tag}{i}", [128, D], F32)) for i in range(2)]
            xlt = [E(nc.sbuf_tensor(f"xlt{tag}{i}", [128, D], BF16)) for i in range(2)]
            junk = E(nc.sbuf_tensor(f"junk{tag}", [128, D], BF16))
            st = {nm: E(nc.sbuf_tensor(f"{nm}{tag}", [128, TC], F32))
                  for nm in ("sx", "sxx", "mu", "musq", "var", "srt", "rstd", "nmr")}
            psX = [E(nc.psum_tensor(f"psX{tag}{i}", [128, 512], F32)) for i in range(2)]
            psL = [E(nc.psum_tensor(f"psL{tag}{i}", [128, 1024], BF16)) for i in range(2)]

            def make_xtm(tc, evX, post):
                if first:
                    w = [x for x in post.get(("xtmfree", tc - 2), []) if x[1] > 0]
                    tok = p.dma("sync", "xld", lambda e, tc=tc: e.dma_start(
                        out=xtm[tc % 2][:, :], in_=x_in[tc * 128:(tc + 1) * 128, :]),
                        waits=w, slot=tc % 2)
                    return [tok, CONSTS_TOK]
                else:
                    last = []
                    for g in range(4):
                        lin = tc * 4 + g
                        w = []
                        if ("psX", lin - 2) in post:
                            w.append(post[("psX", lin - 2)])
                        if g == 0:
                            w += post.get(("xtmfree", tc - 2), [])
                        w = [x for x in w if x[1] > 0]
                        for j in range(4):
                            fc = g * 4 + j
                            p.add("tensor", lambda e, tc=tc, g=g, j=j, fc=fc: e.transpose(
                                psX[g % 2][:, j * 128:(j + 1) * 128],
                                X[:, fc * T + tc * 128: fc * T + tc * 128 + 128], idf[:, :]),
                                waits=(w if j == 0 else []),
                                incs=([("pe", 1)] if j == 3 else []))
                        pe_ok = p.n("pe")
                        p.add("scalar", lambda e, tc=tc, g=g: e.copy(
                            out=xtm[tc % 2][:, :].rearrange("p (g j f) -> p g j f", g=4, j=4)[:, g, :, :],
                            in_=psX[g % 2][:, :].rearrange("p (j f) -> p j f", j=4)),
                            waits=[("pe", pe_ok)], incs=[("act", 1)])
                        post[("psX", lin)] = ("act", p.n("act"))
                    return [("act", p.n("act"))]

            post = {}
            for tc in range(TC):
                if drain:
                    drain()
                rdy = make_xtm(tc, None, post)
                xs = xtm[tc % 2][:, :]
                p.add("vector", lambda e, xs=xs, tc=tc: e.reduce_sum(
                    out=st["sx"][:, tc:tc + 1], in_=xs, axis=AX.X),
                    waits=rdy, incs=[("dve", 1)])
                rsum_ok = p.n("dve")
                p.add("scalar", lambda e, xs=xs, tc=tc: e.activation(
                    out=junk[:, :], in_=xs, func=AF.Square, accum_out=st["sxx"][:, tc:tc + 1]),
                    waits=rdy, incs=[("act", 1)])
                post[("xtmfree", tc)] = [("act", p.n("act")), ("dve", rsum_ok)]
            sxA, sxxA = p.n("dve"), p.n("act")

            lnc0b = lnc[:, 0:1].broadcast_to([128, TC])
            lnc1b = lnc[:, 1:2].broadcast_to([128, TC])
            p.add("gpsimd", lambda e: e.tensor_tensor(
                out=st["mu"][:, :], in0=lnc0b, in1=st["sx"][:, :], op=OP.mult),
                waits=[("dve", sxA)])
            p.add("gpsimd", lambda e: e.tensor_tensor(
                out=st["musq"][:, :], in0=st["mu"][:, :], in1=st["mu"][:, :], op=OP.mult))
            p.add("gpsimd", lambda e: e.tensor_tensor(
                out=st["var"][:, :], in0=lnc0b, in1=st["sxx"][:, :], op=OP.mult),
                waits=[("act", sxxA)])
            p.add("gpsimd", lambda e: e.tensor_tensor(
                out=st["var"][:, :], in0=st["var"][:, :], in1=st["musq"][:, :], op=OP.subtract),
                incs=[("pool", 1)])
            var_ok = p.n("pool")
            p.add("scalar", lambda e: e.activation(
                out=st["srt"][:, :], in_=st["var"][:, :], func=AF.Sqrt,
                bias=np.float32(EPS), scale=np.float32(1.0)),
                waits=[("pool", var_ok)], incs=[("act", 1)])
            srt_ok = p.n("act")
            p.add("vector", lambda e: e.reciprocal(out=st["rstd"][:, :], in_=st["srt"][:, :]),
                  waits=[("act", srt_ok)], incs=[("dve", 1)])
            rstd_ok = p.n("dve")
            p.add("gpsimd", lambda e: e.tensor_tensor(
                out=st["musq"][:, :], in0=lnc1b, in1=st["mu"][:, :], op=OP.mult))
            p.add("gpsimd", lambda e: e.tensor_tensor(
                out=st["nmr"][:, :], in0=st["musq"][:, :], in1=st["rstd"][:, :], op=OP.mult),
                waits=[("dve", rstd_ok)], incs=[("pool", 1)])
            nmr_ok = p.n("pool")

            if LN_PASS_A_ONLY and not first:
                sc.close()
                return
            passA_tail = [("act", p.n("act")), ("dve", p.n("dve"))]
            post = {("psX", -2): passA_tail[0], ("psX", -1): passA_tail[1],
                    ("xtmfree", -2): passA_tail, ("xtmfree", -1): passA_tail}
            aff = {}
            evL = {}
            for tc in range(TC):
                if drain:
                    drain()
                rdy = make_xtm(tc, None, post)
                if first:
                    for g in range(4):
                        lin = tc * 4 + g
                        w = list(rdy) if g == 0 else []
                        if ("psX", lin - 2) in post:
                            w.append(post[("psX", lin - 2)])
                        for j in range(4):
                            fc = g * 4 + j
                            p.add("tensor", lambda e, tc=tc, g=g, j=j, fc=fc: e.transpose(
                                psX[g % 2][:, j * 128:(j + 1) * 128],
                                xtm[tc % 2][:, fc * 128:(fc + 1) * 128], idf[:, :]),
                                waits=(w if j == 0 else []),
                                incs=([("pe", 1)] if j == 3 else []))
                        pe_ok = p.n("pe")
                        p.add("vector", lambda e, tc=tc, g=g: e.tensor_copy(
                            out=X[:, :].rearrange("p (f t) -> p f t", f=FC)[:, g * 4:(g + 1) * 4, tc * 128:(tc + 1) * 128],
                            in_=psX[g % 2][:, :].rearrange("p (j f) -> p j f", j=4)),
                            waits=[("pe", pe_ok)], incs=[("dve", 1)])
                        post[("psX", lin)] = ("dve", p.n("dve"))

                waff = list(rdy) + [("pool", nmr_ok)]
                if tc >= 2:
                    waff.append(("act", evL[(tc - 2) * 4 + 3]))
                p.add("vector", lambda e, tc=tc: e.tensor_scalar(
                    out=xlt[tc % 2][:, :], in0=xtm[tc % 2][:, :],
                    scalar1=st["rstd"][:, tc:tc + 1], scalar2=st["nmr"][:, tc:tc + 1],
                    op0=OP.mult, op1=OP.add),
                    waits=waff, incs=[("dve", 1)])
                aff[tc] = p.n("dve")
                post[("xtmfree", tc)] = [("dve", aff[tc])]

                for g in range(4):
                    lin = tc * 4 + g
                    w = [("dve", aff[tc])] if g == 0 else []
                    if lin >= 2:
                        w.append(("act", evL[lin - 2]))
                    for j in range(4):
                        fc = g * 4 + j
                        p.add("tensor", lambda e, tc=tc, g=g, j=j, fc=fc: e.transpose(
                            psL[g % 2][:, j * 128:(j + 1) * 128],
                            xlt[tc % 2][:, fc * 128:(fc + 1) * 128], idb[:, :]),
                            waits=(w if j == 0 else []),
                            incs=([("pe", 1)] if j == 3 else []))
                    pe_ok = p.n("pe")
                    p.add("scalar", lambda e, Xln=Xln, tc=tc, g=g: e.copy(
                        out=Xln[:, :].rearrange("p (f t) -> p f t", f=FC)[:, g * 4:(g + 1) * 4, tc * 128:(tc + 1) * 128],
                        in_=psL[g % 2][:, 0:512].rearrange("p (j f) -> p j f", j=4)),
                        waits=[("pe", pe_ok)], incs=[("act", 1)])
                    evL[lin] = p.n("act")
            sc.close()

        def ph_wt(drain=None):
            sc = ExitStack()
            E = sc.enter_context
            wraw = [E(nc.sbuf_tensor(f"wraw{i}", [128, D], F32)) for i in range(2)]
            wbf = [E(nc.sbuf_tensor(f"wbf{i}", [128, D], BF16)) for i in range(4)]
            wst = [E(nc.sbuf_tensor(f"wstg{i}", [128, 512], BF16)) for i in range(2)]
            psT = [E(nc.psum_tensor(f"psWT{i}", [128, 1024], BF16)) for i in range(2)]

            cvt = {}
            evc = {}
            dmo = {}
            units = [("qkv", og) for og in range(12)] + [("out", og) for og in range(4)]
            for u, (kind, og) in enumerate(units):
                if drain:
                    drain()
                src = wqkv_in if kind == "qkv" else wout_in
                dst = wqkvT_sc if kind == "qkv" else woutT_sc
                for j in range(4):
                    if j < 2:
                        w = [("act", cvt[(u - 1, j + 2)])] if u >= 1 else []
                    else:
                        w = [("act", cvt[(u, j - 2)])]
                    tok = p.dma("sync", "wraw", lambda e, src=src, og=og, j=j: e.dma_start(
                        out=wraw[j % 2][:, :], in_=src[og * 512 + j * 128: og * 512 + (j + 1) * 128, :]),
                        waits=w, slot=j % 2)
                    w2 = [tok]
                    if u >= 1:
                        w2.append(("pe", pe_units[u - 1]))
                    p.add("scalar", lambda e, j=j: e.copy(out=wbf[j][:, :], in_=wraw[j % 2][:, :]),
                          waits=w2, incs=[("act", 1)])
                    cvt[(u, j)] = p.n("act")
                for kc in range(16):
                    lin = u * 16 + kc
                    w = [("act", cvt[(u, 3)])] if kc == 0 else []
                    if lin >= 2:
                        w.append(("act", evc[lin - 2]))
                    for j in range(4):
                        p.add("tensor", lambda e, kc=kc, j=j: e.transpose(
                            psT[kc % 2][:, j * 128:(j + 1) * 128],
                            wbf[j][:, kc * 128:(kc + 1) * 128], idb[:, :]),
                            waits=(w if j == 0 else []),
                            incs=([("pe", 1)] if j == 3 else []))
                    pe_ok = p.n("pe")
                    w2 = [("pe", pe_ok)]
                    if lin >= 2:
                        w2.append(dmo[lin - 2])
                    p.add("scalar", lambda e, kc=kc: e.copy(out=wst[kc % 2][:, :], in_=psT[kc % 2][:, 0:512]),
                          waits=w2, incs=[("act", 1)])
                    evc[lin] = p.n("act")
                    dmo[lin] = p.dma("gpsimd", "wtw", lambda e, dst=dst, kc=kc, og=og: e.dma_start(
                        out=dst[kc * 128:(kc + 1) * 128, og * 512:(og + 1) * 512], in_=wst[kc % 2][:, :]),
                        waits=[("act", evc[lin])], slot=kc % 2)
                pe_units[u] = p.n("pe")
            sc.close()
        pe_units = {}

        def ph_qkv(Xln, drain=None):
            sc = ExitStack()
            E = sc.enter_context
            wqT = [E(nc.sbuf_tensor(f"wqT{i}", [128, 16 * 512], BF16)) for i in range(2)]
            stg = [E(nc.sbuf_tensor(f"qstg{i}", [128, 512], BF16)) for i in range(3)]
            ra = E(nc.sbuf_tensor("ra", [128, 256], F32))
            rb = E(nc.sbuf_tensor("rb", [128, 256], F32))
            rc = E(nc.sbuf_tensor("rcc", [128, 256], F32))
            rd = E(nc.sbuf_tensor("rdd", [128, 256], F32))
            qp = E(nc.sbuf_tensor("qp", [128, 512], F32))
            me = E(nc.sbuf_tensor("me", [128, 512], F32))
            ex = E(nc.sbuf_tensor("ex", [128, 512], F32))
            rl = E(nc.sbuf_tensor("rl", [128, 512], F32))
            psQ = [E(nc.psum_tensor(f"psQ{i}", [128, 512], F32)) for i in range(2)]

            psum_free = {}
            stage_ok = {}
            spill_ok = {}
            wq_ok = {}
            mm_done = {}
            for og in range(12):
                w = [("pe", mm_done[(og - 2) * 8 + 7])] if og >= 2 else []
                wq_ok[og] = p.dma("sync", "wqT", lambda e, og=og: e.dma_start(
                    out=wqT[og % 2][:, :].rearrange("p (kc f) -> p kc f", kc=16),
                    in_=wqkvT_sc[:, og * 512:(og + 1) * 512].rearrange("(kc p) f -> p kc f", p=128)),
                    waits=w, slot=og % 2)
                kind = "q" if og < 4 else ("k" if og < 8 else "v")
                for tc in range(TC):
                    if drain:
                        drain()
                    i = og * 8 + tc
                    w = [wq_ok[og]] if tc == 0 else []
                    if i >= 2:
                        w.append(psum_free[i - 2])
                    for kc in range(16):
                        p.add("tensor", lambda e, og=og, tc=tc, kc=kc, i=i: e.matmul(
                            psQ[i % 2][:, :],
                            Xln[:, kc * T + tc * 128: kc * T + tc * 128 + 128],
                            wqT[og % 2][:, kc * 512:(kc + 1) * 512],
                            start=(kc == 0), stop=(kc == 15)),
                            waits=(w if kc == 0 else []),
                            incs=([("pe", 1)] if kc == 15 else []))
                    mm_done[i] = p.n("pe")
                    stb = stg[i % 3]
                    wstg = [spill_ok[i - 3]] if i >= 3 else []
                    if kind == "v":
                        p.add("scalar", lambda e, stb=stb, i=i: e.copy(out=stb[:, :], in_=psQ[i % 2][:, :]),
                              waits=[("pe", mm_done[i])] + wstg, incs=[("act", 1)])
                        psum_free[i] = ("act", p.n("act"))
                        stage_ok[i] = ("act", p.n("act"))
                    else:
                        ps = psQ[i % 2][:, :]
                        pse = r3(ps)[:, :, 0::2]
                        pso = r3(ps)[:, :, 1::2]
                        ce = cose[:, tc * 64:(tc + 1) * 64].unsqueeze(1).broadcast_to([128, 4, 64])
                        co = coso[:, tc * 64:(tc + 1) * 64].unsqueeze(1).broadcast_to([128, 4, 64])
                        se = sine[:, tc * 64:(tc + 1) * 64].unsqueeze(1).broadcast_to([128, 4, 64])
                        so = sino[:, tc * 64:(tc + 1) * 64].unsqueeze(1).broadcast_to([128, 4, 64])
                        p.add("vector", lambda e, ce=ce, pse=pse: e.tensor_tensor(
                            out=r3(ra[:, :], 4), in0=ce, in1=pse, op=OP.mult),
                            waits=[("pe", mm_done[i])])
                        p.add("vector", lambda e, se=se, pso=pso: e.tensor_tensor(
                            out=r3(rb[:, :], 4), in0=se, in1=pso, op=OP.mult))
                        p.add("vector", lambda e, co=co, pso=pso: e.tensor_tensor(
                            out=r3(rc[:, :], 4), in0=co, in1=pso, op=OP.mult))
                        p.add("vector", lambda e, so=so, pse=pse: e.tensor_tensor(
                            out=r3(rd[:, :], 4), in0=so, in1=pse, op=OP.mult),
                            incs=[("dve", 1)])
                        psum_free[i] = ("dve", p.n("dve"))
                        p.add("vector", lambda e: e.tensor_tensor(
                            out=r3(qp[:, :])[:, :, 0::2], in0=r3(ra[:, :], 4), in1=r3(rb[:, :], 4),
                            op=OP.subtract))
                        p.add("vector", lambda e: e.tensor_tensor(
                            out=r3(qp[:, :])[:, :, 1::2], in0=r3(rc[:, :], 4), in1=r3(rd[:, :], 4),
                            op=OP.add))
                        p.add("vector", lambda e: e.tensor_scalar(
                            out=me[:, :], in0=qp[:, :], scalar1=0.0, scalar2=None, op0=OP.min),
                            incs=[("dve", 1)])
                        me_ok = p.n("dve")
                        p.add("scalar", lambda e: e.activation(out=ex[:, :], in_=me[:, :], func=AF.Exp),
                              waits=[("dve", me_ok)], incs=[("act", 1)])
                        ex_ok = p.n("act")
                        p.add("vector", lambda e: e.tensor_relu(out=rl[:, :], in_=qp[:, :]))
                        p.add("vector", lambda e, stb=stb: e.tensor_tensor(
                            out=stb[:, :], in0=ex[:, :], in1=rl[:, :], op=OP.add),
                            waits=[("act", ex_ok)] + wstg, incs=[("dve", 1)])
                        stage_ok[i] = ("dve", p.n("dve"))
                    dstm = {"q": q_sc, "k": k_sc, "v": v_sc}[kind]
                    col = (og % 4) * 512
                    spill_ok[i] = p.dma("sync", "spill", lambda e, dstm=dstm, tc=tc, col=col, stb=stb: e.dma_start(
                        out=dstm[tc * 128:(tc + 1) * 128, col:col + 512], in_=stb[:, :]),
                        waits=[stage_ok[i]], slot=i % 3)
            sc.close()

        def ph_attn_a(drain=None):
            sc = ExitStack()
            E = sc.enter_context
            S_run = E(nc.sbuf_tensor("S_run", [128, H * 132], F32))
            S_bf = E(nc.sbuf_tensor("S_bf", [128, H * 132], BF16))
            sc23 = ExitStack()
            E2 = sc23.enter_context
            Sloc = E2(nc.sbuf_tensor("Sloc", [128, H * 132], F32))
            kh = [E2(nc.sbuf_tensor(f"khA{i}", [128, TC * 128], BF16)) for i in range(2)]
            vh = [E2(nc.sbuf_tensor(f"vhA{i}", [128, TC * 128], BF16)) for i in range(2)]
            rkb = [E2(nc.sbuf_tensor(f"rkb{i}", [128, H * 132], F32)) for i in range(2)]
            tmpS = E2(nc.sbuf_tensor("tmpS", [128, H * 132], F32))
            psS = E2(nc.psum_tensor("psS", [128, 512], F32))
            psSz = E2(nc.psum_tensor("psSz", [128, 512], F32))

            ev_ok = 0
            for h in range(H):
                if drain:
                    drain()
                w = []
                if h >= 2:
                    w.append(("pe", st_mm[h - 2]))
                t1 = p.dma("sync", "kh", lambda e, h=h: e.dma_start(
                    out=kh[h % 2][:, :].rearrange("p (tc d) -> p tc d", tc=TC),
                    in_=k_sc[:, h * 128:(h + 1) * 128].rearrange("(tc p) d -> p tc d", p=128)),
                    waits=w, slot=h % 2)
                t2 = p.dma("sync", "vh", lambda e, h=h: e.dma_start(
                    out=vh[h % 2][:, :].rearrange("p (tc d) -> p tc d", tc=TC),
                    in_=v_sc[:, h * 128:(h + 1) * 128].rearrange("(tc p) d -> p tc d", p=128)),
                    slot=h % 2)
                w = [t1, t2]
                if h >= 1:
                    w.append(("dve", ev_ok))
                for mc in range(TC):
                    p.add("tensor", lambda e, h=h, mc=mc: e.matmul(
                        psS[:, 0:128], kh[h % 2][:, mc * 128:(mc + 1) * 128],
                        vh[h % 2][:, mc * 128:(mc + 1) * 128],
                        start=(mc == 0), stop=(mc == TC - 1)),
                        waits=(w if mc == 0 else []))
                    p.add("tensor", lambda e, h=h, mc=mc: e.matmul(
                        psSz[:, 0:1], kh[h % 2][:, mc * 128:(mc + 1) * 128],
                        onesb[:, 0:1],
                        start=(mc == 0), stop=(mc == TC - 1)),
                        incs=([("pe", 1)] if mc == TC - 1 else []))
                st_mm[h] = p.n("pe")
                p.add("vector", lambda e, h=h: e.tensor_copy(
                    out=Sloc[:, h * 132:h * 132 + 128], in_=psS[:, 0:128]),
                    waits=[("pe", st_mm[h])])
                p.add("vector", lambda e, h=h: e.tensor_copy(
                    out=Sloc[:, h * 132 + 128:h * 132 + 129], in_=psSz[:, 0:1]),
                    incs=[("dve", 1)])
                ev_ok = p.n("dve")

            cctok = p.dma("gpsimd", "ccw", lambda e: e.dma_start(
                out=cc_in[0, :].rearrange("(pp c) -> pp c", pp=128), in_=Sloc[:, :]),
                waits=[("dve", ev_ok)])
            p.add("gpsimd", lambda e: e.collective_compute(
                "AllGather", OP.bypass,
                replica_groups=[list(range(NCORES))],
                ins=[cc_in[:, :]], outs=[cc_out[:, :]]),
                waits=[cctok], incs=[("cc", 1)])
            cc_ok = p.n("cc")
            sin_ok = None
            for j in range(NCORES):
                w = [("cc", cc_ok)]
                if j >= 2:
                    w.append(("dve", p.n("dve")))
                ldtok = p.dma("sync", "rk", lambda e, j=j: e.dma_start(
                    out=rkb[j % 2][:, :], in_=cc_out[j, :].rearrange("(pp c) -> pp c", pp=128)),
                    waits=w, slot=j % 2)
                if j == 0:
                    p.add("vector", lambda e, j=j: e.tensor_scalar(
                        out=S_run[:, :], in0=rkb[j % 2][:, :], scalar1=premask[:, j:j + 1],
                        scalar2=None, op0=OP.mult),
                        waits=[ldtok], incs=[("dve", 1)])
                else:
                    p.add("vector", lambda e, j=j: e.tensor_scalar(
                        out=tmpS[:, :], in0=rkb[j % 2][:, :], scalar1=premask[:, j:j + 1],
                        scalar2=None, op0=OP.mult),
                        waits=[ldtok])
                    p.add("vector", lambda e: e.tensor_tensor(
                        out=S_run[:, :], in0=tmpS[:, :], in1=S_run[:, :], op=OP.add),
                        incs=[("dve", 1)])
            sin_ok = p.n("dve")
            p.add("scalar", lambda e: e.copy(out=S_bf[:, :], in_=S_run[:, :]),
                  waits=[("dve", sin_ok)], incs=[("act", 1)])
            sbf_ok = {(-1, h): p.n("act") for h in range(H)}
            sc23.close()
            wout_sb = E(nc.sbuf_tensor("wout_sb", [128, FC * D], BF16))
            o_buf = E(nc.sbuf_tensor("o_buf", [128, H * 256], BF16))
            qch = [E(nc.sbuf_tensor(f"qch{i}", [128, 256], BF16)) for i in range(2)]
            kch = [E(nc.sbuf_tensor(f"kch{i}", [128, 256], BF16)) for i in range(2)]
            vch = [E(nc.sbuf_tensor(f"vch{i}", [128, 256], BF16)) for i in range(2)]
            qfm = [E(nc.sbuf_tensor(f"qfm{i}", [128, 256], BF16)) for i in range(2)]
            kfm = [E(nc.sbuf_tensor(f"kfm{i}", [128, 256], BF16)) for i in range(2)]
            p2 = E(nc.sbuf_tensor("p2t", [128, 2 * 256], BF16))
            dr_t = E(nc.sbuf_tensor("dr_t", [1, 512], F32))
            dsb = dr_t[:, 0:256]
            rden = dr_t[:, 256:512]
            bsb = E(nc.sbuf_tensor("bsb", [128, 256], F32))
            spare = E(nc.sbuf_tensor("spareA", [128, 2], F32))
            psT2 = E(nc.psum_tensor("psT2", [128, 1024], BF16))
            psA = [E(nc.psum_tensor(f"psA{i}", [128, 512], F32)) for i in range(2)]
            psUt = E(nc.psum_tensor("psUt", [128, 512], F32))
            psUs = psUt[:, 0:128]
            psDZ = E(nc.psum_tensor("psDZ", [128, 512], F32))
            psD = psDZ[0:1, 0:256]
            psUz = psDZ[:, 256:257]
            psPt = E(nc.psum_tensor("psPt", [128, 512], F32))
            psP0 = psPt[:, 0:256]
            psN = psPt[:, 256:512]

            for ec in range(FC):
                wout_ok = p.dma("sync", "wo", lambda e, ec=ec: e.dma_start(
                    out=wout_sb[:, ec * D:(ec + 1) * D],
                    in_=woutT_sc[ec * 128:(ec + 1) * 128, :]))

            o_ok = {}
            xadd_ok = 0
            for c in range(NCH):
                for h in range(H):
                    if drain:
                        drain()
                    i = c * H + h
                    w = []
                    if i >= 2:
                        w.append(("pe", ld_reuse[i - 2]))
                    chtoks = []
                    for nmx, (dst, srcm) in (("chq", (qch, q_sc)), ("chk", (kch, k_sc)), ("chv", (vch, v_sc))):
                        chtoks.append(p.dma("sync", nmx, lambda e, dst=dst, srcm=srcm, c=c, h=h, i=i: e.dma_start(
                            out=dst[i % 2][:, :].rearrange("p (i d) -> p i d", i=2),
                            in_=srcm[c * 256:(c + 1) * 256, h * 128:(h + 1) * 128]
                                .rearrange("(i p) d -> p i d", p=128)),
                            waits=(w if dst is qch else []), slot=i % 2))
                    w = list(chtoks)
                    if i >= 2:
                        w.append(("act", fm_ok[i - 2]))
                    for ii in range(2):
                        p.add("tensor", lambda e, i=i, ii=ii: e.transpose(
                            psT2[:, ii * 128:(ii + 1) * 128],
                            qch[i % 2][:, ii * 128:(ii + 1) * 128], idb[:, :]),
                            waits=(w if ii == 0 else []))
                    for ii in range(2):
                        p.add("tensor", lambda e, i=i, ii=ii: e.transpose(
                            psT2[:, 256 + ii * 128: 256 + (ii + 1) * 128],
                            kch[i % 2][:, ii * 128:(ii + 1) * 128], idb[:, :]),
                            incs=([("pe", 1)] if ii == 1 else []))
                    tr_ok = p.n("pe")
                    p.add("scalar", lambda e, i=i: e.copy(out=qfm[i % 2][:, :], in_=psT2[:, 0:256]),
                          waits=[("pe", tr_ok)])
                    p.add("scalar", lambda e, i=i: e.copy(out=kfm[i % 2][:, :], in_=psT2[:, 256:512]),
                          incs=[("act", 1)])
                    fm_ok[i] = p.n("act")
                    w = [("act", fm_ok[i])]
                    if i >= 1:
                        w.append(("dve", p2_ok[i - 1]))
                        w.append(("dve", onum_ok[i - 1]))
                    for mc in range(2):
                        p.add("tensor", lambda e, i=i, mc=mc: e.matmul(
                            psA[mc][:, 0:256], kfm[i % 2][:, mc * 128:(mc + 1) * 128],
                            qfm[i % 2][:, :], start=True, stop=True),
                            waits=(w if mc == 0 else []),
                            incs=([("pe", 1)] if mc == 1 else []))
                    sc_ok = p.n("pe")
                    if drain:
                        drain()
                    for mc in range(2):
                        p.add("vector", lambda e, mc=mc: e.tensor_tensor(
                            out=p2[:, mc * 256:(mc + 1) * 256],
                            in0=maska[:, mc * 256:(mc + 1) * 256], in1=psA[mc][:, 0:256], op=OP.mult),
                            waits=([("pe", sc_ok)] if mc == 0 else []),
                            incs=([("dve", 1)] if mc == 1 else []))
                    p2_ok[i] = p.n("dve")
                    w = [("dve", p2_ok[i]), ("act", sbf_ok[(c - 1, h)])]
                    if i >= 1:
                        w.append(("dve", den_ok[i - 1]))
                        w.append(("dve", supd_ok[i - 1]))
                    p.add("tensor", lambda e, i=i: e.matmul(
                        psD, onesb[:, 0:1], p2[:, 0:256], start=True, stop=False), waits=w)
                    p.add("tensor", lambda e, i=i: e.matmul(
                        psD, onesb[:, 0:1], p2[:, 256:512], start=False, stop=False))
                    p.add("tensor", lambda e, i=i, h=h: e.matmul(
                        psD, S_bf[:, h * 132 + 128:h * 132 + 129], qfm[i % 2][:, :],
                        start=False, stop=True), incs=[("pe", 1)])
                    dmm_ok = p.n("pe")
                    w = []
                    if i >= 1:
                        w.append(("dve", onum_ok[i - 1]))
                    if h == 0 and c >= 1:
                        w.append(("dve", proj_ok[FC - 1]))
                    p.add("tensor", lambda e, i=i, h=h: e.matmul(
                        psN[:, :], S_bf[:, h * 132:h * 132 + 128], qfm[i % 2][:, :],
                        start=True, stop=False), waits=w)
                    for mc in range(2):
                        p.add("tensor", lambda e, i=i, mc=mc: e.matmul(
                            psN[:, :], vch[i % 2][:, mc * 128:(mc + 1) * 128],
                            p2[:, mc * 256:(mc + 1) * 256], start=False, stop=(mc == 1)),
                            incs=([("pe", 1)] if mc == 1 else []))
                    nmm_ok = p.n("pe")
                    if drain:
                        drain()
                    p.add("vector", lambda e: e.tensor_scalar(
                        out=dsb, in0=psD, scalar1=1e-6, scalar2=None, op0=OP.max),
                        waits=[("pe", dmm_ok)], incs=[("dve", 1)])
                    den_ok[i] = p.n("dve")
                    p.add("vector", lambda e: e.tensor_copy(out=spare[:, :], in_=onesb[:, :]))
                    p.add("vector", lambda e: e.reciprocal(out=rden, in_=dsb),
                          incs=[("dve", 1)])
                    rd_ok = p.n("dve")
                    w = [("dve", rd_ok)]
                    p.add("tensor", lambda e: e.matmul(
                        psA[0][:, 256:512], onesrf[0:1, :], rden, start=True, stop=True),
                        waits=w, incs=[("pe", 1)])
                    bmm_ok = p.n("pe")
                    wb = [("pe", bmm_ok)]
                    if i >= 1:
                        wb.append(("dve", onum_ok[i - 1]))
                    p.add("scalar", lambda e: e.copy(out=bsb[:, :], in_=psA[0][:, 256:512]),
                          waits=wb, incs=[("act", 1)])
                    bsb_ok[i] = ("act", p.n("act"))
                    if drain:
                        drain()
                    p.add("vector", lambda e, h=h: e.tensor_tensor(
                        out=o_buf[:, h * 256:(h + 1) * 256], in0=bsb[:, :], in1=psN[:, :], op=OP.mult),
                        waits=[bsb_ok[i], ("pe", nmm_ok)], incs=[("dve", 1)])
                    onum_ok[i] = p.n("dve")
                    o_ok[(c, h)] = p.n("dve")
                    w = [("dve", onum_ok[i])]
                    if i >= 1:
                        w.append(("dve", supd_ok[i - 1]))
                    for mc in range(2):
                        p.add("tensor", lambda e, i=i, mc=mc: e.matmul(
                            psUs, kch[i % 2][:, mc * 128:(mc + 1) * 128],
                            vch[i % 2][:, mc * 128:(mc + 1) * 128],
                            start=(mc == 0), stop=(mc == 1)),
                            waits=(w if mc == 0 else []))
                        p.add("tensor", lambda e, i=i, mc=mc: e.matmul(
                            psUz, kch[i % 2][:, mc * 128:(mc + 1) * 128],
                            onesb[:, 0:1], start=(mc == 0), stop=(mc == 1)),
                            incs=([("pe", 1)] if mc == 1 else []))
                    umm_ok = p.n("pe")
                    ld_reuse[i] = umm_ok
                    p.add("vector", lambda e, h=h: e.tensor_tensor(
                        out=S_run[:, h * 132:h * 132 + 128],
                        in0=S_run[:, h * 132:h * 132 + 128], in1=psUs, op=OP.add),
                        waits=[("pe", umm_ok)])
                    p.add("vector", lambda e, h=h: e.tensor_tensor(
                        out=S_run[:, h * 132 + 128:h * 132 + 129],
                        in0=S_run[:, h * 132 + 128:h * 132 + 129], in1=psUz, op=OP.add),
                        incs=[("dve", 1)])
                    supd_ok[i] = p.n("dve")
                    p.add("scalar", lambda e, h=h: e.copy(
                        out=S_bf[:, h * 132:h * 132 + 129], in_=S_run[:, h * 132:h * 132 + 129]),
                        waits=[("dve", supd_ok[i])], incs=[("act", 1)])
                    sbf_ok[(c, h)] = p.n("act")

                w_all_o = [("dve", o_ok[(c, H - 1)]), wout_ok]
                for hc in range(FC):
                    w = list(w_all_o) if hc == 0 else []
                    if hc >= 1:
                        w.append(("dve", proj_ok[hc - 1]))
                    for ec in range(FC):
                        p.add("tensor", lambda e, hc=hc, ec=ec: e.matmul(
                            psP0, wout_sb[:, ec * D + hc * 128: ec * D + (hc + 1) * 128],
                            o_buf[:, ec * 256:(ec + 1) * 256],
                            start=(ec == 0), stop=(ec == FC - 1)),
                            waits=(w if ec == 0 else []),
                            incs=([("pe", 1)] if ec == FC - 1 else []))
                    pmm_ok = p.n("pe")
                    p.add("vector", lambda e, hc=hc, c=c: e.tensor_tensor(
                        out=X[:, hc * T + c * 256: hc * T + (c + 1) * 256],
                        in0=X[:, hc * T + c * 256: hc * T + (c + 1) * 256],
                        in1=psP0, op=OP.add),
                        waits=[("pe", pmm_ok)], incs=[("dve", 1)])
                    proj_ok[hc] = p.n("dve")
                ld_reuse[c * H + H - 1] = p.n("pe")
            if "sdbg_sc" in DEBUG_SCRATCH:
                p.dma("gpsimd", "dbg", lambda e: e.dma_start(out=sdbg_sc, in_=S_run[:, :]),
                      waits=[("dve", p.n("dve")), ("act", p.n("act")), ("pe", p.n("pe"))])
                p.dma("gpsimd", "dbg", lambda e: e.dma_start(out=odbg_sc, in_=o_buf[:, :]))
            sc.close()

        st_mm = {}
        fm_ok = {}
        p2_ok = {}
        den_ok = {}
        onum_ok = {}
        bsb_ok = {}
        supd_ok = {}
        ld_reuse = {}
        proj_ok = {}

        class Decoder:
            def __init__(self):
                self.sc = ExitStack()
                E = self.sc.enter_context
                self.pk = [E(nc.sbuf_tensor(f"pk{i}", [128, 1024], I32)) for i in range(2)]
                self.m32 = [E(nc.sbuf_tensor(f"m32{i}", [128, 1024], I32)) for i in range(2)]
                self.s32 = [E(nc.sbuf_tensor(f"s32{i}", [128, 1024], I32)) for i in range(2)]
                self.c32 = [E(nc.sbuf_tensor(f"c32{i}", [128, 1024], I32)) for i in range(2)]
                self.dec = [E(nc.sbuf_tensor(f"dec{i}", [128, 2048], BF16)) for i in range(2)]
                self.scf = [E(nc.sbuf_tensor(f"scf{i}", [128, 64], F32)) for i in range(2)]
                self.scb = [E(nc.sbuf_tensor(f"scb{i}", [128, 64], BF16)) for i in range(2)]
                self.wstg = [E(nc.sbuf_tensor(f"dstg{i}", [128, 512], BF16)) for i in range(2)]
                self.psDC = [E(nc.psum_tensor(f"psDC{i}", [128, 1024], BF16)) for i in range(2)]
                self.n = 48
                self.e1 = 0
                self.e2 = 0
                self.e3 = 0
                self.u_ok = {}
                self.s_ok = {}
                self.c_ok = {}
                self.or_ok = {}
                self.scale_ok = {}
                self.scv = {}
                self.cvt_done = {}
                self.dcvt = {}
                self.dscale_prev = {}
                self.trall = {}
                self.ev_done = {}
                self.wr_done = {}
                self.wr_unit = {}

            def stage1(self):
                i = self.e1
                if i >= self.n:
                    return
                w = [("dve", self.s_ok.get(i - 2, 0))]
                tok = p.dma("sync", "pk", lambda e, i=i: e.dma_start(
                    out=self.pk[i % 2][:, :], in_=mypk_in[i * 128:(i + 1) * 128, :]),
                    waits=[x for x in w if x[1] > 0], slot=i % 2)
                slot = i % 2
                w2 = [("act", self.cvt_done.get(i - 2, 0))]
                sl = p.dma("sync", "scg", lambda e, i=i, slot=slot: e.dma_start(
                    out=self.scf[slot][:, 0:64], in_=mysc_in[i * 128:(i + 1) * 128, :]),
                    waits=[x for x in w2 if x[1] > 0], slot=slot)
                w3 = [sl, ("dve", self.scale_ok.get(i - 2, 0))]
                p.add("scalar", lambda e, slot=slot: e.copy(
                    out=self.scb[slot][:, 0:64], in_=self.scf[slot][:, 0:64]),
                    waits=[x for x in w3 if not (isinstance(x[1], int) and x[1] <= 0)],
                    incs=[("act", 1)])
                self.cvt_done[i] = p.n("act")
                self.scv[i] = (self.scb[slot][:, 0:64], ("act", self.cvt_done[i]))
                p.add("gpsimd", lambda e, i=i: e.tensor_scalar(
                    out=self.pk[i % 2][:, :], in0=self.pk[i % 2][:, :],
                    scalar1=4097, scalar2=None, op0=OP.mult),
                    waits=[tok], incs=[("pool", 1)])
                self.u_ok[i] = p.n("pool")
                w = [("pool", self.u_ok[i])]
                p.add("vector", lambda e, i=i: e.tensor_scalar(
                    out=self.m32[i % 2][:, :], in0=self.pk[i % 2][:, :],
                    scalar1=0x00070007, scalar2=None, op0=OP.bitwise_and),
                    waits=[x for x in w if x[1] > 0])
                p.add("vector", lambda e, i=i: e.tensor_scalar(
                    out=self.s32[i % 2][:, :], in0=self.pk[i % 2][:, :],
                    scalar1=0x00080008, scalar2=12,
                    op0=OP.bitwise_and, op1=OP.logical_shift_left),
                    incs=[("dve", 1)])
                self.s_ok[i] = p.n("dve")
                self.e1 = i + 1

            def stage2(self):
                i = self.e2
                if i >= self.n or i >= self.e1:
                    return
                p.add("vector", lambda e, i=i: e.tensor_scalar(
                    out=self.c32[i % 2][:, :].bitcast(I16), in0=self.m32[i % 2][:, :].bitcast(I16),
                    scalar1=0x40, scalar2=0x3F00, op0=OP.mult, op1=OP.add),
                    incs=[("dve", 1)])
                self.c_ok[i] = p.n("dve")
                self.e2 = i + 1

            def stage3(self):
                i = self.e3
                if i >= self.n or i >= self.e2:
                    return
                p.add("vector", lambda e, i=i: e.tensor_scalar(
                    out=self.m32[i % 2][:, :].bitcast(BF16), in0=self.c32[i % 2][:, :].bitcast(BF16),
                    scalar1=-1.0, scalar2=0.0, op0=OP.add, op1=OP.min))
                p.add("vector", lambda e, i=i: e.tensor_tensor(
                    out=self.c32[i % 2][:, :].bitcast(BF16), in0=self.m32[i % 2][:, :].bitcast(BF16),
                    in1=self.c32[i % 2][:, :].bitcast(BF16), op=OP.add))
                p.add("vector", lambda e, i=i: e.tensor_tensor(
                    out=self.c32[i % 2][:, :], in0=self.s32[i % 2][:, :], in1=self.c32[i % 2][:, :],
                    op=OP.bitwise_or), incs=[("dve", 1)])
                self.or_ok[i] = p.n("dve")
                scsl, stok = self.scv[i]
                sc3 = scsl.unsqueeze(2).broadcast_to([128, 64, 32])
                w = [stok, ("pe", self.trall.get(i - 2, 0))]
                p.add("vector", lambda e, i=i, sc3=sc3: e.tensor_tensor(
                    out=self.dec[i % 2][:, :].rearrange("p (b k) -> p b k", b=64),
                    in0=sc3, in1=self.c32[i % 2][:, :].bitcast(BF16).rearrange("p (b k) -> p b k", b=64),
                    op=OP.mult),
                    waits=[x for x in w if not (isinstance(x[1], int) and x[1] <= 0)],
                    incs=[("dve", 1)])
                self.scale_ok[i] = p.n("dve")
                dstt = ccw_in
                rowblk = i
                for g in range(4):
                    lv = i * 4 + g
                    w = [("dve", self.scale_ok[i]), CONSTS_TOK] if g == 0 else []
                    if lv >= 2:
                        w.append(("act", self.ev_done[lv - 2]))
                    for j in range(4):
                        kc = g * 4 + j
                        p.add("tensor", lambda e, i=i, g=g, j=j, kc=kc: e.transpose(
                            self.psDC[g % 2][:, j * 128:(j + 1) * 128],
                            self.dec[i % 2][:, kc * 128:(kc + 1) * 128], idb[:, :]),
                            waits=(w if j == 0 else []),
                            incs=([("pe", 1)] if j == 3 else []))
                    tr_ok = p.n("pe")
                    w2 = [("pe", tr_ok)]
                    if lv >= 2:
                        w2.append(self.wr_done[lv - 2])
                    p.add("scalar", lambda e, g=g: e.copy(
                        out=self.wstg[g % 2][:, :], in_=self.psDC[g % 2][:, 0:512]),
                        waits=w2, incs=[("act", 1)])
                    self.ev_done[lv] = p.n("act")
                    self.wr_done[lv] = p.dma("scalar", "dw",
                        lambda e, dstt=dstt, rowblk=rowblk, g=g: e.dma_start(
                            out=dstt[rowblk * 128:(rowblk + 1) * 128, g * 512:(g + 1) * 512],
                            in_=self.wstg[g % 2][:, :]),
                        waits=[("act", self.ev_done[lv])], slot=lv % 2)
                self.trall[i] = p.n("pe")
                self.e3 = i + 1

            def drain(self, k):
                for _ in range(k):
                    self.stage3()
                    self.stage2()
                    self.stage1()

            def flush(self):
                while self.e3 < self.n:
                    self.stage1()
                    self.stage2()
                    self.stage3()
                nmax = self.n * 4 - 1
                w = [self.wr_done[nmax], self.wr_done[nmax - 1]]
                p.add("gpsimd", lambda e: e.collective_compute(
                    "AllGather", OP.bypass,
                    replica_groups=[list(range(NCORES))],
                    ins=[ccw_in], outs=[ccd_out]),
                    waits=w, incs=[("cc", 1)])
                self.cc_tok = ("cc", p.n("cc"))
                self.sc.close()


        def ph_attn_b(Xln):
            sc = ExitStack()
            E = sc.enter_context
            gsc_sb = E(nc.sbuf_tensor("gsc_sb", [128, 256], BF16))
            p1s = E(nc.sbuf_tensor("p1s", [128, 2 * 256], BF16))
            p1n = E(nc.sbuf_tensor("p1n", [128, 2 * 256], BF16))
            p2g = E(nc.sbuf_tensor("p2g", [128, 768], BF16))
            nmx = E(nc.sbuf_tensor("nmx", [128, 2], F32))
            denb = E(nc.sbuf_tensor("denb", [128, 2], F32))
            rdb = E(nc.sbuf_tensor("rdb", [128, 2], F32))
            abdbg = E(nc.sbuf_tensor("abdbg", [128, 2048], F32))
            psS1 = [E(nc.psum_tensor(f"psS1{i}", [128, 512], F32)) for i in range(2)]
            psP2 = E(nc.psum_tensor("psP2", [128, 1024], BF16))
            psO = E(nc.psum_tensor("psO", [128, 512], F32))

            gsc_ok = 0
            exp_ok = {}
            evac_ok = 0
            xadd_ok = 0
            prev_sc = 0
            for s in range(NSEG):
                for h in range(H):
                    i = s * H + h
                    par = h % 2
                    base = h * T + s * 512 + par

                    def gv(lo, cnt, base=base):
                        st = base + 2 * lo
                        return Xln[:, st: st + 2 * cnt - 1: 2]

                    w = [("pe", prev_sc)] if i >= 1 else []
                    p.add("vector", lambda e, gv=gv: e.tensor_scalar(
                        out=gsc_sb[:, :], in0=gv(0, 256), scalar1=BSCALE, scalar2=None, op0=OP.mult),
                        waits=w, incs=[("dve", 1)])
                    gsc_ok = p.n("dve")
                    w = [("dve", gsc_ok)]
                    if i >= 1:
                        w.append(("act", exp_ok[i - 1]))
                    for lc in range(2):
                        p.add("tensor", lambda e, lc=lc, gv=gv: e.matmul(
                            psS1[lc][:, 0:256], gsc_sb[:, lc * 128:(lc + 1) * 128],
                            gv(0, 256), start=True, stop=True),
                            waits=(w if lc == 0 else []),
                            incs=([("pe", 1)] if lc == 1 else []))
                    sc_ok = p.n("pe")
                    prev_sc = sc_ok
                    for lc in range(2):
                        p.add("vector", lambda e, lc=lc: e.reduce_max(
                            out=nmx[:, lc:lc + 1], in_=psS1[lc][:, 0:256], axis=AX.X, negate=True),
                            waits=([("pe", sc_ok)] if lc == 0 else []),
                            incs=([("dve", 1)] if lc == 1 else []))
                    nmx_ok = p.n("dve")
                    for lc in range(2):
                        p.add("scalar", lambda e, lc=lc: e.activation(
                            out=p1s[:, lc * 256:(lc + 1) * 256], in_=psS1[lc][:, 0:256],
                            func=AF.Exp, bias=nmx[:, lc:lc + 1]),
                            waits=([("dve", nmx_ok)] if lc == 0 else []),
                            incs=([("act", 1)] if lc == 1 else []))
                    exp_ok[i] = p.n("act")
                    for lc in range(2):
                        p.add("vector", lambda e, lc=lc: e.reduce_sum(
                            out=denb[:, lc:lc + 1], in_=p1s[:, lc * 256:(lc + 1) * 256], axis=AX.X),
                            waits=([("act", exp_ok[i])] if lc == 0 else []))
                    p.add("vector", lambda e: e.reciprocal(out=rdb[:, :], in_=denb[:, :]))
                    p.add("vector", lambda e: e.tensor_copy(out=nmx[:, :], in_=denb[:, :]))
                    p.add("vector", lambda e: e.tensor_copy(out=nmx[:, :], in_=denb[:, :]))
                    for lc in range(2):
                        p.add("vector", lambda e, lc=lc: e.tensor_scalar(
                            out=p1n[:, lc * 256:(lc + 1) * 256], in0=p1s[:, lc * 256:(lc + 1) * 256],
                            scalar1=rdb[:, lc:lc + 1], scalar2=None, op0=OP.mult),
                            incs=([("dve", 1)] if lc == 1 else []))
                    p1n_ok = p.n("dve")
                    w = [("dve", p1n_ok)]
                    if i >= 1:
                        w.append(("act", evac_ok))
                    for mc in range(2):
                        for lc in range(2):
                            p.add("tensor", lambda e, mc=mc, lc=lc: e.transpose(
                                psP2[:, mc * 256 + lc * 128: mc * 256 + (lc + 1) * 128],
                                p1n[:, lc * 256 + mc * 128: lc * 256 + (mc + 1) * 128], idb[:, :]),
                                waits=(w if (mc == 0 and lc == 0) else []))
                    for mc in range(2):
                        p.add("tensor", lambda e, mc=mc, gv=gv: e.transpose(
                            psP2[:, 512 + mc * 128: 512 + (mc + 1) * 128],
                            gv(mc * 128, 128), idb[:, :]),
                            incs=([("pe", 1)] if mc == 1 else []))
                    tr_ok = p.n("pe")
                    w2 = [("pe", tr_ok)]
                    if i >= 1:
                        w2.append(("dve", xadd_ok))
                    p.add("scalar", lambda e: e.copy(out=p2g[:, :], in_=psP2[:, 0:768]),
                          waits=w2, incs=[("act", 1)])
                    evac_ok = p.n("act")
                    for mc in range(2):
                        wav = [("act", evac_ok)]
                        if i >= 1:
                            wav.append(("dve", xadd_ok))
                        p.add("tensor", lambda e, mc=mc: e.matmul(
                            psO[:, 0:256], p2g[:, 512 + mc * 128: 512 + (mc + 1) * 128],
                            p2g[:, mc * 256:(mc + 1) * 256], start=(mc == 0), stop=(mc == 1)),
                            waits=(wav if mc == 0 else []),
                            incs=([("pe", 1)] if mc == 1 else []))
                    av_ok = p.n("pe")
                    if i == 0 and "abdbg_sc" in DEBUG_SCRATCH:
                        p.add("vector", lambda e: e.tensor_copy(out=abdbg[:, 0:256], in_=gsc_sb[:, :]),
                              waits=[("pe", av_ok)])
                        p.add("vector", lambda e: e.tensor_copy(out=abdbg[:, 256:512], in_=p1n[:, 0:256]))
                        p.add("vector", lambda e: e.tensor_copy(out=abdbg[:, 512:768], in_=p1n[:, 256:512]))
                        p.add("vector", lambda e: e.tensor_copy(out=abdbg[:, 768:770], in_=denb[:, :]))
                        p.add("vector", lambda e: e.tensor_copy(out=abdbg[:, 770:772], in_=rdb[:, :]))
                        p.add("vector", lambda e: e.tensor_copy(out=abdbg[:, 772:1540], in_=p2g[:, :]))
                        p.add("vector", lambda e: e.tensor_copy(out=abdbg[:, 1540:1796], in_=psO[:, 0:256]),
                              incs=[("dve", 1)])
                        p.dma("gpsimd", "dbg", lambda e: e.dma_start(out=abdbg_sc[:, :], in_=abdbg[:, :]),
                              waits=[("dve", p.n("dve"))])
                    p.add("vector", lambda e, base=base: e.tensor_tensor(
                        out=Xs_strided(base), in0=Xs_strided(base), in1=psO[:, 0:256], op=OP.add),
                        waits=[("pe", av_ok)], incs=[("dve", 1)])
                    xadd_ok = p.n("dve")
            sc.close()

        def Xs_strided(base):
            return X[:, base: base + 511: 2]

        def ph_out():
            sc = ExitStack()
            E = sc.enter_context
            yst = [E(nc.sbuf_tensor(f"yst{i}", [128, D], F32)) for i in range(2)]
            psY = [E(nc.psum_tensor(f"psY{i}", [128, 512], F32)) for i in range(2)]
            evy = {}
            wry = {}
            for tc in range(TC):
                for g in range(4):
                    lin = tc * 4 + g
                    w = []
                    if lin >= 2:
                        w.append(("dve", evy[lin - 2]))
                    if g == 0 and tc >= 2:
                        w.append(wry[tc - 2])
                    for j in range(4):
                        fc = g * 4 + j
                        p.add("tensor", lambda e, tc=tc, g=g, j=j, fc=fc: e.transpose(
                            psY[g % 2][:, j * 128:(j + 1) * 128],
                            X[:, fc * T + tc * 128: fc * T + tc * 128 + 128], idf[:, :]),
                            waits=(w if j == 0 else []),
                            incs=([("pe", 1)] if j == 3 else []))
                    pe_ok = p.n("pe")
                    p.add("vector", lambda e, tc=tc, g=g: e.tensor_copy(
                        out=yst[tc % 2][:, :].rearrange("p (g j f) -> p g j f", g=4, j=4)[:, g, :, :],
                        in_=psY[g % 2][:, :].rearrange("p (j f) -> p j f", j=4)),
                        waits=[("pe", pe_ok)], incs=[("dve", 1)])
                    evy[lin] = p.n("dve")
                wry[tc] = p.dma("gpsimd", "yout", lambda e, tc=tc: e.dma_start(
                    out=y_out[tc * 128:(tc + 1) * 128, :], in_=yst[tc % 2][:, :]),
                    waits=[("dve", evy[tc * 4 + 3])], slot=tc % 2)
            sc.close()

        def ph_ffn(Xln, tagn):
            sc = ExitStack()
            E = sc.enter_context
            wgb = [E(nc.sbuf_tensor(f"wgb{tagn}{i}", [128, 2048], BF16)) for i in range(2)]
            wub = [E(nc.sbuf_tensor(f"wub{tagn}{i}", [128, 2048], BF16)) for i in range(2)]
            wdb = [E(nc.sbuf_tensor(f"wdb{tagn}{i}", [128, 2048], BF16)) for i in range(2)]
            inter = E(nc.sbuf_tensor(f"inter{tagn}", [128, 16 * 1024], BF16))
            sqb = E(nc.sbuf_tensor(f"sqb{tagn}", [128, 1024], F32))
            cgb = E(nc.sbuf_tensor(f"cgb{tagn}", [128, 1024], F32))
            sgb = E(nc.sbuf_tensor(f"sgb{tagn}", [128, 1024], F32))
            glb = E(nc.sbuf_tensor(f"glb{tagn}", [128, 1024], BF16))
            usb = E(nc.sbuf_tensor(f"usb{tagn}", [128, 1024], BF16))
            psG = E(nc.psum_tensor(f"psG{tagn}", [128, 1024], F32))
            psUp = E(nc.psum_tensor(f"psU{tagn}", [128, 1024], F32))
            psDn = [E(nc.psum_tensor(f"psDn{tagn}{i}", [128, 512], F32)) for i in range(2)]

            gld = {}
            uld = {}
            dld = {}
            gmm = {}
            umm = {}
            inter_ok = {}
            gl_ok = {}
            ue_ok = {}
            dadd = {}
            for ng in range(8):
                for ncx in range(16):
                    nb = ng * 16 + ncx
                    w = [("pe", gmm.get(nb - 2, 0))]
                    if nb == 0:
                        w.append(dec_obj.cc_tok)
                    gld[nb] = p.dma("sync", "wgb", lambda e, nb=nb: e.dma_start(
                        out=wgb[nb % 2][:, :], in_=ccd_out[nb * 128:(nb + 1) * 128, :]),
                        waits=[x for x in w if not (isinstance(x[1], int) and x[1] <= 0)], slot=nb % 2)
                    w = [("pe", umm.get(nb - 2, 0))]
                    uld[nb] = p.dma("sync", "wub", lambda e, nb=nb: e.dma_start(
                        out=wub[nb % 2][:, :], in_=ccd_out[(128 + nb) * 128:(129 + nb) * 128, :]),
                        waits=[x for x in w if x[1] > 0], slot=nb % 2)
                    w = [gld[nb]]
                    if nb >= 1:
                        w.append(("dve", gl_ok[nb - 1]))
                    for kc in range(16):
                        for half in range(2):
                            p.add("tensor", lambda e, nb=nb, kc=kc, half=half: e.matmul(
                                psG[:, half * 512:(half + 1) * 512],
                                wgb[nb % 2][:, kc * 128:(kc + 1) * 128],
                                Xln[:, kc * T + half * 512: kc * T + (half + 1) * 512],
                                start=(kc == 0), stop=(kc == 15)),
                                waits=(w if kc == 0 and half == 0 else []),
                                incs=([("pe", 1)] if kc == 15 and half == 1 else []))
                    gmm[nb] = p.n("pe")
                    w = [uld[nb]]
                    if nb >= 1:
                        w.append(("act", ue_ok[nb - 1]))
                    for kc in range(16):
                        for half in range(2):
                            p.add("tensor", lambda e, nb=nb, kc=kc, half=half: e.matmul(
                                psUp[:, half * 512:(half + 1) * 512],
                                wub[nb % 2][:, kc * 128:(kc + 1) * 128],
                                Xln[:, kc * T + half * 512: kc * T + (half + 1) * 512],
                                start=(kc == 0), stop=(kc == 15)),
                                waits=(w if kc == 0 and half == 0 else []),
                                incs=([("pe", 1)] if kc == 15 and half == 1 else []))
                    umm[nb] = p.n("pe")
                    w = [("pe", gmm[nb])]
                    if nb >= 1:
                        w.append(("dve", cg_prev[nb - 1]))
                    p.add("scalar", lambda e: e.activation(
                        out=sqb[:, :], in_=psG[:, :], func=AF.Square),
                        waits=w, incs=[("act", 1)])
                    sq_ok = p.n("act")
                    p.add("vector", lambda e: e.tensor_scalar(
                        out=cgb[:, :], in0=sqb[:, :], scalar1=GELU_A, scalar2=1.0,
                        op0=OP.mult, op1=OP.add),
                        waits=[("act", sq_ok)])
                    p.add("vector", lambda e: e.tensor_tensor(
                        out=cgb[:, :], in0=cgb[:, :], in1=psG[:, :], op=OP.mult),
                        incs=[("dve", 1)])
                    cg_ok = p.n("dve")
                    cg_prev[nb] = cg_ok
                    w = [("dve", cg_ok)]
                    if nb >= 1:
                        w.append(("dve", gl_ok[nb - 1]))
                    p.add("scalar", lambda e: e.activation(
                        out=sgb[:, :], in_=cgb[:, :], func=AF.Sigmoid, scale=np.float32(GELU_S)),
                        waits=w, incs=[("act", 1)])
                    sg_ok = p.n("act")
                    w = [("pe", umm[nb])]
                    if nb >= 1:
                        w.append(("pool", inter_ok[nb - 1]))
                    p.add("scalar", lambda e: e.copy(out=usb[:, :], in_=psUp[:, :]),
                          waits=w, incs=[("act", 1)])
                    ue_ok[nb] = p.n("act")
                    w = [("act", sg_ok)]
                    if nb >= 1:
                        w.append(("pool", inter_ok[nb - 1]))
                    p.add("vector", lambda e: e.tensor_tensor(
                        out=glb[:, :], in0=sgb[:, :], in1=psG[:, :], op=OP.mult),
                        waits=w, incs=[("dve", 1)])
                    gl_ok[nb] = p.n("dve")
                    w = [("act", ue_ok[nb]), ("dve", gl_ok[nb])]
                    if ng >= 1 and ncx == 0:
                        w.append(("pe", dmmprev[(ng - 1, 15)]))
                    p.add("gpsimd", lambda e, ncx=ncx: e.tensor_tensor(
                        out=inter[:, ncx * 1024:(ncx + 1) * 1024], in0=usb[:, :], in1=glb[:, :],
                        op=OP.mult),
                        waits=w, incs=[("pool", 1)])
                    inter_ok[nb] = p.n("pool")
                for hc in range(16):
                    if hc >= 2:
                        w = [("pe", dmmprev[(ng, hc - 2)])]
                    elif ng >= 1:
                        w = [("pe", dmmprev[(ng - 1, 14 + hc)])]
                    else:
                        w = []
                    dld[(ng, hc)] = p.dma("sync", "wdb", lambda e, ng=ng, hc=hc: e.dma_start(
                        out=wdb[hc % 2][:, :],
                        in_=ccd_out[(256 + hc * 8 + ng) * 128:(256 + hc * 8 + ng + 1) * 128, :]),
                        waits=w, slot=hc % 2)
                    w = [dld[(ng, hc)], ("pool", inter_ok[ng * 16 + 15])]
                    for th in range(2):
                        wth = list(w) if th == 0 else []
                        idx = hc * 2 + th
                        if idx >= 2:
                            wth.append(("dve", dadd[(ng, idx - 2)]))
                        elif ng >= 1:
                            wth.append(("dve", dadd[(ng - 1, 30 + idx)]))
                        for ncx in range(16):
                            p.add("tensor", lambda e, ng=ng, hc=hc, th=th, ncx=ncx: e.matmul(
                                psDn[(hc * 2 + th) % 2][:, :],
                                wdb[hc % 2][:, ncx * 128:(ncx + 1) * 128],
                                inter[:, ncx * 1024 + th * 512: ncx * 1024 + (th + 1) * 512],
                                start=(ncx == 0), stop=(ncx == 15)),
                                waits=(wth if ncx == 0 else []),
                                incs=([("pe", 1)] if ncx == 15 else []))
                        dm_ok = p.n("pe")
                        p.add("vector", lambda e, hc=hc, th=th, ng=ng, idx=idx: e.tensor_tensor(
                            out=X[:, hc * T + th * 512: hc * T + (th + 1) * 512],
                            in0=X[:, hc * T + th * 512: hc * T + (th + 1) * 512],
                            in1=psDn[idx % 2][:, :], op=OP.add),
                            waits=[("pe", dm_ok)], incs=[("dve", 1)])
                        dadd[(ng, idx)] = p.n("dve")
                    dmmprev[(ng, hc)] = p.n("pe")
            sc.close()
        dmmprev = {}
        cg_prev = {}

        def ph_attn_b(Xln):
            sc = ExitStack()
            E = sc.enter_context
            gsc_sb = E(nc.sbuf_tensor("gsc_sb", [128, 256], BF16))
            p1s = E(nc.sbuf_tensor("p1s", [128, 2 * 256], BF16))
            p1n = E(nc.sbuf_tensor("p1n", [128, 2 * 256], BF16))
            p2g = E(nc.sbuf_tensor("p2g", [128, 768], BF16))
            nmx = E(nc.sbuf_tensor("nmx", [128, 2], F32))
            denb = E(nc.sbuf_tensor("denb", [128, 2], F32))
            rdb = E(nc.sbuf_tensor("rdb", [128, 2], F32))
            abdbg = E(nc.sbuf_tensor("abdbg", [128, 2048], F32))
            psS1 = [E(nc.psum_tensor(f"psS1{i}", [128, 512], F32)) for i in range(2)]
            psP2 = E(nc.psum_tensor("psP2", [128, 1024], BF16))
            psO = E(nc.psum_tensor("psO", [128, 512], F32))

            gsc_ok = 0
            exp_ok = {}
            evac_ok = 0
            xadd_ok = 0
            prev_sc = 0
            for s in range(NSEG):
                for h in range(H):
                    i = s * H + h
                    par = h % 2
                    base = h * T + s * 512 + par

                    def gv(lo, cnt, base=base):
                        st = base + 2 * lo
                        return Xln[:, st: st + 2 * cnt - 1: 2]

                    w = [("pe", prev_sc)] if i >= 1 else []
                    p.add("vector", lambda e, gv=gv: e.tensor_scalar(
                        out=gsc_sb[:, :], in0=gv(0, 256), scalar1=BSCALE, scalar2=None, op0=OP.mult),
                        waits=w, incs=[("dve", 1)])
                    gsc_ok = p.n("dve")
                    w = [("dve", gsc_ok)]
                    if i >= 1:
                        w.append(("act", exp_ok[i - 1]))
                    for lc in range(2):
                        p.add("tensor", lambda e, lc=lc, gv=gv: e.matmul(
                            psS1[lc][:, 0:256], gsc_sb[:, lc * 128:(lc + 1) * 128],
                            gv(0, 256), start=True, stop=True),
                            waits=(w if lc == 0 else []),
                            incs=([("pe", 1)] if lc == 1 else []))
                    sc_ok = p.n("pe")
                    prev_sc = sc_ok
                    for lc in range(2):
                        p.add("vector", lambda e, lc=lc: e.reduce_max(
                            out=nmx[:, lc:lc + 1], in_=psS1[lc][:, 0:256], axis=AX.X, negate=True),
                            waits=([("pe", sc_ok)] if lc == 0 else []),
                            incs=([("dve", 1)] if lc == 1 else []))
                    nmx_ok = p.n("dve")
                    for lc in range(2):
                        p.add("scalar", lambda e, lc=lc: e.activation(
                            out=p1s[:, lc * 256:(lc + 1) * 256], in_=psS1[lc][:, 0:256],
                            func=AF.Exp, bias=nmx[:, lc:lc + 1]),
                            waits=([("dve", nmx_ok)] if lc == 0 else []),
                            incs=([("act", 1)] if lc == 1 else []))
                    exp_ok[i] = p.n("act")
                    for lc in range(2):
                        p.add("vector", lambda e, lc=lc: e.reduce_sum(
                            out=denb[:, lc:lc + 1], in_=p1s[:, lc * 256:(lc + 1) * 256], axis=AX.X),
                            waits=([("act", exp_ok[i])] if lc == 0 else []))
                    p.add("vector", lambda e: e.reciprocal(out=rdb[:, :], in_=denb[:, :]))
                    p.add("vector", lambda e: e.tensor_copy(out=nmx[:, :], in_=denb[:, :]))
                    p.add("vector", lambda e: e.tensor_copy(out=nmx[:, :], in_=denb[:, :]))
                    for lc in range(2):
                        p.add("vector", lambda e, lc=lc: e.tensor_scalar(
                            out=p1n[:, lc * 256:(lc + 1) * 256], in0=p1s[:, lc * 256:(lc + 1) * 256],
                            scalar1=rdb[:, lc:lc + 1], scalar2=None, op0=OP.mult),
                            incs=([("dve", 1)] if lc == 1 else []))
                    p1n_ok = p.n("dve")
                    w = [("dve", p1n_ok)]
                    if i >= 1:
                        w.append(("act", evac_ok))
                    for mc in range(2):
                        for lc in range(2):
                            p.add("tensor", lambda e, mc=mc, lc=lc: e.transpose(
                                psP2[:, mc * 256 + lc * 128: mc * 256 + (lc + 1) * 128],
                                p1n[:, lc * 256 + mc * 128: lc * 256 + (mc + 1) * 128], idb[:, :]),
                                waits=(w if (mc == 0 and lc == 0) else []))
                    for mc in range(2):
                        p.add("tensor", lambda e, mc=mc, gv=gv: e.transpose(
                            psP2[:, 512 + mc * 128: 512 + (mc + 1) * 128],
                            gv(mc * 128, 128), idb[:, :]),
                            incs=([("pe", 1)] if mc == 1 else []))
                    tr_ok = p.n("pe")
                    w2 = [("pe", tr_ok)]
                    if i >= 1:
                        w2.append(("dve", xadd_ok))
                    p.add("scalar", lambda e: e.copy(out=p2g[:, :], in_=psP2[:, 0:768]),
                          waits=w2, incs=[("act", 1)])
                    evac_ok = p.n("act")
                    for mc in range(2):
                        wav = [("act", evac_ok)]
                        if i >= 1:
                            wav.append(("dve", xadd_ok))
                        p.add("tensor", lambda e, mc=mc: e.matmul(
                            psO[:, 0:256], p2g[:, 512 + mc * 128: 512 + (mc + 1) * 128],
                            p2g[:, mc * 256:(mc + 1) * 256], start=(mc == 0), stop=(mc == 1)),
                            waits=(wav if mc == 0 else []),
                            incs=([("pe", 1)] if mc == 1 else []))
                    av_ok = p.n("pe")
                    if i == 0 and "abdbg_sc" in DEBUG_SCRATCH:
                        p.add("vector", lambda e: e.tensor_copy(out=abdbg[:, 0:256], in_=gsc_sb[:, :]),
                              waits=[("pe", av_ok)])
                        p.add("vector", lambda e: e.tensor_copy(out=abdbg[:, 256:512], in_=p1n[:, 0:256]))
                        p.add("vector", lambda e: e.tensor_copy(out=abdbg[:, 512:768], in_=p1n[:, 256:512]))
                        p.add("vector", lambda e: e.tensor_copy(out=abdbg[:, 768:770], in_=denb[:, :]))
                        p.add("vector", lambda e: e.tensor_copy(out=abdbg[:, 770:772], in_=rdb[:, :]))
                        p.add("vector", lambda e: e.tensor_copy(out=abdbg[:, 772:1540], in_=p2g[:, :]))
                        p.add("vector", lambda e: e.tensor_copy(out=abdbg[:, 1540:1796], in_=psO[:, 0:256]),
                              incs=[("dve", 1)])
                        p.dma("gpsimd", "dbg", lambda e: e.dma_start(out=abdbg_sc[:, :], in_=abdbg[:, :]),
                              waits=[("dve", p.n("dve"))])
                    p.add("vector", lambda e, base=base: e.tensor_tensor(
                        out=Xs_strided(base), in0=Xs_strided(base), in1=psO[:, 0:256], op=OP.add),
                        waits=[("pe", av_ok)], incs=[("dve", 1)])
                    xadd_ok = p.n("dve")
            sc.close()

        def Xs_strided(base):
            return X[:, base: base + 511: 2]

        def ph_out():
            sc = ExitStack()
            E = sc.enter_context
            yst = [E(nc.sbuf_tensor(f"yst{i}", [128, D], F32)) for i in range(2)]
            psY = [E(nc.psum_tensor(f"psY{i}", [128, 512], F32)) for i in range(2)]
            evy = {}
            wry = {}
            for tc in range(TC):
                for g in range(4):
                    lin = tc * 4 + g
                    w = []
                    if lin >= 2:
                        w.append(("dve", evy[lin - 2]))
                    if g == 0 and tc >= 2:
                        w.append(wry[tc - 2])
                    for j in range(4):
                        fc = g * 4 + j
                        p.add("tensor", lambda e, tc=tc, g=g, j=j, fc=fc: e.transpose(
                            psY[g % 2][:, j * 128:(j + 1) * 128],
                            X[:, fc * T + tc * 128: fc * T + tc * 128 + 128], idf[:, :]),
                            waits=(w if j == 0 else []),
                            incs=([("pe", 1)] if j == 3 else []))
                    pe_ok = p.n("pe")
                    p.add("vector", lambda e, tc=tc, g=g: e.tensor_copy(
                        out=yst[tc % 2][:, :].rearrange("p (g j f) -> p g j f", g=4, j=4)[:, g, :, :],
                        in_=psY[g % 2][:, :].rearrange("p (j f) -> p j f", j=4)),
                        waits=[("pe", pe_ok)], incs=[("dve", 1)])
                    evy[lin] = p.n("dve")
                wry[tc] = p.dma("gpsimd", "yout", lambda e, tc=tc: e.dma_start(
                    out=y_out[tc * 128:(tc + 1) * 128, :], in_=yst[tc % 2][:, :]),
                    waits=[("dve", evy[tc * 4 + 3])], slot=tc % 2)
            sc.close()

        DECODE_ON = STOP_AFTER in (None, "attnb", "ffn2", "ffn1", "decode") and "decode" not in DEBUG_SKIP
        dec_obj = Decoder() if DECODE_ON else None

        def drain2():
            if dec_obj is not None:
                dec_obj.drain(1)
        cos_sc = ExitStack()
        cose = cos_sc.enter_context(nc.sbuf_tensor("cose", [128, TC * 64], F32))
        coso = cos_sc.enter_context(nc.sbuf_tensor("coso", [128, TC * 64], F32))
        sine = cos_sc.enter_context(nc.sbuf_tensor("sine", [128, TC * 64], F32))
        sino = cos_sc.enter_context(nc.sbuf_tensor("sino", [128, TC * 64], F32))
        for dst, src in ((idb, idb_in), (idf, idf_in), (premask, premask_in), (onesb, ones_in), (lnc, lnc_in)):
            p.dma("sync", "const", lambda e, d=dst, s=src: e.dma_start(out=d[:, :], in_=s))
        for dst, src in ((cose, cose_in), (coso, coso_in), (sine, sine_in), (sino, sino_in)):
            p.dma("sync", "const", lambda e, d=dst, s=src: e.dma_start(
                out=d[:, :].rearrange("p (tc j) -> p tc j", tc=TC),
                in_=s.rearrange("(tc p) j -> p tc j", p=128)))
        p.dma("sync", "const", lambda e: e.dma_start(
            out=maska[:, :].rearrange("p (m l) -> p m l", m=2),
            in_=maska_in.rearrange("(m p) l -> p m l", p=128)))
        p.dma("sync", "const", lambda e: e.dma_start(out=ones_row[:, :], in_=onesr_in))
        CONSTS_TOK = p.dma("sync", "const", lambda e: e.dma_start(out=onesrf[:, :], in_=onesrf_in))
        Xsc = ExitStack()
        Xln1 = Xsc.enter_context(nc.sbuf_tensor("Xln1", [128, FC * T], BF16))
        ph_ln("1", Xln1, first=True, drain=drain2)
        barrier()
        if "xdbg_sc" in DEBUG_SCRATCH:
            p.dma("gpsimd", "dbg", lambda e: e.dma_start(out=xdbg_sc, in_=X[:, :]))
            p.dma("gpsimd", "dbg", lambda e: e.dma_start(out=xlndbg_sc, in_=Xln1[:, :]))
            barrier()
        if STOP_AFTER not in ("ln1",) and "wt" not in DEBUG_SKIP:
            ph_wt(drain=drain2)
            barrier()
        if STOP_AFTER not in ("ln1", "wt") and "qkv" not in DEBUG_SKIP:
            ph_qkv(Xln1, drain=drain2)
            barrier()
        Xsc.close()
        cos_sc.close()
        if STOP_AFTER not in ("ln1", "wt", "qkv") and "attna" not in DEBUG_SKIP:
            ph_attn_a(drain=drain2)
            barrier()
            if "xdbg_sc" in DEBUG_SCRATCH and STOP_AFTER == "attna":
                p.dma("gpsimd", "dbg", lambda e: e.dma_start(out=xdbg_sc, in_=X[:, :]))
                barrier()
        if dec_obj is not None:
            dec_obj.flush()
            barrier()
        if STOP_AFTER not in ("ln1", "wt", "qkv", "attna", "decode"):
            Xsc2 = ExitStack()
            Xln2 = Xsc2.enter_context(nc.sbuf_tensor("Xln2", [128, FC * T], BF16))
            ph_ln("2", Xln2, first=False)
            barrier()
            if STOP_AFTER == "ln2":
                if "xlndbg_sc" in DEBUG_SCRATCH:
                    p.dma("gpsimd", "dbg", lambda e: e.dma_start(out=xlndbg_sc, in_=Xln2[:, :]))
                    barrier()
            else:
                ph_ffn(Xln2, 1)
                barrier()
            Xsc2.close()
        if STOP_AFTER in (None, "attnb", "ffn2"):
            Xsc3 = ExitStack()
            Xln3 = Xsc3.enter_context(nc.sbuf_tensor("Xln3", [128, FC * T], BF16))
            ph_ln("3", Xln3, first=False)
            barrier()
            ph_attn_b(Xln3)
            Xsc3.close()
            barrier()
            if "xdbg_sc" in DEBUG_SCRATCH and STOP_AFTER == "attnb":
                p.dma("gpsimd", "dbg", lambda e: e.dma_start(out=xdbg_sc, in_=X[:, :]))
                barrier()
        if STOP_AFTER in (None, "ffn2"):
            Xsc4 = ExitStack()
            Xln4 = Xsc4.enter_context(nc.sbuf_tensor("Xln4", [128, FC * T], BF16))
            ph_ln("4", Xln4, first=False)
            barrier()
            ph_ffn(Xln4, 2)
            Xsc4.close()
            barrier()
        if STOP_AFTER is None:
            ph_out()
            if "xdbg_sc" in DEBUG_SCRATCH and STOP_AFTER == "ffn1":
                p.dma("gpsimd", "dbg", lambda e: e.dma_start(out=xdbg_sc, in_=X[:, :]))
                barrier()


        p.add("gpsimd", None, waits=p.all_sems(), incs=[("fin", 1)])

        block = es.enter_context(nc.Block())
        p.emit(block)
    return nc


def host_consts(core_id):
    pos0 = T * core_id
    invf = (10000.0 ** (-np.arange(0, HD, 2, dtype=np.float64) / HD))
    pos = np.arange(pos0, pos0 + T, dtype=np.float64)
    f = np.outer(pos, invf)
    emb = np.concatenate([f, f], axis=-1)
    cos = np.cos(emb).astype(np.float32)
    sin = np.sin(emb).astype(np.float32)
    return {
        "cose_in": np.ascontiguousarray(cos[:, 0::2]),
        "coso_in": np.ascontiguousarray(cos[:, 1::2]),
        "sine_in": np.ascontiguousarray(sin[:, 0::2]),
        "sino_in": np.ascontiguousarray(sin[:, 1::2]),
        "idb_in": np.eye(128, dtype=np.float32).astype(BF),
        "idf_in": np.eye(128, dtype=np.float32),
        "maska_in": np.concatenate([
            (np.arange(256)[None, :] >= np.arange(128)[:, None]).astype(np.float32),
            (np.arange(256)[None, :] >= (np.arange(128) + 128)[:, None]).astype(np.float32),
        ], axis=0),
        "ones_in": np.ones((128, 2), dtype=np.float32).astype(BF),
        "onesr_in": np.ones((1, 128), dtype=np.float32).astype(BF),
        "onesrf_in": np.ones((1, 128), dtype=np.float32),
        "lnc_in": np.broadcast_to(np.array([1.0 / D, -1.0], np.float32), (128, 2)).copy(),
        "premask_in": np.broadcast_to(
            (np.arange(8)[None, :] < core_id).astype(np.float32), (128, 8)).copy(),
    }


def make_in_maps(inputs):
    x = np.ascontiguousarray(np.asarray(inputs["L0_attn_a_x"], dtype=np.float32)[0])
    shared = {
        "wqkv_in": np.ascontiguousarray(np.asarray(inputs["L0_attn_a_W_qkv"], dtype=np.float32)),
        "wout_in": np.ascontiguousarray(np.asarray(inputs["L0_attn_a_W_out"], dtype=np.float32)),
        "gpk_in": np.ascontiguousarray(np.asarray(inputs["L0_ffn_gate_W_packed"], dtype=np.int32)),
        "gsc_in": np.ascontiguousarray(np.asarray(inputs["L0_ffn_gate_scales"], dtype=np.float32)),
        "upk_in": np.ascontiguousarray(np.asarray(inputs["L0_ffn_up_W_packed"], dtype=np.int32)),
        "usc_in": np.ascontiguousarray(np.asarray(inputs["L0_ffn_up_scales"], dtype=np.float32)),
        "dpk_in": np.ascontiguousarray(np.asarray(inputs["L0_ffn_down_W_packed"], dtype=np.int32)),
        "dsc_in": np.ascontiguousarray(np.asarray(inputs["L0_ffn_down_scales"], dtype=np.float32)),
    }
    gpk = shared["gpk_in"]; upk = shared["upk_in"]; dpk = shared["dpk_in"]
    gsc = shared["gsc_in"]; usc = shared["usc_in"]; dsc = shared["dsc_in"]

    def unit_slices(u):
        if u < 128:
            return gpk[u * 128:(u + 1) * 128, :], gsc[u * 128:(u + 1) * 128, :]
        if u < 256:
            nb = u - 128
            return upk[nb * 128:(nb + 1) * 128, :], usc[nb * 128:(nb + 1) * 128, :]
        hb, sb = (u - 256) // 8, (u - 256) % 8
        return (dpk[hb * 128:(hb + 1) * 128, sb * 1024:(sb + 1) * 1024],
                dsc[hb * 128:(hb + 1) * 128, sb * 64:(sb + 1) * 64])

    in_maps = []
    for i in range(NCORES):
        m = {"x_in": np.ascontiguousarray(x[i * T:(i + 1) * T])}
        pks, scs = [], []
        for u in range(48 * i, 48 * (i + 1)):
            a, b = unit_slices(u)
            pks.append(a)
            scs.append(b)
        m["mypk_in"] = np.ascontiguousarray(np.concatenate(pks, axis=0))
        m["mysc_in"] = np.ascontiguousarray(np.concatenate(scs, axis=0))
        m.update(shared)
        m.update(host_consts(i))
        in_maps.append(m)
    return in_maps


def kernel(L0_attn_a_x, L0_attn_a_W_qkv, L0_attn_a_W_out,
           L0_ffn_gate_W_packed, L0_ffn_gate_scales,
           L0_ffn_up_W_packed, L0_ffn_up_scales,
           L0_ffn_down_W_packed, L0_ffn_down_scales):
    in_maps = make_in_maps(dict(
        L0_attn_a_x=L0_attn_a_x, L0_attn_a_W_qkv=L0_attn_a_W_qkv,
        L0_attn_a_W_out=L0_attn_a_W_out,
        L0_ffn_gate_W_packed=L0_ffn_gate_W_packed, L0_ffn_gate_scales=L0_ffn_gate_scales,
        L0_ffn_up_W_packed=L0_ffn_up_W_packed, L0_ffn_up_scales=L0_ffn_up_scales,
        L0_ffn_down_W_packed=L0_ffn_down_W_packed, L0_ffn_down_scales=L0_ffn_down_scales))
    nc = build()
    res = run_bass_kernel_spmd(nc, in_maps, core_ids=list(range(NCORES)))
    out = np.concatenate([res.results[i]["y_out"] for i in range(NCORES)], axis=0)
    return out[None, :, :]


# version 07
# speedup vs baseline: 1.5674x; 1.0013x over previous
import numpy as np
import ml_dtypes
from contextlib import ExitStack

import concourse.bass as bass
import concourse.mybir as mybir
from concourse.bass_utils import run_bass_kernel_spmd

F32 = mybir.dt.float32
BF16 = mybir.dt.bfloat16
I32 = mybir.dt.int32
I16 = mybir.dt.int16
AF = mybir.ActivationFunctionType
OP = mybir.AluOpType
AX = mybir.AxisListType
BF = ml_dtypes.bfloat16

NCORES = 8
T = 1024
TC = 8
D = 2048
FC = 16
H = 16
HD = 128
CH = 256
NCH = 4
SEG = 512
NSEG = 2
NMLP = 16384
EPS = 1e-5
GELU_A = 0.044715
GELU_S = 1.5957691216057308
BSCALE = 0.08838834764831845

ENGINES = ("sync", "tensor", "vector", "scalar", "gpsimd")

STOP_AFTER = None
DEBUG_SCRATCH = ()
DEBUG_SKIP = ()
RACE_DETECT = True
LN_PASS_A_ONLY = False


class Prog:

    def __init__(self, nc, es):
        self.nc = nc
        self.es = es
        self.ops = {e: [] for e in ENGINES}
        self.sem_h = {}
        self.sem_n = {}

    def sem(self, name):
        if name not in self.sem_h:
            self.sem_h[name] = self.es.enter_context(self.nc.semaphore(name))
            self.sem_n[name] = 0
        return name

    def n(self, name):
        return self.sem_n[name]

    def dma(self, eng, stream, fn, waits=(), slot=0):
        s = self.sem(f"{stream}.{slot}")
        self.add(eng, fn, waits=waits, incs=[(s, 16)])
        return (s, self.n(s))

    def all_sems(self):
        return [(s, self.sem_n[s]) for s in self.sem_n]

    def add(self, eng, fn=None, waits=(), incs=()):
        waits = [(self.sem_h[s], v) for s, v in waits if v > 0]
        incs_h = [(self.sem_h[s], k) for s, k in incs]
        for s, k in incs:
            self.sem_n[s] += k

        def emit(e):
            for h, v in waits:
                e.wait_ge(h, v)
            if fn is not None:
                ins = fn(e)
                for h, k in incs_h:
                    ins.then_inc(h, k, skip_validation=(k == 16))
            else:
                for h, k in incs_h:
                    e.sem_inc(h, k)
        self.ops[eng].append(emit)

    def emit(self, block):
        for ename in ENGINES:
            lst = self.ops[ename]
            if not lst:
                continue
            sec = getattr(block, ename)

            def make(lst):
                def _(e):
                    for op in lst:
                        op(e)
                return _
            sec(make(lst))


def r3(ap, b=4):
    return ap.rearrange("p (b d) -> p b d", b=b)


def build():
    nc = bass.Bass("TRN2", target_bir_lowering=False, detect_race_conditions=RACE_DETECT)

    def din(name, shape, dt):
        return nc.dram_tensor(name, shape, dt, kind="ExternalInput").ap()

    x_in = din("x_in", [T, D], F32)
    wqkv_in = din("wqkv_in", [3 * D, D], F32)
    wout_in = din("wout_in", [D, D], F32)
    gpk_in = din("gpk_in", [NMLP, D // 2], I32)
    gsc_in = din("gsc_in", [NMLP, 64], F32)
    upk_in = din("upk_in", [NMLP, D // 2], I32)
    usc_in = din("usc_in", [NMLP, 64], F32)
    dpk_in = din("dpk_in", [D, NMLP // 2], I32)
    dsc_in = din("dsc_in", [D, 512], F32)
    idb_in = din("idb_in", [128, 128], BF16)
    idf_in = din("idf_in", [128, 128], F32)
    cose_in = din("cose_in", [T, 64], F32)
    coso_in = din("coso_in", [T, 64], F32)
    sine_in = din("sine_in", [T, 64], F32)
    sino_in = din("sino_in", [T, 64], F32)
    maska_in = din("maska_in", [256, 256], F32)
    ones_in = din("ones_in", [128, 2], BF16)
    onesr_in = din("onesr_in", [1, 128], BF16)
    onesrf_in = din("onesrf_in", [1, 128], F32)
    premask_in = din("premask_in", [128, 8], F32)
    mypk_in = din("mypk_in", [48 * 128, 1024], I32)
    mysc_in = din("mysc_in", [48 * 128, 64], F32)
    lnc_in = din("lnc_in", [128, 2], F32)

    y_out = nc.dram_tensor("y_out", [T, D], F32, kind="ExternalOutput").ap()

    def dscr(name, shape, dt):
        kind = "ExternalOutput" if name in DEBUG_SCRATCH else "Internal"
        return nc.dram_tensor(name, shape, dt, kind=kind).ap()

    wqkvT_sc = dscr("wqkvT_sc", [D, 3 * D], BF16)
    woutT_sc = dscr("woutT_sc", [D, D], BF16)
    q_sc = dscr("q_sc", [T, D], BF16)
    k_sc = dscr("k_sc", [T, D], BF16)
    v_sc = dscr("v_sc", [T, D], BF16)
    ccw_in = dscr("ccw_in", [48 * 128, 2048], BF16)
    ccd_out = nc.dram_tensor("ccd_out", [8 * 48 * 128, 2048], BF16,
                             kind="ExternalOutput" if "ccd_out" in DEBUG_SCRATCH else "Internal",
                             addr_space="Shared").ap()
    cc_in = dscr("cc_in", [1, 128 * H * 132], F32)
    cc_out = dscr("cc_out", [NCORES, 128 * H * 132], F32)
    xdbg_sc = dscr("xdbg_sc", [128, FC * T], F32)
    sdbg_sc = dscr("sdbg_sc", [128, H * 132], F32)
    abdbg_sc = dscr("abdbg_sc", [128, 2048], F32)
    odbg_sc = dscr("odbg_sc", [128, H * 256], BF16)
    xlndbg_sc = dscr("xlndbg_sc", [128, FC * T], BF16)

    with ExitStack() as es:
        p = Prog(nc, es)
        for s in ("pe", "dve", "act", "pool", "cc", "fin",
                  "b_sync", "b_tensor", "b_vector", "b_scalar", "b_gpsimd"):
            p.sem(s)
        EB = es.enter_context

        def barrier():
            for e in ENGINES:
                p.add(e, None, incs=[(f"b_{e}", 1)])
            w = [x for x in p.all_sems() if x[0] != "cc"]
            for e in ENGINES:
                p.add(e, None, waits=w)

        X = EB(nc.sbuf_tensor("X", [128, FC * T], F32))
        idb = EB(nc.sbuf_tensor("idb", [128, 128], BF16))
        idf = EB(nc.sbuf_tensor("idf", [128, 128], F32))
        maska = EB(nc.sbuf_tensor("maska", [128, 2 * 256], F32))
        onesb = EB(nc.sbuf_tensor("onesb", [128, 2], BF16))
        ones_row = EB(nc.sbuf_tensor("ones_row", [1, 128], BF16))
        onesrf = EB(nc.sbuf_tensor("onesrf", [1, 128], F32))
        premask = EB(nc.sbuf_tensor("premask", [128, 8], F32))
        lnc = EB(nc.sbuf_tensor("lnc", [128, 2], F32))


        def ph_ln(tag, Xln, first, drain=None):
            sc = ExitStack()
            E = sc.enter_context
            xtm = [E(nc.sbuf_tensor(f"xtm{tag}{i}", [128, D], F32)) for i in range(2)]
            xlt = [E(nc.sbuf_tensor(f"xlt{tag}{i}", [128, D], BF16)) for i in range(2)]
            junk = E(nc.sbuf_tensor(f"junk{tag}", [128, D], BF16))
            st = {nm: E(nc.sbuf_tensor(f"{nm}{tag}", [128, TC], F32))
                  for nm in ("sx", "sxx", "mu", "musq", "var", "srt", "rstd", "nmr")}
            psX = [E(nc.psum_tensor(f"psX{tag}{i}", [128, 512], F32)) for i in range(2)]
            psL = [E(nc.psum_tensor(f"psL{tag}{i}", [128, 1024], BF16)) for i in range(2)]

            def make_xtm(tc, evX, post):
                if first:
                    w = [x for x in post.get(("xtmfree", tc - 2), []) if x[1] > 0]
                    tok = p.dma("sync", "xld", lambda e, tc=tc: e.dma_start(
                        out=xtm[tc % 2][:, :], in_=x_in[tc * 128:(tc + 1) * 128, :]),
                        waits=w, slot=tc % 2)
                    return [tok, CONSTS_TOK]
                else:
                    last = []
                    for g in range(4):
                        lin = tc * 4 + g
                        w = []
                        if ("psX", lin - 2) in post:
                            w.append(post[("psX", lin - 2)])
                        if g == 0:
                            w += post.get(("xtmfree", tc - 2), [])
                        w = [x for x in w if x[1] > 0]
                        for j in range(4):
                            fc = g * 4 + j
                            p.add("tensor", lambda e, tc=tc, g=g, j=j, fc=fc: e.transpose(
                                psX[g % 2][:, j * 128:(j + 1) * 128],
                                X[:, fc * T + tc * 128: fc * T + tc * 128 + 128], idf[:, :]),
                                waits=(w if j == 0 else []),
                                incs=([("pe", 1)] if j == 3 else []))
                        pe_ok = p.n("pe")
                        p.add("scalar", lambda e, tc=tc, g=g: e.copy(
                            out=xtm[tc % 2][:, :].rearrange("p (g j f) -> p g j f", g=4, j=4)[:, g, :, :],
                            in_=psX[g % 2][:, :].rearrange("p (j f) -> p j f", j=4)),
                            waits=[("pe", pe_ok)], incs=[("act", 1)])
                        post[("psX", lin)] = ("act", p.n("act"))
                    return [("act", p.n("act"))]

            post = {}
            for tc in range(TC):
                if drain:
                    drain()
                rdy = make_xtm(tc, None, post)
                xs = xtm[tc % 2][:, :]
                p.add("vector", lambda e, xs=xs, tc=tc: e.reduce_sum(
                    out=st["sx"][:, tc:tc + 1], in_=xs, axis=AX.X),
                    waits=rdy, incs=[("dve", 1)])
                rsum_ok = p.n("dve")
                p.add("scalar", lambda e, xs=xs, tc=tc: e.activation(
                    out=junk[:, :], in_=xs, func=AF.Square, accum_out=st["sxx"][:, tc:tc + 1]),
                    waits=rdy, incs=[("act", 1)])
                post[("xtmfree", tc)] = [("act", p.n("act")), ("dve", rsum_ok)]
            sxA, sxxA = p.n("dve"), p.n("act")

            lnc0b = lnc[:, 0:1].broadcast_to([128, TC])
            lnc1b = lnc[:, 1:2].broadcast_to([128, TC])
            p.add("gpsimd", lambda e: e.tensor_tensor(
                out=st["mu"][:, :], in0=lnc0b, in1=st["sx"][:, :], op=OP.mult),
                waits=[("dve", sxA)])
            p.add("gpsimd", lambda e: e.tensor_tensor(
                out=st["musq"][:, :], in0=st["mu"][:, :], in1=st["mu"][:, :], op=OP.mult))
            p.add("gpsimd", lambda e: e.tensor_tensor(
                out=st["var"][:, :], in0=lnc0b, in1=st["sxx"][:, :], op=OP.mult),
                waits=[("act", sxxA)])
            p.add("gpsimd", lambda e: e.tensor_tensor(
                out=st["var"][:, :], in0=st["var"][:, :], in1=st["musq"][:, :], op=OP.subtract),
                incs=[("pool", 1)])
            var_ok = p.n("pool")
            p.add("scalar", lambda e: e.activation(
                out=st["srt"][:, :], in_=st["var"][:, :], func=AF.Sqrt,
                bias=np.float32(EPS), scale=np.float32(1.0)),
                waits=[("pool", var_ok)], incs=[("act", 1)])
            srt_ok = p.n("act")
            p.add("vector", lambda e: e.reciprocal(out=st["rstd"][:, :], in_=st["srt"][:, :]),
                  waits=[("act", srt_ok)], incs=[("dve", 1)])
            rstd_ok = p.n("dve")
            p.add("gpsimd", lambda e: e.tensor_tensor(
                out=st["musq"][:, :], in0=lnc1b, in1=st["mu"][:, :], op=OP.mult))
            p.add("gpsimd", lambda e: e.tensor_tensor(
                out=st["nmr"][:, :], in0=st["musq"][:, :], in1=st["rstd"][:, :], op=OP.mult),
                waits=[("dve", rstd_ok)], incs=[("pool", 1)])
            nmr_ok = p.n("pool")

            if LN_PASS_A_ONLY and not first:
                sc.close()
                return
            passA_tail = [("act", p.n("act")), ("dve", p.n("dve"))]
            post = {("psX", -2): passA_tail[0], ("psX", -1): passA_tail[1],
                    ("xtmfree", -2): passA_tail, ("xtmfree", -1): passA_tail}
            aff = {}
            evL = {}
            for tc in range(TC):
                if drain:
                    drain()
                rdy = make_xtm(tc, None, post)
                if first:
                    for g in range(4):
                        lin = tc * 4 + g
                        w = list(rdy) if g == 0 else []
                        if ("psX", lin - 2) in post:
                            w.append(post[("psX", lin - 2)])
                        for j in range(4):
                            fc = g * 4 + j
                            p.add("tensor", lambda e, tc=tc, g=g, j=j, fc=fc: e.transpose(
                                psX[g % 2][:, j * 128:(j + 1) * 128],
                                xtm[tc % 2][:, fc * 128:(fc + 1) * 128], idf[:, :]),
                                waits=(w if j == 0 else []),
                                incs=([("pe", 1)] if j == 3 else []))
                        pe_ok = p.n("pe")
                        p.add("vector", lambda e, tc=tc, g=g: e.tensor_copy(
                            out=X[:, :].rearrange("p (f t) -> p f t", f=FC)[:, g * 4:(g + 1) * 4, tc * 128:(tc + 1) * 128],
                            in_=psX[g % 2][:, :].rearrange("p (j f) -> p j f", j=4)),
                            waits=[("pe", pe_ok)], incs=[("dve", 1)])
                        post[("psX", lin)] = ("dve", p.n("dve"))

                waff = list(rdy) + [("pool", nmr_ok)]
                if tc >= 2:
                    waff.append(("act", evL[(tc - 2) * 4 + 3]))
                p.add("vector", lambda e, tc=tc: e.tensor_scalar(
                    out=xlt[tc % 2][:, :], in0=xtm[tc % 2][:, :],
                    scalar1=st["rstd"][:, tc:tc + 1], scalar2=st["nmr"][:, tc:tc + 1],
                    op0=OP.mult, op1=OP.add),
                    waits=waff, incs=[("dve", 1)])
                aff[tc] = p.n("dve")
                post[("xtmfree", tc)] = [("dve", aff[tc])]

                for g in range(4):
                    lin = tc * 4 + g
                    w = [("dve", aff[tc])] if g == 0 else []
                    if lin >= 2:
                        w.append(("act", evL[lin - 2]))
                    for j in range(4):
                        fc = g * 4 + j
                        p.add("tensor", lambda e, tc=tc, g=g, j=j, fc=fc: e.transpose(
                            psL[g % 2][:, j * 128:(j + 1) * 128],
                            xlt[tc % 2][:, fc * 128:(fc + 1) * 128], idb[:, :]),
                            waits=(w if j == 0 else []),
                            incs=([("pe", 1)] if j == 3 else []))
                    pe_ok = p.n("pe")
                    p.add("scalar", lambda e, Xln=Xln, tc=tc, g=g: e.copy(
                        out=Xln[:, :].rearrange("p (f t) -> p f t", f=FC)[:, g * 4:(g + 1) * 4, tc * 128:(tc + 1) * 128],
                        in_=psL[g % 2][:, 0:512].rearrange("p (j f) -> p j f", j=4)),
                        waits=[("pe", pe_ok)], incs=[("act", 1)])
                    evL[lin] = p.n("act")
            sc.close()

        def ph_wt(drain=None):
            sc = ExitStack()
            E = sc.enter_context
            wraw = [E(nc.sbuf_tensor(f"wraw{i}", [128, D], F32)) for i in range(2)]
            wbf = [E(nc.sbuf_tensor(f"wbf{i}", [128, D], BF16)) for i in range(4)]
            wst = [E(nc.sbuf_tensor(f"wstg{i}", [128, 512], BF16)) for i in range(2)]
            psT = [E(nc.psum_tensor(f"psWT{i}", [128, 1024], BF16)) for i in range(2)]

            cvt = {}
            evc = {}
            dmo = {}
            units = [("qkv", og) for og in range(12)] + [("out", og) for og in range(4)]
            for u, (kind, og) in enumerate(units):
                if drain:
                    drain()
                src = wqkv_in if kind == "qkv" else wout_in
                dst = wqkvT_sc if kind == "qkv" else woutT_sc
                for j in range(4):
                    if j < 2:
                        w = [("act", cvt[(u - 1, j + 2)])] if u >= 1 else []
                    else:
                        w = [("act", cvt[(u, j - 2)])]
                    tok = p.dma("sync", "wraw", lambda e, src=src, og=og, j=j: e.dma_start(
                        out=wraw[j % 2][:, :], in_=src[og * 512 + j * 128: og * 512 + (j + 1) * 128, :]),
                        waits=w, slot=j % 2)
                    w2 = [tok]
                    if u >= 1:
                        w2.append(("pe", pe_units[u - 1]))
                    p.add("scalar", lambda e, j=j: e.copy(out=wbf[j][:, :], in_=wraw[j % 2][:, :]),
                          waits=w2, incs=[("act", 1)])
                    cvt[(u, j)] = p.n("act")
                for kc in range(16):
                    lin = u * 16 + kc
                    w = [("act", cvt[(u, 3)])] if kc == 0 else []
                    if lin >= 2:
                        w.append(("act", evc[lin - 2]))
                    for j in range(4):
                        p.add("tensor", lambda e, kc=kc, j=j: e.transpose(
                            psT[kc % 2][:, j * 128:(j + 1) * 128],
                            wbf[j][:, kc * 128:(kc + 1) * 128], idb[:, :]),
                            waits=(w if j == 0 else []),
                            incs=([("pe", 1)] if j == 3 else []))
                    pe_ok = p.n("pe")
                    w2 = [("pe", pe_ok)]
                    if lin >= 2:
                        w2.append(dmo[lin - 2])
                    p.add("scalar", lambda e, kc=kc: e.copy(out=wst[kc % 2][:, :], in_=psT[kc % 2][:, 0:512]),
                          waits=w2, incs=[("act", 1)])
                    evc[lin] = p.n("act")
                    dmo[lin] = p.dma("gpsimd", "wtw", lambda e, dst=dst, kc=kc, og=og: e.dma_start(
                        out=dst[kc * 128:(kc + 1) * 128, og * 512:(og + 1) * 512], in_=wst[kc % 2][:, :]),
                        waits=[("act", evc[lin])], slot=kc % 2)
                pe_units[u] = p.n("pe")
            sc.close()
        pe_units = {}

        def ph_qkv(Xln, drain=None):
            sc = ExitStack()
            E = sc.enter_context
            wqT = [E(nc.sbuf_tensor(f"wqT{i}", [128, 16 * 512], BF16)) for i in range(2)]
            stg = [E(nc.sbuf_tensor(f"qstg{i}", [128, 512], BF16)) for i in range(3)]
            ra = E(nc.sbuf_tensor("ra", [128, 256], F32))
            rb = E(nc.sbuf_tensor("rb", [128, 256], F32))
            rc = E(nc.sbuf_tensor("rcc", [128, 256], F32))
            rd = E(nc.sbuf_tensor("rdd", [128, 256], F32))
            qp = E(nc.sbuf_tensor("qp", [128, 512], F32))
            me = E(nc.sbuf_tensor("me", [128, 512], F32))
            ex = E(nc.sbuf_tensor("ex", [128, 512], F32))
            rl = E(nc.sbuf_tensor("rl", [128, 512], F32))
            psQ = [E(nc.psum_tensor(f"psQ{i}", [128, 512], F32)) for i in range(2)]

            psum_free = {}
            stage_ok = {}
            spill_ok = {}
            wq_ok = {}
            mm_done = {}
            for og in range(12):
                w = [("pe", mm_done[(og - 2) * 8 + 7])] if og >= 2 else []
                wq_ok[og] = p.dma("sync", "wqT", lambda e, og=og: e.dma_start(
                    out=wqT[og % 2][:, :].rearrange("p (kc f) -> p kc f", kc=16),
                    in_=wqkvT_sc[:, og * 512:(og + 1) * 512].rearrange("(kc p) f -> p kc f", p=128)),
                    waits=w, slot=og % 2)
                kind = "q" if og < 4 else ("k" if og < 8 else "v")
                for tc in range(TC):
                    if drain:
                        drain()
                    i = og * 8 + tc
                    w = [wq_ok[og]] if tc == 0 else []
                    if i >= 2:
                        w.append(psum_free[i - 2])
                    for kc in range(16):
                        p.add("tensor", lambda e, og=og, tc=tc, kc=kc, i=i: e.matmul(
                            psQ[i % 2][:, :],
                            Xln[:, kc * T + tc * 128: kc * T + tc * 128 + 128],
                            wqT[og % 2][:, kc * 512:(kc + 1) * 512],
                            start=(kc == 0), stop=(kc == 15)),
                            waits=(w if kc == 0 else []),
                            incs=([("pe", 1)] if kc == 15 else []))
                    mm_done[i] = p.n("pe")
                    stb = stg[i % 3]
                    wstg = [spill_ok[i - 3]] if i >= 3 else []
                    if kind == "v":
                        p.add("scalar", lambda e, stb=stb, i=i: e.copy(out=stb[:, :], in_=psQ[i % 2][:, :]),
                              waits=[("pe", mm_done[i])] + wstg, incs=[("act", 1)])
                        psum_free[i] = ("act", p.n("act"))
                        stage_ok[i] = ("act", p.n("act"))
                    else:
                        ps = psQ[i % 2][:, :]
                        pse = r3(ps)[:, :, 0::2]
                        pso = r3(ps)[:, :, 1::2]
                        ce = cose[:, tc * 64:(tc + 1) * 64].unsqueeze(1).broadcast_to([128, 4, 64])
                        co = coso[:, tc * 64:(tc + 1) * 64].unsqueeze(1).broadcast_to([128, 4, 64])
                        se = sine[:, tc * 64:(tc + 1) * 64].unsqueeze(1).broadcast_to([128, 4, 64])
                        so = sino[:, tc * 64:(tc + 1) * 64].unsqueeze(1).broadcast_to([128, 4, 64])
                        p.add("vector", lambda e, ce=ce, pse=pse: e.tensor_tensor(
                            out=r3(ra[:, :], 4), in0=ce, in1=pse, op=OP.mult),
                            waits=[("pe", mm_done[i])])
                        p.add("vector", lambda e, se=se, pso=pso: e.tensor_tensor(
                            out=r3(rb[:, :], 4), in0=se, in1=pso, op=OP.mult))
                        p.add("vector", lambda e, co=co, pso=pso: e.tensor_tensor(
                            out=r3(rc[:, :], 4), in0=co, in1=pso, op=OP.mult))
                        p.add("vector", lambda e, so=so, pse=pse: e.tensor_tensor(
                            out=r3(rd[:, :], 4), in0=so, in1=pse, op=OP.mult),
                            incs=[("dve", 1)])
                        psum_free[i] = ("dve", p.n("dve"))
                        p.add("vector", lambda e: e.tensor_tensor(
                            out=r3(qp[:, :])[:, :, 0::2], in0=r3(ra[:, :], 4), in1=r3(rb[:, :], 4),
                            op=OP.subtract))
                        p.add("vector", lambda e: e.tensor_tensor(
                            out=r3(qp[:, :])[:, :, 1::2], in0=r3(rc[:, :], 4), in1=r3(rd[:, :], 4),
                            op=OP.add))
                        p.add("vector", lambda e: e.tensor_scalar(
                            out=me[:, :], in0=qp[:, :], scalar1=0.0, scalar2=None, op0=OP.min),
                            incs=[("dve", 1)])
                        me_ok = p.n("dve")
                        p.add("scalar", lambda e: e.activation(out=ex[:, :], in_=me[:, :], func=AF.Exp),
                              waits=[("dve", me_ok)], incs=[("act", 1)])
                        ex_ok = p.n("act")
                        p.add("vector", lambda e: e.tensor_relu(out=rl[:, :], in_=qp[:, :]))
                        p.add("vector", lambda e, stb=stb: e.tensor_tensor(
                            out=stb[:, :], in0=ex[:, :], in1=rl[:, :], op=OP.add),
                            waits=[("act", ex_ok)] + wstg, incs=[("dve", 1)])
                        stage_ok[i] = ("dve", p.n("dve"))
                    dstm = {"q": q_sc, "k": k_sc, "v": v_sc}[kind]
                    col = (og % 4) * 512
                    spill_ok[i] = p.dma("sync", "spill", lambda e, dstm=dstm, tc=tc, col=col, stb=stb: e.dma_start(
                        out=dstm[tc * 128:(tc + 1) * 128, col:col + 512], in_=stb[:, :]),
                        waits=[stage_ok[i]], slot=i % 3)
            sc.close()

        def ph_attn_a(drain=None):
            sc = ExitStack()
            E = sc.enter_context
            S_run = E(nc.sbuf_tensor("S_run", [128, H * 132], F32))
            S_bf = E(nc.sbuf_tensor("S_bf", [128, H * 132], BF16))
            sc23 = ExitStack()
            E2 = sc23.enter_context
            Sloc = E2(nc.sbuf_tensor("Sloc", [128, H * 132], F32))
            kh = [E2(nc.sbuf_tensor(f"khA{i}", [128, TC * 128], BF16)) for i in range(2)]
            vh = [E2(nc.sbuf_tensor(f"vhA{i}", [128, TC * 128], BF16)) for i in range(2)]
            rkb = [E2(nc.sbuf_tensor(f"rkb{i}", [128, H * 132], F32)) for i in range(2)]
            tmpS = E2(nc.sbuf_tensor("tmpS", [128, H * 132], F32))
            psS = E2(nc.psum_tensor("psS", [128, 512], F32))
            psSz = E2(nc.psum_tensor("psSz", [128, 512], F32))

            ev_ok = 0
            for h in range(H):
                if drain:
                    drain()
                w = []
                if h >= 2:
                    w.append(("pe", st_mm[h - 2]))
                t1 = p.dma("sync", "kh", lambda e, h=h: e.dma_start(
                    out=kh[h % 2][:, :].rearrange("p (tc d) -> p tc d", tc=TC),
                    in_=k_sc[:, h * 128:(h + 1) * 128].rearrange("(tc p) d -> p tc d", p=128)),
                    waits=w, slot=h % 2)
                t2 = p.dma("sync", "vh", lambda e, h=h: e.dma_start(
                    out=vh[h % 2][:, :].rearrange("p (tc d) -> p tc d", tc=TC),
                    in_=v_sc[:, h * 128:(h + 1) * 128].rearrange("(tc p) d -> p tc d", p=128)),
                    slot=h % 2)
                w = [t1, t2]
                if h >= 1:
                    w.append(("dve", ev_ok))
                for mc in range(TC):
                    p.add("tensor", lambda e, h=h, mc=mc: e.matmul(
                        psS[:, 0:128], kh[h % 2][:, mc * 128:(mc + 1) * 128],
                        vh[h % 2][:, mc * 128:(mc + 1) * 128],
                        start=(mc == 0), stop=(mc == TC - 1)),
                        waits=(w if mc == 0 else []))
                    p.add("tensor", lambda e, h=h, mc=mc: e.matmul(
                        psSz[:, 0:1], kh[h % 2][:, mc * 128:(mc + 1) * 128],
                        onesb[:, 0:1],
                        start=(mc == 0), stop=(mc == TC - 1)),
                        incs=([("pe", 1)] if mc == TC - 1 else []))
                st_mm[h] = p.n("pe")
                p.add("vector", lambda e, h=h: e.tensor_copy(
                    out=Sloc[:, h * 132:h * 132 + 128], in_=psS[:, 0:128]),
                    waits=[("pe", st_mm[h])])
                p.add("vector", lambda e, h=h: e.tensor_copy(
                    out=Sloc[:, h * 132 + 128:h * 132 + 129], in_=psSz[:, 0:1]),
                    incs=[("dve", 1)])
                ev_ok = p.n("dve")

            cctok = p.dma("gpsimd", "ccw", lambda e: e.dma_start(
                out=cc_in[0, :].rearrange("(pp c) -> pp c", pp=128), in_=Sloc[:, :]),
                waits=[("dve", ev_ok)])
            p.add("gpsimd", lambda e: e.collective_compute(
                "AllGather", OP.bypass,
                replica_groups=[list(range(NCORES))],
                ins=[cc_in[:, :]], outs=[cc_out[:, :]]),
                waits=[cctok], incs=[("cc", 1)])
            cc_ok = p.n("cc")
            sin_ok = None
            for j in range(NCORES):
                w = [("cc", cc_ok)]
                if j >= 2:
                    w.append(("dve", p.n("dve")))
                ldtok = p.dma("sync", "rk", lambda e, j=j: e.dma_start(
                    out=rkb[j % 2][:, :], in_=cc_out[j, :].rearrange("(pp c) -> pp c", pp=128)),
                    waits=w, slot=j % 2)
                if j == 0:
                    p.add("vector", lambda e, j=j: e.tensor_scalar(
                        out=S_run[:, :], in0=rkb[j % 2][:, :], scalar1=premask[:, j:j + 1],
                        scalar2=None, op0=OP.mult),
                        waits=[ldtok], incs=[("dve", 1)])
                else:
                    p.add("vector", lambda e, j=j: e.tensor_scalar(
                        out=tmpS[:, :], in0=rkb[j % 2][:, :], scalar1=premask[:, j:j + 1],
                        scalar2=None, op0=OP.mult),
                        waits=[ldtok])
                    p.add("vector", lambda e: e.tensor_tensor(
                        out=S_run[:, :], in0=tmpS[:, :], in1=S_run[:, :], op=OP.add),
                        incs=[("dve", 1)])
            sin_ok = p.n("dve")
            p.add("scalar", lambda e: e.copy(out=S_bf[:, :], in_=S_run[:, :]),
                  waits=[("dve", sin_ok)], incs=[("act", 1)])
            sbf_ok = {(-1, h): p.n("act") for h in range(H)}
            sc23.close()
            wout_sb = E(nc.sbuf_tensor("wout_sb", [128, FC * D], BF16))
            o_buf = E(nc.sbuf_tensor("o_buf", [128, H * 256], BF16))
            qch = [E(nc.sbuf_tensor(f"qch{i}", [128, 256], BF16)) for i in range(2)]
            kch = [E(nc.sbuf_tensor(f"kch{i}", [128, 256], BF16)) for i in range(2)]
            vch = [E(nc.sbuf_tensor(f"vch{i}", [128, 256], BF16)) for i in range(2)]
            qfm = [E(nc.sbuf_tensor(f"qfm{i}", [128, 256], BF16)) for i in range(2)]
            kfm = [E(nc.sbuf_tensor(f"kfm{i}", [128, 256], BF16)) for i in range(2)]
            p2 = E(nc.sbuf_tensor("p2t", [128, 2 * 256], BF16))
            dr_t = E(nc.sbuf_tensor("dr_t", [1, 512], F32))
            dsb = dr_t[:, 0:256]
            rden = dr_t[:, 256:512]
            bsb = E(nc.sbuf_tensor("bsb", [128, 256], F32))
            spare = E(nc.sbuf_tensor("spareA", [128, 2], F32))
            psT2 = E(nc.psum_tensor("psT2", [128, 1024], BF16))
            psA = [E(nc.psum_tensor(f"psA{i}", [128, 512], F32)) for i in range(2)]
            psUt = E(nc.psum_tensor("psUt", [128, 512], F32))
            psUs = psUt[:, 0:128]
            psDZ = E(nc.psum_tensor("psDZ", [128, 512], F32))
            psD = psDZ[0:1, 0:256]
            psUz = psDZ[:, 256:257]
            psPt = E(nc.psum_tensor("psPt", [128, 512], F32))
            psP0 = psPt[:, 0:256]
            psN = psPt[:, 256:512]

            for ec in range(FC):
                wout_ok = p.dma("sync", "wo", lambda e, ec=ec: e.dma_start(
                    out=wout_sb[:, ec * D:(ec + 1) * D],
                    in_=woutT_sc[ec * 128:(ec + 1) * 128, :]))

            o_ok = {}
            xadd_ok = 0
            for c in range(NCH):
                for h in range(H):
                    if drain:
                        drain()
                    i = c * H + h
                    w = []
                    if i >= 2:
                        w.append(("pe", ld_reuse[i - 2]))
                    chtoks = []
                    for nmx, (dst, srcm) in (("chq", (qch, q_sc)), ("chk", (kch, k_sc)), ("chv", (vch, v_sc))):
                        chtoks.append(p.dma("sync", nmx, lambda e, dst=dst, srcm=srcm, c=c, h=h, i=i: e.dma_start(
                            out=dst[i % 2][:, :].rearrange("p (i d) -> p i d", i=2),
                            in_=srcm[c * 256:(c + 1) * 256, h * 128:(h + 1) * 128]
                                .rearrange("(i p) d -> p i d", p=128)),
                            waits=(w if dst is qch else []), slot=i % 2))
                    w = list(chtoks)
                    if i >= 2:
                        w.append(("act", fm_ok[i - 2]))
                    for ii in range(2):
                        p.add("tensor", lambda e, i=i, ii=ii: e.transpose(
                            psT2[:, ii * 128:(ii + 1) * 128],
                            qch[i % 2][:, ii * 128:(ii + 1) * 128], idb[:, :]),
                            waits=(w if ii == 0 else []))
                    for ii in range(2):
                        p.add("tensor", lambda e, i=i, ii=ii: e.transpose(
                            psT2[:, 256 + ii * 128: 256 + (ii + 1) * 128],
                            kch[i % 2][:, ii * 128:(ii + 1) * 128], idb[:, :]),
                            incs=([("pe", 1)] if ii == 1 else []))
                    tr_ok = p.n("pe")
                    p.add("scalar", lambda e, i=i: e.copy(out=qfm[i % 2][:, :], in_=psT2[:, 0:256]),
                          waits=[("pe", tr_ok)])
                    p.add("scalar", lambda e, i=i: e.copy(out=kfm[i % 2][:, :], in_=psT2[:, 256:512]),
                          incs=[("act", 1)])
                    fm_ok[i] = p.n("act")
                    w = [("act", fm_ok[i])]
                    if i >= 1:
                        w.append(("dve", p2_ok[i - 1]))
                        w.append(("dve", onum_ok[i - 1]))
                    for mc in range(2):
                        p.add("tensor", lambda e, i=i, mc=mc: e.matmul(
                            psA[mc][:, 0:256], kfm[i % 2][:, mc * 128:(mc + 1) * 128],
                            qfm[i % 2][:, :], start=True, stop=True),
                            waits=(w if mc == 0 else []),
                            incs=([("pe", 1)] if mc == 1 else []))
                    sc_ok = p.n("pe")
                    if drain:
                        drain()
                    for mc in range(2):
                        p.add("vector", lambda e, mc=mc: e.tensor_tensor(
                            out=p2[:, mc * 256:(mc + 1) * 256],
                            in0=maska[:, mc * 256:(mc + 1) * 256], in1=psA[mc][:, 0:256], op=OP.mult),
                            waits=([("pe", sc_ok)] if mc == 0 else []),
                            incs=([("dve", 1)] if mc == 1 else []))
                    p2_ok[i] = p.n("dve")
                    w = [("dve", p2_ok[i]), ("act", sbf_ok[(c - 1, h)])]
                    if i >= 1:
                        w.append(("dve", den_ok[i - 1]))
                        w.append(("dve", supd_ok[i - 1]))
                    p.add("tensor", lambda e, i=i: e.matmul(
                        psD, onesb[:, 0:1], p2[:, 0:256], start=True, stop=False), waits=w)
                    p.add("tensor", lambda e, i=i: e.matmul(
                        psD, onesb[:, 0:1], p2[:, 256:512], start=False, stop=False))
                    p.add("tensor", lambda e, i=i, h=h: e.matmul(
                        psD, S_bf[:, h * 132 + 128:h * 132 + 129], qfm[i % 2][:, :],
                        start=False, stop=True), incs=[("pe", 1)])
                    dmm_ok = p.n("pe")
                    w = []
                    if i >= 1:
                        w.append(("dve", onum_ok[i - 1]))
                    if h == 0 and c >= 1:
                        w.append(("dve", proj_ok[FC - 1]))
                    p.add("tensor", lambda e, i=i, h=h: e.matmul(
                        psN[:, :], S_bf[:, h * 132:h * 132 + 128], qfm[i % 2][:, :],
                        start=True, stop=False), waits=w)
                    for mc in range(2):
                        p.add("tensor", lambda e, i=i, mc=mc: e.matmul(
                            psN[:, :], vch[i % 2][:, mc * 128:(mc + 1) * 128],
                            p2[:, mc * 256:(mc + 1) * 256], start=False, stop=(mc == 1)),
                            incs=([("pe", 1)] if mc == 1 else []))
                    nmm_ok = p.n("pe")
                    if drain:
                        drain()
                    p.add("vector", lambda e: e.tensor_scalar(
                        out=dsb, in0=psD, scalar1=1e-6, scalar2=None, op0=OP.max),
                        waits=[("pe", dmm_ok)], incs=[("dve", 1)])
                    den_ok[i] = p.n("dve")
                    p.add("vector", lambda e: e.tensor_copy(out=spare[:, :], in_=onesb[:, :]))
                    p.add("vector", lambda e: e.reciprocal(out=rden, in_=dsb),
                          incs=[("dve", 1)])
                    rd_ok = p.n("dve")
                    w = [("dve", rd_ok)]
                    p.add("tensor", lambda e: e.matmul(
                        psA[0][:, 256:512], onesrf[0:1, :], rden, start=True, stop=True),
                        waits=w, incs=[("pe", 1)])
                    bmm_ok = p.n("pe")
                    wb = [("pe", bmm_ok)]
                    if i >= 1:
                        wb.append(("dve", onum_ok[i - 1]))
                    p.add("scalar", lambda e: e.copy(out=bsb[:, :], in_=psA[0][:, 256:512]),
                          waits=wb, incs=[("act", 1)])
                    bsb_ok[i] = ("act", p.n("act"))
                    if drain:
                        drain()
                    p.add("vector", lambda e, h=h: e.tensor_tensor(
                        out=o_buf[:, h * 256:(h + 1) * 256], in0=bsb[:, :], in1=psN[:, :], op=OP.mult),
                        waits=[bsb_ok[i], ("pe", nmm_ok)], incs=[("dve", 1)])
                    onum_ok[i] = p.n("dve")
                    o_ok[(c, h)] = p.n("dve")
                    w = [("dve", onum_ok[i])]
                    if i >= 1:
                        w.append(("dve", supd_ok[i - 1]))
                    for mc in range(2):
                        p.add("tensor", lambda e, i=i, mc=mc: e.matmul(
                            psUs, kch[i % 2][:, mc * 128:(mc + 1) * 128],
                            vch[i % 2][:, mc * 128:(mc + 1) * 128],
                            start=(mc == 0), stop=(mc == 1)),
                            waits=(w if mc == 0 else []))
                        p.add("tensor", lambda e, i=i, mc=mc: e.matmul(
                            psUz, kch[i % 2][:, mc * 128:(mc + 1) * 128],
                            onesb[:, 0:1], start=(mc == 0), stop=(mc == 1)),
                            incs=([("pe", 1)] if mc == 1 else []))
                    umm_ok = p.n("pe")
                    ld_reuse[i] = umm_ok
                    p.add("vector", lambda e, h=h: e.tensor_tensor(
                        out=S_run[:, h * 132:h * 132 + 128],
                        in0=S_run[:, h * 132:h * 132 + 128], in1=psUs, op=OP.add),
                        waits=[("pe", umm_ok)])
                    p.add("vector", lambda e, h=h: e.tensor_tensor(
                        out=S_run[:, h * 132 + 128:h * 132 + 129],
                        in0=S_run[:, h * 132 + 128:h * 132 + 129], in1=psUz, op=OP.add),
                        incs=[("dve", 1)])
                    supd_ok[i] = p.n("dve")
                    p.add("scalar", lambda e, h=h: e.copy(
                        out=S_bf[:, h * 132:h * 132 + 129], in_=S_run[:, h * 132:h * 132 + 129]),
                        waits=[("dve", supd_ok[i])], incs=[("act", 1)])
                    sbf_ok[(c, h)] = p.n("act")

                w_all_o = [("dve", o_ok[(c, H - 1)]), wout_ok]
                for hc in range(FC):
                    w = list(w_all_o) if hc == 0 else []
                    if hc >= 1:
                        w.append(("dve", proj_ok[hc - 1]))
                    for ec in range(FC):
                        p.add("tensor", lambda e, hc=hc, ec=ec: e.matmul(
                            psP0, wout_sb[:, ec * D + hc * 128: ec * D + (hc + 1) * 128],
                            o_buf[:, ec * 256:(ec + 1) * 256],
                            start=(ec == 0), stop=(ec == FC - 1)),
                            waits=(w if ec == 0 else []),
                            incs=([("pe", 1)] if ec == FC - 1 else []))
                    pmm_ok = p.n("pe")
                    p.add("vector", lambda e, hc=hc, c=c: e.tensor_tensor(
                        out=X[:, hc * T + c * 256: hc * T + (c + 1) * 256],
                        in0=X[:, hc * T + c * 256: hc * T + (c + 1) * 256],
                        in1=psP0, op=OP.add),
                        waits=[("pe", pmm_ok)], incs=[("dve", 1)])
                    proj_ok[hc] = p.n("dve")
                ld_reuse[c * H + H - 1] = p.n("pe")
            if "sdbg_sc" in DEBUG_SCRATCH:
                p.dma("gpsimd", "dbg", lambda e: e.dma_start(out=sdbg_sc, in_=S_run[:, :]),
                      waits=[("dve", p.n("dve")), ("act", p.n("act")), ("pe", p.n("pe"))])
                p.dma("gpsimd", "dbg", lambda e: e.dma_start(out=odbg_sc, in_=o_buf[:, :]))
            sc.close()

        st_mm = {}
        fm_ok = {}
        p2_ok = {}
        den_ok = {}
        onum_ok = {}
        bsb_ok = {}
        supd_ok = {}
        ld_reuse = {}
        proj_ok = {}

        class Decoder:
            def __init__(self):
                self.sc = ExitStack()
                E = self.sc.enter_context
                self.pk = [E(nc.sbuf_tensor(f"pk{i}", [128, 1024], I32)) for i in range(2)]
                self.m32 = [E(nc.sbuf_tensor(f"m32{i}", [128, 1024], I32)) for i in range(2)]
                self.s32 = [E(nc.sbuf_tensor(f"s32{i}", [128, 1024], I32)) for i in range(2)]
                self.c32 = [E(nc.sbuf_tensor(f"c32{i}", [128, 1024], I32)) for i in range(2)]
                self.dec = [E(nc.sbuf_tensor(f"dec{i}", [128, 2048], BF16)) for i in range(2)]
                self.scf = [E(nc.sbuf_tensor(f"scf{i}", [128, 64], F32)) for i in range(2)]
                self.scb = [E(nc.sbuf_tensor(f"scb{i}", [128, 64], BF16)) for i in range(2)]
                self.wstg = [E(nc.sbuf_tensor(f"dstg{i}", [128, 512], BF16)) for i in range(2)]
                self.psDC = [E(nc.psum_tensor(f"psDC{i}", [128, 1024], BF16)) for i in range(2)]
                self.n = 48
                self.e1 = 0
                self.e2 = 0
                self.e3 = 0
                self.u_ok = {}
                self.s_ok = {}
                self.c_ok = {}
                self.or_ok = {}
                self.scale_ok = {}
                self.scv = {}
                self.cvt_done = {}
                self.dcvt = {}
                self.dscale_prev = {}
                self.trall = {}
                self.ev_done = {}
                self.wr_done = {}
                self.wr_unit = {}

            def stage1(self):
                i = self.e1
                if i >= self.n:
                    return
                w = [("dve", self.s_ok.get(i - 2, 0))]
                tok = p.dma("sync", "pk", lambda e, i=i: e.dma_start(
                    out=self.pk[i % 2][:, :], in_=mypk_in[i * 128:(i + 1) * 128, :]),
                    waits=[x for x in w if x[1] > 0], slot=i % 2)
                slot = i % 2
                w2 = [("act", self.cvt_done.get(i - 2, 0))]
                sl = p.dma("sync", "scg", lambda e, i=i, slot=slot: e.dma_start(
                    out=self.scf[slot][:, 0:64], in_=mysc_in[i * 128:(i + 1) * 128, :]),
                    waits=[x for x in w2 if x[1] > 0], slot=slot)
                w3 = [sl, ("dve", self.scale_ok.get(i - 2, 0))]
                p.add("scalar", lambda e, slot=slot: e.copy(
                    out=self.scb[slot][:, 0:64], in_=self.scf[slot][:, 0:64]),
                    waits=[x for x in w3 if not (isinstance(x[1], int) and x[1] <= 0)],
                    incs=[("act", 1)])
                self.cvt_done[i] = p.n("act")
                self.scv[i] = (self.scb[slot][:, 0:64], ("act", self.cvt_done[i]))
                p.add("gpsimd", lambda e, i=i: e.tensor_scalar(
                    out=self.pk[i % 2][:, :], in0=self.pk[i % 2][:, :],
                    scalar1=4097, scalar2=None, op0=OP.mult),
                    waits=[tok], incs=[("pool", 1)])
                self.u_ok[i] = p.n("pool")
                w = [("pool", self.u_ok[i])]
                p.add("vector", lambda e, i=i: e.tensor_scalar(
                    out=self.m32[i % 2][:, :], in0=self.pk[i % 2][:, :],
                    scalar1=0x00070007, scalar2=None, op0=OP.bitwise_and),
                    waits=[x for x in w if x[1] > 0])
                p.add("vector", lambda e, i=i: e.tensor_scalar(
                    out=self.s32[i % 2][:, :], in0=self.pk[i % 2][:, :],
                    scalar1=0x00080008, scalar2=12,
                    op0=OP.bitwise_and, op1=OP.logical_shift_left),
                    incs=[("dve", 1)])
                self.s_ok[i] = p.n("dve")
                self.e1 = i + 1

            def stage2(self):
                i = self.e2
                if i >= self.n or i >= self.e1:
                    return
                p.add("vector", lambda e, i=i: e.tensor_scalar(
                    out=self.c32[i % 2][:, :].bitcast(I16), in0=self.m32[i % 2][:, :].bitcast(I16),
                    scalar1=0x40, scalar2=0x3F00, op0=OP.mult, op1=OP.add),
                    incs=[("dve", 1)])
                self.c_ok[i] = p.n("dve")
                self.e2 = i + 1

            def stage3(self):
                i = self.e3
                if i >= self.n or i >= self.e2:
                    return
                p.add("vector", lambda e, i=i: e.tensor_scalar(
                    out=self.m32[i % 2][:, :].bitcast(BF16), in0=self.c32[i % 2][:, :].bitcast(BF16),
                    scalar1=-1.0, scalar2=0.0, op0=OP.add, op1=OP.min))
                p.add("vector", lambda e, i=i: e.tensor_tensor(
                    out=self.c32[i % 2][:, :].bitcast(BF16), in0=self.m32[i % 2][:, :].bitcast(BF16),
                    in1=self.c32[i % 2][:, :].bitcast(BF16), op=OP.add))
                p.add("vector", lambda e, i=i: e.tensor_tensor(
                    out=self.c32[i % 2][:, :], in0=self.s32[i % 2][:, :], in1=self.c32[i % 2][:, :],
                    op=OP.bitwise_or), incs=[("dve", 1)])
                self.or_ok[i] = p.n("dve")
                scsl, stok = self.scv[i]
                sc3 = scsl.unsqueeze(2).broadcast_to([128, 64, 32])
                w = [stok, ("pe", self.trall.get(i - 2, 0))]
                p.add("vector", lambda e, i=i, sc3=sc3: e.tensor_tensor(
                    out=self.dec[i % 2][:, :].rearrange("p (b k) -> p b k", b=64),
                    in0=sc3, in1=self.c32[i % 2][:, :].bitcast(BF16).rearrange("p (b k) -> p b k", b=64),
                    op=OP.mult),
                    waits=[x for x in w if not (isinstance(x[1], int) and x[1] <= 0)],
                    incs=[("dve", 1)])
                self.scale_ok[i] = p.n("dve")
                dstt = ccw_in
                rowblk = i
                for g in range(4):
                    lv = i * 4 + g
                    w = [("dve", self.scale_ok[i]), CONSTS_TOK] if g == 0 else []
                    if lv >= 2:
                        w.append(("act", self.ev_done[lv - 2]))
                    for j in range(4):
                        kc = g * 4 + j
                        p.add("tensor", lambda e, i=i, g=g, j=j, kc=kc: e.transpose(
                            self.psDC[g % 2][:, j * 128:(j + 1) * 128],
                            self.dec[i % 2][:, kc * 128:(kc + 1) * 128], idb[:, :]),
                            waits=(w if j == 0 else []),
                            incs=([("pe", 1)] if j == 3 else []))
                    tr_ok = p.n("pe")
                    w2 = [("pe", tr_ok)]
                    if lv >= 2:
                        w2.append(self.wr_done[lv - 2])
                    p.add("scalar", lambda e, g=g: e.copy(
                        out=self.wstg[g % 2][:, :], in_=self.psDC[g % 2][:, 0:512]),
                        waits=w2, incs=[("act", 1)])
                    self.ev_done[lv] = p.n("act")
                    self.wr_done[lv] = p.dma("scalar", "dw",
                        lambda e, dstt=dstt, rowblk=rowblk, g=g: e.dma_start(
                            out=dstt[rowblk * 128:(rowblk + 1) * 128, g * 512:(g + 1) * 512],
                            in_=self.wstg[g % 2][:, :]),
                        waits=[("act", self.ev_done[lv])], slot=lv % 2)
                self.trall[i] = p.n("pe")
                self.e3 = i + 1

            def drain(self, k):
                for _ in range(k):
                    self.stage3()
                    self.stage2()
                    self.stage1()

            def flush(self):
                while self.e3 < self.n:
                    self.stage1()
                    self.stage2()
                    self.stage3()
                nmax = self.n * 4 - 1
                w = [self.wr_done[nmax], self.wr_done[nmax - 1]]
                p.add("gpsimd", lambda e: e.collective_compute(
                    "AllGather", OP.bypass,
                    replica_groups=[list(range(NCORES))],
                    ins=[ccw_in], outs=[ccd_out]),
                    waits=w, incs=[("cc", 1)])
                self.cc_tok = ("cc", p.n("cc"))


        def ph_attn_b(Xln):
            sc = ExitStack()
            E = sc.enter_context
            gsc_sb = E(nc.sbuf_tensor("gsc_sb", [128, 256], BF16))
            p1s = E(nc.sbuf_tensor("p1s", [128, 2 * 256], BF16))
            p1n = E(nc.sbuf_tensor("p1n", [128, 2 * 256], BF16))
            p2g = E(nc.sbuf_tensor("p2g", [128, 768], BF16))
            nmx = E(nc.sbuf_tensor("nmx", [128, 2], F32))
            denb = E(nc.sbuf_tensor("denb", [128, 2], F32))
            rdb = E(nc.sbuf_tensor("rdb", [128, 2], F32))
            abdbg = E(nc.sbuf_tensor("abdbg", [128, 2048], F32))
            psS1 = [E(nc.psum_tensor(f"psS1{i}", [128, 512], F32)) for i in range(2)]
            psP2 = E(nc.psum_tensor("psP2", [128, 1024], BF16))
            psO = E(nc.psum_tensor("psO", [128, 512], F32))

            gsc_ok = 0
            exp_ok = {}
            evac_ok = 0
            xadd_ok = 0
            prev_sc = 0
            for s in range(NSEG):
                for h in range(H):
                    i = s * H + h
                    par = h % 2
                    base = h * T + s * 512 + par

                    def gv(lo, cnt, base=base):
                        st = base + 2 * lo
                        return Xln[:, st: st + 2 * cnt - 1: 2]

                    w = [("pe", prev_sc)] if i >= 1 else []
                    p.add("vector", lambda e, gv=gv: e.tensor_scalar(
                        out=gsc_sb[:, :], in0=gv(0, 256), scalar1=BSCALE, scalar2=None, op0=OP.mult),
                        waits=w, incs=[("dve", 1)])
                    gsc_ok = p.n("dve")
                    w = [("dve", gsc_ok)]
                    if i >= 1:
                        w.append(("act", exp_ok[i - 1]))
                    for lc in range(2):
                        p.add("tensor", lambda e, lc=lc, gv=gv: e.matmul(
                            psS1[lc][:, 0:256], gsc_sb[:, lc * 128:(lc + 1) * 128],
                            gv(0, 256), start=True, stop=True),
                            waits=(w if lc == 0 else []),
                            incs=([("pe", 1)] if lc == 1 else []))
                    sc_ok = p.n("pe")
                    prev_sc = sc_ok
                    for lc in range(2):
                        p.add("vector", lambda e, lc=lc: e.reduce_max(
                            out=nmx[:, lc:lc + 1], in_=psS1[lc][:, 0:256], axis=AX.X, negate=True),
                            waits=([("pe", sc_ok)] if lc == 0 else []),
                            incs=([("dve", 1)] if lc == 1 else []))
                    nmx_ok = p.n("dve")
                    for lc in range(2):
                        p.add("scalar", lambda e, lc=lc: e.activation(
                            out=p1s[:, lc * 256:(lc + 1) * 256], in_=psS1[lc][:, 0:256],
                            func=AF.Exp, bias=nmx[:, lc:lc + 1]),
                            waits=([("dve", nmx_ok)] if lc == 0 else []),
                            incs=([("act", 1)] if lc == 1 else []))
                    exp_ok[i] = p.n("act")
                    for lc in range(2):
                        p.add("vector", lambda e, lc=lc: e.reduce_sum(
                            out=denb[:, lc:lc + 1], in_=p1s[:, lc * 256:(lc + 1) * 256], axis=AX.X),
                            waits=([("act", exp_ok[i])] if lc == 0 else []))
                    p.add("vector", lambda e: e.reciprocal(out=rdb[:, :], in_=denb[:, :]))
                    p.add("vector", lambda e: e.tensor_copy(out=nmx[:, :], in_=denb[:, :]))
                    p.add("vector", lambda e: e.tensor_copy(out=nmx[:, :], in_=denb[:, :]))
                    for lc in range(2):
                        p.add("vector", lambda e, lc=lc: e.tensor_scalar(
                            out=p1n[:, lc * 256:(lc + 1) * 256], in0=p1s[:, lc * 256:(lc + 1) * 256],
                            scalar1=rdb[:, lc:lc + 1], scalar2=None, op0=OP.mult),
                            incs=([("dve", 1)] if lc == 1 else []))
                    p1n_ok = p.n("dve")
                    w = [("dve", p1n_ok)]
                    if i >= 1:
                        w.append(("act", evac_ok))
                    for mc in range(2):
                        for lc in range(2):
                            p.add("tensor", lambda e, mc=mc, lc=lc: e.transpose(
                                psP2[:, mc * 256 + lc * 128: mc * 256 + (lc + 1) * 128],
                                p1n[:, lc * 256 + mc * 128: lc * 256 + (mc + 1) * 128], idb[:, :]),
                                waits=(w if (mc == 0 and lc == 0) else []))
                    for mc in range(2):
                        p.add("tensor", lambda e, mc=mc, gv=gv: e.transpose(
                            psP2[:, 512 + mc * 128: 512 + (mc + 1) * 128],
                            gv(mc * 128, 128), idb[:, :]),
                            incs=([("pe", 1)] if mc == 1 else []))
                    tr_ok = p.n("pe")
                    w2 = [("pe", tr_ok)]
                    if i >= 1:
                        w2.append(("dve", xadd_ok))
                    p.add("scalar", lambda e: e.copy(out=p2g[:, :], in_=psP2[:, 0:768]),
                          waits=w2, incs=[("act", 1)])
                    evac_ok = p.n("act")
                    for mc in range(2):
                        wav = [("act", evac_ok)]
                        if i >= 1:
                            wav.append(("dve", xadd_ok))
                        p.add("tensor", lambda e, mc=mc: e.matmul(
                            psO[:, 0:256], p2g[:, 512 + mc * 128: 512 + (mc + 1) * 128],
                            p2g[:, mc * 256:(mc + 1) * 256], start=(mc == 0), stop=(mc == 1)),
                            waits=(wav if mc == 0 else []),
                            incs=([("pe", 1)] if mc == 1 else []))
                    av_ok = p.n("pe")
                    if i == 0 and "abdbg_sc" in DEBUG_SCRATCH:
                        p.add("vector", lambda e: e.tensor_copy(out=abdbg[:, 0:256], in_=gsc_sb[:, :]),
                              waits=[("pe", av_ok)])
                        p.add("vector", lambda e: e.tensor_copy(out=abdbg[:, 256:512], in_=p1n[:, 0:256]))
                        p.add("vector", lambda e: e.tensor_copy(out=abdbg[:, 512:768], in_=p1n[:, 256:512]))
                        p.add("vector", lambda e: e.tensor_copy(out=abdbg[:, 768:770], in_=denb[:, :]))
                        p.add("vector", lambda e: e.tensor_copy(out=abdbg[:, 770:772], in_=rdb[:, :]))
                        p.add("vector", lambda e: e.tensor_copy(out=abdbg[:, 772:1540], in_=p2g[:, :]))
                        p.add("vector", lambda e: e.tensor_copy(out=abdbg[:, 1540:1796], in_=psO[:, 0:256]),
                              incs=[("dve", 1)])
                        p.dma("gpsimd", "dbg", lambda e: e.dma_start(out=abdbg_sc[:, :], in_=abdbg[:, :]),
                              waits=[("dve", p.n("dve"))])
                    p.add("vector", lambda e, base=base: e.tensor_tensor(
                        out=Xs_strided(base), in0=Xs_strided(base), in1=psO[:, 0:256], op=OP.add),
                        waits=[("pe", av_ok)], incs=[("dve", 1)])
                    xadd_ok = p.n("dve")
            sc.close()

        def Xs_strided(base):
            return X[:, base: base + 511: 2]

        def ph_out():
            sc = ExitStack()
            E = sc.enter_context
            yst = [E(nc.sbuf_tensor(f"yst{i}", [128, D], F32)) for i in range(2)]
            psY = [E(nc.psum_tensor(f"psY{i}", [128, 512], F32)) for i in range(2)]
            evy = {}
            wry = {}
            for tc in range(TC):
                for g in range(4):
                    lin = tc * 4 + g
                    w = []
                    if lin >= 2:
                        w.append(("dve", evy[lin - 2]))
                    if g == 0 and tc >= 2:
                        w.append(wry[tc - 2])
                    for j in range(4):
                        fc = g * 4 + j
                        p.add("tensor", lambda e, tc=tc, g=g, j=j, fc=fc: e.transpose(
                            psY[g % 2][:, j * 128:(j + 1) * 128],
                            X[:, fc * T + tc * 128: fc * T + tc * 128 + 128], idf[:, :]),
                            waits=(w if j == 0 else []),
                            incs=([("pe", 1)] if j == 3 else []))
                    pe_ok = p.n("pe")
                    p.add("vector", lambda e, tc=tc, g=g: e.tensor_copy(
                        out=yst[tc % 2][:, :].rearrange("p (g j f) -> p g j f", g=4, j=4)[:, g, :, :],
                        in_=psY[g % 2][:, :].rearrange("p (j f) -> p j f", j=4)),
                        waits=[("pe", pe_ok)], incs=[("dve", 1)])
                    evy[lin] = p.n("dve")
                wry[tc] = p.dma("gpsimd", "yout", lambda e, tc=tc: e.dma_start(
                    out=y_out[tc * 128:(tc + 1) * 128, :], in_=yst[tc % 2][:, :]),
                    waits=[("dve", evy[tc * 4 + 3])], slot=tc % 2)
            sc.close()

        def ph_ffn(Xln, tagn):
            sc = ExitStack()
            E = sc.enter_context
            wgb = [E(nc.sbuf_tensor(f"wgb{tagn}{i}", [128, 2048], BF16)) for i in range(2)]
            wub = [E(nc.sbuf_tensor(f"wub{tagn}{i}", [128, 2048], BF16)) for i in range(2)]
            wdb = [E(nc.sbuf_tensor(f"wdb{tagn}{i}", [128, 2048], BF16)) for i in range(2)]
            inter = E(nc.sbuf_tensor(f"inter{tagn}", [128, 16 * 1024], BF16))
            sqb = E(nc.sbuf_tensor(f"sqb{tagn}", [128, 1024], F32))
            cgb = E(nc.sbuf_tensor(f"cgb{tagn}", [128, 1024], F32))
            sgb = E(nc.sbuf_tensor(f"sgb{tagn}", [128, 1024], F32))
            glb = E(nc.sbuf_tensor(f"glb{tagn}", [128, 1024], BF16))
            usb = E(nc.sbuf_tensor(f"usb{tagn}", [128, 1024], BF16))
            psG = E(nc.psum_tensor(f"psG{tagn}", [128, 1024], F32))
            psUp = E(nc.psum_tensor(f"psU{tagn}", [128, 1024], F32))
            psDn = [E(nc.psum_tensor(f"psDn{tagn}{i}", [128, 512], F32)) for i in range(2)]

            gld = {}
            uld = {}
            dld = {}
            gmm = {}
            umm = {}
            inter_ok = {}
            gl_ok = {}
            ue_ok = {}
            dadd = {}
            for ng in range(8):
                for ncx in range(16):
                    nb = ng * 16 + ncx
                    w = [("pe", gmm.get(nb - 2, 0))]
                    if nb == 0:
                        w.append(dec_obj.cc_tok)
                    gld[nb] = p.dma("sync", "wgb", lambda e, nb=nb: e.dma_start(
                        out=wgb[nb % 2][:, :], in_=ccd_out[nb * 128:(nb + 1) * 128, :]),
                        waits=[x for x in w if not (isinstance(x[1], int) and x[1] <= 0)], slot=nb % 2)
                    w = [("pe", umm.get(nb - 2, 0))]
                    uld[nb] = p.dma("sync", "wub", lambda e, nb=nb: e.dma_start(
                        out=wub[nb % 2][:, :], in_=ccd_out[(128 + nb) * 128:(129 + nb) * 128, :]),
                        waits=[x for x in w if x[1] > 0], slot=nb % 2)
                    w = [gld[nb]]
                    if nb >= 1:
                        w.append(("dve", gl_ok[nb - 1]))
                    for kc in range(16):
                        for half in range(2):
                            p.add("tensor", lambda e, nb=nb, kc=kc, half=half: e.matmul(
                                psG[:, half * 512:(half + 1) * 512],
                                wgb[nb % 2][:, kc * 128:(kc + 1) * 128],
                                Xln[:, kc * T + half * 512: kc * T + (half + 1) * 512],
                                start=(kc == 0), stop=(kc == 15)),
                                waits=(w if kc == 0 and half == 0 else []),
                                incs=([("pe", 1)] if kc == 15 and half == 1 else []))
                    gmm[nb] = p.n("pe")
                    w = [uld[nb]]
                    if nb >= 1:
                        w.append(("act", ue_ok[nb - 1]))
                    for kc in range(16):
                        for half in range(2):
                            p.add("tensor", lambda e, nb=nb, kc=kc, half=half: e.matmul(
                                psUp[:, half * 512:(half + 1) * 512],
                                wub[nb % 2][:, kc * 128:(kc + 1) * 128],
                                Xln[:, kc * T + half * 512: kc * T + (half + 1) * 512],
                                start=(kc == 0), stop=(kc == 15)),
                                waits=(w if kc == 0 and half == 0 else []),
                                incs=([("pe", 1)] if kc == 15 and half == 1 else []))
                    umm[nb] = p.n("pe")
                    w = [("pe", gmm[nb])]
                    if nb >= 1:
                        w.append(("dve", cg_prev[nb - 1]))
                    p.add("scalar", lambda e: e.activation(
                        out=sqb[:, :], in_=psG[:, :], func=AF.Square),
                        waits=w, incs=[("act", 1)])
                    sq_ok = p.n("act")
                    p.add("vector", lambda e: e.tensor_scalar(
                        out=cgb[:, :], in0=sqb[:, :], scalar1=GELU_A, scalar2=1.0,
                        op0=OP.mult, op1=OP.add),
                        waits=[("act", sq_ok)])
                    p.add("vector", lambda e: e.tensor_tensor(
                        out=cgb[:, :], in0=cgb[:, :], in1=psG[:, :], op=OP.mult),
                        incs=[("dve", 1)])
                    cg_ok = p.n("dve")
                    cg_prev[nb] = cg_ok
                    w = [("dve", cg_ok)]
                    if nb >= 1:
                        w.append(("dve", gl_ok[nb - 1]))
                    p.add("scalar", lambda e: e.activation(
                        out=sgb[:, :], in_=cgb[:, :], func=AF.Sigmoid, scale=np.float32(GELU_S)),
                        waits=w, incs=[("act", 1)])
                    sg_ok = p.n("act")
                    w = [("pe", umm[nb])]
                    if nb >= 1:
                        w.append(("pool", inter_ok[nb - 1]))
                    p.add("scalar", lambda e: e.copy(out=usb[:, :], in_=psUp[:, :]),
                          waits=w, incs=[("act", 1)])
                    ue_ok[nb] = p.n("act")
                    w = [("act", sg_ok)]
                    if nb >= 1:
                        w.append(("pool", inter_ok[nb - 1]))
                    p.add("vector", lambda e: e.tensor_tensor(
                        out=glb[:, :], in0=sgb[:, :], in1=psG[:, :], op=OP.mult),
                        waits=w, incs=[("dve", 1)])
                    gl_ok[nb] = p.n("dve")
                    w = [("act", ue_ok[nb]), ("dve", gl_ok[nb])]
                    if ng >= 1 and ncx == 0:
                        w.append(("pe", dmmprev[(ng - 1, 15)]))
                    p.add("gpsimd", lambda e, ncx=ncx: e.tensor_tensor(
                        out=inter[:, ncx * 1024:(ncx + 1) * 1024], in0=usb[:, :], in1=glb[:, :],
                        op=OP.mult),
                        waits=w, incs=[("pool", 1)])
                    inter_ok[nb] = p.n("pool")
                for hc in range(16):
                    if hc >= 2:
                        w = [("pe", dmmprev[(ng, hc - 2)])]
                    elif ng >= 1:
                        w = [("pe", dmmprev[(ng - 1, 14 + hc)])]
                    else:
                        w = []
                    dld[(ng, hc)] = p.dma("sync", "wdb", lambda e, ng=ng, hc=hc: e.dma_start(
                        out=wdb[hc % 2][:, :],
                        in_=ccd_out[(256 + hc * 8 + ng) * 128:(256 + hc * 8 + ng + 1) * 128, :]),
                        waits=w, slot=hc % 2)
                    w = [dld[(ng, hc)], ("pool", inter_ok[ng * 16 + 15])]
                    for th in range(2):
                        wth = list(w) if th == 0 else []
                        idx = hc * 2 + th
                        if idx >= 2:
                            wth.append(("dve", dadd[(ng, idx - 2)]))
                        elif ng >= 1:
                            wth.append(("dve", dadd[(ng - 1, 30 + idx)]))
                        for ncx in range(16):
                            p.add("tensor", lambda e, ng=ng, hc=hc, th=th, ncx=ncx: e.matmul(
                                psDn[(hc * 2 + th) % 2][:, :],
                                wdb[hc % 2][:, ncx * 128:(ncx + 1) * 128],
                                inter[:, ncx * 1024 + th * 512: ncx * 1024 + (th + 1) * 512],
                                start=(ncx == 0), stop=(ncx == 15)),
                                waits=(wth if ncx == 0 else []),
                                incs=([("pe", 1)] if ncx == 15 else []))
                        dm_ok = p.n("pe")
                        p.add("vector", lambda e, hc=hc, th=th, ng=ng, idx=idx: e.tensor_tensor(
                            out=X[:, hc * T + th * 512: hc * T + (th + 1) * 512],
                            in0=X[:, hc * T + th * 512: hc * T + (th + 1) * 512],
                            in1=psDn[idx % 2][:, :], op=OP.add),
                            waits=[("pe", dm_ok)], incs=[("dve", 1)])
                        dadd[(ng, idx)] = p.n("dve")
                    dmmprev[(ng, hc)] = p.n("pe")
            sc.close()
        dmmprev = {}
        cg_prev = {}

        def ph_attn_b(Xln):
            sc = ExitStack()
            E = sc.enter_context
            gsc_sb = E(nc.sbuf_tensor("gsc_sb", [128, 256], BF16))
            p1s = E(nc.sbuf_tensor("p1s", [128, 2 * 256], BF16))
            p1n = E(nc.sbuf_tensor("p1n", [128, 2 * 256], BF16))
            p2g = E(nc.sbuf_tensor("p2g", [128, 768], BF16))
            nmx = E(nc.sbuf_tensor("nmx", [128, 2], F32))
            denb = E(nc.sbuf_tensor("denb", [128, 2], F32))
            rdb = E(nc.sbuf_tensor("rdb", [128, 2], F32))
            abdbg = E(nc.sbuf_tensor("abdbg", [128, 2048], F32))
            psS1 = [E(nc.psum_tensor(f"psS1{i}", [128, 512], F32)) for i in range(2)]
            psP2 = E(nc.psum_tensor("psP2", [128, 1024], BF16))
            psO = E(nc.psum_tensor("psO", [128, 512], F32))

            gsc_ok = 0
            exp_ok = {}
            evac_ok = 0
            xadd_ok = 0
            prev_sc = 0
            for s in range(NSEG):
                for h in range(H):
                    i = s * H + h
                    par = h % 2
                    base = h * T + s * 512 + par

                    def gv(lo, cnt, base=base):
                        st = base + 2 * lo
                        return Xln[:, st: st + 2 * cnt - 1: 2]

                    w = [("pe", prev_sc)] if i >= 1 else []
                    p.add("vector", lambda e, gv=gv: e.tensor_scalar(
                        out=gsc_sb[:, :], in0=gv(0, 256), scalar1=BSCALE, scalar2=None, op0=OP.mult),
                        waits=w, incs=[("dve", 1)])
                    gsc_ok = p.n("dve")
                    w = [("dve", gsc_ok)]
                    if i >= 1:
                        w.append(("act", exp_ok[i - 1]))
                    for lc in range(2):
                        p.add("tensor", lambda e, lc=lc, gv=gv: e.matmul(
                            psS1[lc][:, 0:256], gsc_sb[:, lc * 128:(lc + 1) * 128],
                            gv(0, 256), start=True, stop=True),
                            waits=(w if lc == 0 else []),
                            incs=([("pe", 1)] if lc == 1 else []))
                    sc_ok = p.n("pe")
                    prev_sc = sc_ok
                    for lc in range(2):
                        p.add("vector", lambda e, lc=lc: e.reduce_max(
                            out=nmx[:, lc:lc + 1], in_=psS1[lc][:, 0:256], axis=AX.X, negate=True),
                            waits=([("pe", sc_ok)] if lc == 0 else []),
                            incs=([("dve", 1)] if lc == 1 else []))
                    nmx_ok = p.n("dve")
                    for lc in range(2):
                        p.add("scalar", lambda e, lc=lc: e.activation(
                            out=p1s[:, lc * 256:(lc + 1) * 256], in_=psS1[lc][:, 0:256],
                            func=AF.Exp, bias=nmx[:, lc:lc + 1]),
                            waits=([("dve", nmx_ok)] if lc == 0 else []),
                            incs=([("act", 1)] if lc == 1 else []))
                    exp_ok[i] = p.n("act")
                    for lc in range(2):
                        p.add("vector", lambda e, lc=lc: e.reduce_sum(
                            out=denb[:, lc:lc + 1], in_=p1s[:, lc * 256:(lc + 1) * 256], axis=AX.X),
                            waits=([("act", exp_ok[i])] if lc == 0 else []))
                    p.add("vector", lambda e: e.reciprocal(out=rdb[:, :], in_=denb[:, :]))
                    p.add("vector", lambda e: e.tensor_copy(out=nmx[:, :], in_=denb[:, :]))
                    p.add("vector", lambda e: e.tensor_copy(out=nmx[:, :], in_=denb[:, :]))
                    for lc in range(2):
                        p.add("vector", lambda e, lc=lc: e.tensor_scalar(
                            out=p1n[:, lc * 256:(lc + 1) * 256], in0=p1s[:, lc * 256:(lc + 1) * 256],
                            scalar1=rdb[:, lc:lc + 1], scalar2=None, op0=OP.mult),
                            incs=([("dve", 1)] if lc == 1 else []))
                    p1n_ok = p.n("dve")
                    w = [("dve", p1n_ok)]
                    if i >= 1:
                        w.append(("act", evac_ok))
                    for mc in range(2):
                        for lc in range(2):
                            p.add("tensor", lambda e, mc=mc, lc=lc: e.transpose(
                                psP2[:, mc * 256 + lc * 128: mc * 256 + (lc + 1) * 128],
                                p1n[:, lc * 256 + mc * 128: lc * 256 + (mc + 1) * 128], idb[:, :]),
                                waits=(w if (mc == 0 and lc == 0) else []))
                    for mc in range(2):
                        p.add("tensor", lambda e, mc=mc, gv=gv: e.transpose(
                            psP2[:, 512 + mc * 128: 512 + (mc + 1) * 128],
                            gv(mc * 128, 128), idb[:, :]),
                            incs=([("pe", 1)] if mc == 1 else []))
                    tr_ok = p.n("pe")
                    w2 = [("pe", tr_ok)]
                    if i >= 1:
                        w2.append(("dve", xadd_ok))
                    p.add("scalar", lambda e: e.copy(out=p2g[:, :], in_=psP2[:, 0:768]),
                          waits=w2, incs=[("act", 1)])
                    evac_ok = p.n("act")
                    for mc in range(2):
                        wav = [("act", evac_ok)]
                        if i >= 1:
                            wav.append(("dve", xadd_ok))
                        p.add("tensor", lambda e, mc=mc: e.matmul(
                            psO[:, 0:256], p2g[:, 512 + mc * 128: 512 + (mc + 1) * 128],
                            p2g[:, mc * 256:(mc + 1) * 256], start=(mc == 0), stop=(mc == 1)),
                            waits=(wav if mc == 0 else []),
                            incs=([("pe", 1)] if mc == 1 else []))
                    av_ok = p.n("pe")
                    if i == 0 and "abdbg_sc" in DEBUG_SCRATCH:
                        p.add("vector", lambda e: e.tensor_copy(out=abdbg[:, 0:256], in_=gsc_sb[:, :]),
                              waits=[("pe", av_ok)])
                        p.add("vector", lambda e: e.tensor_copy(out=abdbg[:, 256:512], in_=p1n[:, 0:256]))
                        p.add("vector", lambda e: e.tensor_copy(out=abdbg[:, 512:768], in_=p1n[:, 256:512]))
                        p.add("vector", lambda e: e.tensor_copy(out=abdbg[:, 768:770], in_=denb[:, :]))
                        p.add("vector", lambda e: e.tensor_copy(out=abdbg[:, 770:772], in_=rdb[:, :]))
                        p.add("vector", lambda e: e.tensor_copy(out=abdbg[:, 772:1540], in_=p2g[:, :]))
                        p.add("vector", lambda e: e.tensor_copy(out=abdbg[:, 1540:1796], in_=psO[:, 0:256]),
                              incs=[("dve", 1)])
                        p.dma("gpsimd", "dbg", lambda e: e.dma_start(out=abdbg_sc[:, :], in_=abdbg[:, :]),
                              waits=[("dve", p.n("dve"))])
                    p.add("vector", lambda e, base=base: e.tensor_tensor(
                        out=Xs_strided(base), in0=Xs_strided(base), in1=psO[:, 0:256], op=OP.add),
                        waits=[("pe", av_ok)], incs=[("dve", 1)])
                    xadd_ok = p.n("dve")
            sc.close()

        def Xs_strided(base):
            return X[:, base: base + 511: 2]

        def ph_out():
            sc = ExitStack()
            E = sc.enter_context
            yst = [E(nc.sbuf_tensor(f"yst{i}", [128, D], F32)) for i in range(2)]
            psY = [E(nc.psum_tensor(f"psY{i}", [128, 512], F32)) for i in range(2)]
            evy = {}
            wry = {}
            for tc in range(TC):
                for g in range(4):
                    lin = tc * 4 + g
                    w = []
                    if lin >= 2:
                        w.append(("dve", evy[lin - 2]))
                    if g == 0 and tc >= 2:
                        w.append(wry[tc - 2])
                    for j in range(4):
                        fc = g * 4 + j
                        p.add("tensor", lambda e, tc=tc, g=g, j=j, fc=fc: e.transpose(
                            psY[g % 2][:, j * 128:(j + 1) * 128],
                            X[:, fc * T + tc * 128: fc * T + tc * 128 + 128], idf[:, :]),
                            waits=(w if j == 0 else []),
                            incs=([("pe", 1)] if j == 3 else []))
                    pe_ok = p.n("pe")
                    p.add("vector", lambda e, tc=tc, g=g: e.tensor_copy(
                        out=yst[tc % 2][:, :].rearrange("p (g j f) -> p g j f", g=4, j=4)[:, g, :, :],
                        in_=psY[g % 2][:, :].rearrange("p (j f) -> p j f", j=4)),
                        waits=[("pe", pe_ok)], incs=[("dve", 1)])
                    evy[lin] = p.n("dve")
                wry[tc] = p.dma("gpsimd", "yout", lambda e, tc=tc: e.dma_start(
                    out=y_out[tc * 128:(tc + 1) * 128, :], in_=yst[tc % 2][:, :]),
                    waits=[("dve", evy[tc * 4 + 3])], slot=tc % 2)
            sc.close()

        DECODE_ON = STOP_AFTER in (None, "attnb", "ffn2", "ffn1", "decode") and "decode" not in DEBUG_SKIP
        dec_obj = Decoder() if DECODE_ON else None

        def drain2():
            if dec_obj is not None:
                dec_obj.drain(2)
        cos_sc = ExitStack()
        cose = cos_sc.enter_context(nc.sbuf_tensor("cose", [128, TC * 64], F32))
        coso = cos_sc.enter_context(nc.sbuf_tensor("coso", [128, TC * 64], F32))
        sine = cos_sc.enter_context(nc.sbuf_tensor("sine", [128, TC * 64], F32))
        sino = cos_sc.enter_context(nc.sbuf_tensor("sino", [128, TC * 64], F32))
        for dst, src in ((idb, idb_in), (idf, idf_in), (premask, premask_in), (onesb, ones_in), (lnc, lnc_in)):
            p.dma("sync", "const", lambda e, d=dst, s=src: e.dma_start(out=d[:, :], in_=s))
        for dst, src in ((cose, cose_in), (coso, coso_in), (sine, sine_in), (sino, sino_in)):
            p.dma("sync", "const", lambda e, d=dst, s=src: e.dma_start(
                out=d[:, :].rearrange("p (tc j) -> p tc j", tc=TC),
                in_=s.rearrange("(tc p) j -> p tc j", p=128)))
        p.dma("sync", "const", lambda e: e.dma_start(
            out=maska[:, :].rearrange("p (m l) -> p m l", m=2),
            in_=maska_in.rearrange("(m p) l -> p m l", p=128)))
        p.dma("sync", "const", lambda e: e.dma_start(out=ones_row[:, :], in_=onesr_in))
        CONSTS_TOK = p.dma("sync", "const", lambda e: e.dma_start(out=onesrf[:, :], in_=onesrf_in))
        Xsc = ExitStack()
        Xln1 = Xsc.enter_context(nc.sbuf_tensor("Xln1", [128, FC * T], BF16))
        ph_ln("1", Xln1, first=True, drain=drain2)
        barrier()
        if "xdbg_sc" in DEBUG_SCRATCH:
            p.dma("gpsimd", "dbg", lambda e: e.dma_start(out=xdbg_sc, in_=X[:, :]))
            p.dma("gpsimd", "dbg", lambda e: e.dma_start(out=xlndbg_sc, in_=Xln1[:, :]))
            barrier()
        if STOP_AFTER not in ("ln1",) and "wt" not in DEBUG_SKIP:
            ph_wt(drain=drain2)
            if dec_obj is not None:
                dec_obj.flush()
            barrier()
        if STOP_AFTER not in ("ln1", "wt") and "qkv" not in DEBUG_SKIP:
            ph_qkv(Xln1, drain=drain2)
            barrier()
        Xsc.close()
        cos_sc.close()
        if STOP_AFTER not in ("ln1", "wt", "qkv") and "attna" not in DEBUG_SKIP:
            ph_attn_a(drain=drain2)
            barrier()
            if "xdbg_sc" in DEBUG_SCRATCH and STOP_AFTER == "attna":
                p.dma("gpsimd", "dbg", lambda e: e.dma_start(out=xdbg_sc, in_=X[:, :]))
                barrier()
        if dec_obj is not None:
            dec_obj.sc.close()
        barrier()
        if STOP_AFTER not in ("ln1", "wt", "qkv", "attna", "decode"):
            Xsc2 = ExitStack()
            Xln2 = Xsc2.enter_context(nc.sbuf_tensor("Xln2", [128, FC * T], BF16))
            ph_ln("2", Xln2, first=False)
            barrier()
            if STOP_AFTER == "ln2":
                if "xlndbg_sc" in DEBUG_SCRATCH:
                    p.dma("gpsimd", "dbg", lambda e: e.dma_start(out=xlndbg_sc, in_=Xln2[:, :]))
                    barrier()
            else:
                ph_ffn(Xln2, 1)
                barrier()
            Xsc2.close()
        if STOP_AFTER in (None, "attnb", "ffn2"):
            Xsc3 = ExitStack()
            Xln3 = Xsc3.enter_context(nc.sbuf_tensor("Xln3", [128, FC * T], BF16))
            ph_ln("3", Xln3, first=False)
            barrier()
            ph_attn_b(Xln3)
            Xsc3.close()
            barrier()
            if "xdbg_sc" in DEBUG_SCRATCH and STOP_AFTER == "attnb":
                p.dma("gpsimd", "dbg", lambda e: e.dma_start(out=xdbg_sc, in_=X[:, :]))
                barrier()
        if STOP_AFTER in (None, "ffn2"):
            Xsc4 = ExitStack()
            Xln4 = Xsc4.enter_context(nc.sbuf_tensor("Xln4", [128, FC * T], BF16))
            ph_ln("4", Xln4, first=False)
            barrier()
            ph_ffn(Xln4, 2)
            Xsc4.close()
            barrier()
        if STOP_AFTER is None:
            ph_out()
            if "xdbg_sc" in DEBUG_SCRATCH and STOP_AFTER == "ffn1":
                p.dma("gpsimd", "dbg", lambda e: e.dma_start(out=xdbg_sc, in_=X[:, :]))
                barrier()


        p.add("gpsimd", None, waits=p.all_sems(), incs=[("fin", 1)])

        block = es.enter_context(nc.Block())
        p.emit(block)
    return nc


def host_consts(core_id):
    pos0 = T * core_id
    invf = (10000.0 ** (-np.arange(0, HD, 2, dtype=np.float64) / HD))
    pos = np.arange(pos0, pos0 + T, dtype=np.float64)
    f = np.outer(pos, invf)
    emb = np.concatenate([f, f], axis=-1)
    cos = np.cos(emb).astype(np.float32)
    sin = np.sin(emb).astype(np.float32)
    return {
        "cose_in": np.ascontiguousarray(cos[:, 0::2]),
        "coso_in": np.ascontiguousarray(cos[:, 1::2]),
        "sine_in": np.ascontiguousarray(sin[:, 0::2]),
        "sino_in": np.ascontiguousarray(sin[:, 1::2]),
        "idb_in": np.eye(128, dtype=np.float32).astype(BF),
        "idf_in": np.eye(128, dtype=np.float32),
        "maska_in": np.concatenate([
            (np.arange(256)[None, :] >= np.arange(128)[:, None]).astype(np.float32),
            (np.arange(256)[None, :] >= (np.arange(128) + 128)[:, None]).astype(np.float32),
        ], axis=0),
        "ones_in": np.ones((128, 2), dtype=np.float32).astype(BF),
        "onesr_in": np.ones((1, 128), dtype=np.float32).astype(BF),
        "onesrf_in": np.ones((1, 128), dtype=np.float32),
        "lnc_in": np.broadcast_to(np.array([1.0 / D, -1.0], np.float32), (128, 2)).copy(),
        "premask_in": np.broadcast_to(
            (np.arange(8)[None, :] < core_id).astype(np.float32), (128, 8)).copy(),
    }


def make_in_maps(inputs):
    x = np.ascontiguousarray(np.asarray(inputs["L0_attn_a_x"], dtype=np.float32)[0])
    shared = {
        "wqkv_in": np.ascontiguousarray(np.asarray(inputs["L0_attn_a_W_qkv"], dtype=np.float32)),
        "wout_in": np.ascontiguousarray(np.asarray(inputs["L0_attn_a_W_out"], dtype=np.float32)),
        "gpk_in": np.ascontiguousarray(np.asarray(inputs["L0_ffn_gate_W_packed"], dtype=np.int32)),
        "gsc_in": np.ascontiguousarray(np.asarray(inputs["L0_ffn_gate_scales"], dtype=np.float32)),
        "upk_in": np.ascontiguousarray(np.asarray(inputs["L0_ffn_up_W_packed"], dtype=np.int32)),
        "usc_in": np.ascontiguousarray(np.asarray(inputs["L0_ffn_up_scales"], dtype=np.float32)),
        "dpk_in": np.ascontiguousarray(np.asarray(inputs["L0_ffn_down_W_packed"], dtype=np.int32)),
        "dsc_in": np.ascontiguousarray(np.asarray(inputs["L0_ffn_down_scales"], dtype=np.float32)),
    }
    gpk = shared["gpk_in"]; upk = shared["upk_in"]; dpk = shared["dpk_in"]
    gsc = shared["gsc_in"]; usc = shared["usc_in"]; dsc = shared["dsc_in"]

    def unit_slices(u):
        if u < 128:
            return gpk[u * 128:(u + 1) * 128, :], gsc[u * 128:(u + 1) * 128, :]
        if u < 256:
            nb = u - 128
            return upk[nb * 128:(nb + 1) * 128, :], usc[nb * 128:(nb + 1) * 128, :]
        hb, sb = (u - 256) // 8, (u - 256) % 8
        return (dpk[hb * 128:(hb + 1) * 128, sb * 1024:(sb + 1) * 1024],
                dsc[hb * 128:(hb + 1) * 128, sb * 64:(sb + 1) * 64])

    in_maps = []
    for i in range(NCORES):
        m = {"x_in": np.ascontiguousarray(x[i * T:(i + 1) * T])}
        pks, scs = [], []
        for u in range(48 * i, 48 * (i + 1)):
            a, b = unit_slices(u)
            pks.append(a)
            scs.append(b)
        m["mypk_in"] = np.ascontiguousarray(np.concatenate(pks, axis=0))
        m["mysc_in"] = np.ascontiguousarray(np.concatenate(scs, axis=0))
        m.update(shared)
        m.update(host_consts(i))
        in_maps.append(m)
    return in_maps


def kernel(L0_attn_a_x, L0_attn_a_W_qkv, L0_attn_a_W_out,
           L0_ffn_gate_W_packed, L0_ffn_gate_scales,
           L0_ffn_up_W_packed, L0_ffn_up_scales,
           L0_ffn_down_W_packed, L0_ffn_down_scales):
    in_maps = make_in_maps(dict(
        L0_attn_a_x=L0_attn_a_x, L0_attn_a_W_qkv=L0_attn_a_W_qkv,
        L0_attn_a_W_out=L0_attn_a_W_out,
        L0_ffn_gate_W_packed=L0_ffn_gate_W_packed, L0_ffn_gate_scales=L0_ffn_gate_scales,
        L0_ffn_up_W_packed=L0_ffn_up_W_packed, L0_ffn_up_scales=L0_ffn_up_scales,
        L0_ffn_down_W_packed=L0_ffn_down_W_packed, L0_ffn_down_scales=L0_ffn_down_scales))
    nc = build()
    res = run_bass_kernel_spmd(nc, in_maps, core_ids=list(range(NCORES)))
    out = np.concatenate([res.results[i]["y_out"] for i in range(NCORES)], axis=0)
    return out[None, :, :]
